# version 05
# baseline (speedup 1.0000x reference)
import numpy as np

import concourse.bass as bass
import concourse.mybir as mybir
from concourse.bass_utils import run_bass_kernel_spmd
from concourse import dve_ops
from concourse.dve_ops import DveOp, OPS
from concourse.dve_spec import Spec, Src0, Src1, C0, C1, Bin, lower as dve_lower, maxx
from concourse.dve_uop import AluOp as DAlu
from concourse.dve_table_gen import dve_ver_for

F32 = mybir.dt.float32
BF16 = mybir.dt.bfloat16
AF = mybir.ActivationFunctionType
ALU = mybir.AluOpType
AXX = mybir.AxisListType.X

B, S, D, I = 8, 4096, 1024, 2816
NCHUNK = S // 128
GRP = 4
NGRP = NCHUNK // GRP
IH = I // 2
NH, DH, M = 16, 64, 11
NPAIR = NH // 2
SUB = 2


def _fp4_ref(in0, in1, s0, s1, imm2):
    z = np.asarray(in0)
    az = np.abs(z)
    s_ = np.maximum(np.minimum(az, 4.0), 1.0)
    s_ = np.exp2(np.floor(np.log2(s_)))
    h = s_ / 2
    return ((np.round(z / h) * h) * in1).astype(z.dtype)


def _register_fp4():
    for op in OPS:
        if op.name == "FP4_QD":
            return op
    c_ = maxx(Bin(DAlu.BITWISE_AND, Bin(DAlu.MULTIPLY, Src0, C1), C0), C1)
    X_ = Bin(DAlu.ADD, Bin(DAlu.ADD, c_, c_), c_)
    rq_ = Bin(DAlu.SUBTRACT, Bin(DAlu.ADD, Src0, X_), X_)
    spec = Spec(body=Bin(DAlu.MULTIPLY, rq_, Src1))
    ver = dve_ver_for("TRN2")
    sha = dve_lower(spec, ver=ver, name="FP4_QD").sha(ver)
    op = DveOp("FP4_QD", spec, subdim=False, uops_sha={ver: sha}, reference=_fp4_ref)
    OPS.append(op)
    dve_ops._SUB_OPCODE_FOR_NAME[op.name] = dve_ops._CUSTOM_DVE_ROW_BASE + len(OPS) - 1
    dve_ops.CUSTOM_DVE_SPECS[op.name] = spec
    dve_ops.CUSTOM_DVE_REFS[op.name] = _fp4_ref
    return op


FP4_QD = _register_fp4()
FP4_C0 = float(2 ** 24)
FP4_C1 = float(2 ** 21)


class Sem:
    def __init__(self, h):
        self.h = h
        self.n = 0


def bc16(t, nblk):
    return t[:, :, None].broadcast_to([128, nblk, 16])


def r16(ap):
    return ap.rearrange("p (b k) -> p b k", k=16)


def build_program(phases=("ffn1", "attn", "ffn2")):
    from contextlib import ExitStack

    nc = bass.Bass("TRN2", target_bir_lowering=False,
                   detect_race_conditions=False)

    x_in = nc.dram_tensor("x", [S, D], F32, kind="ExternalInput").ap()
    w_in = nc.dram_tensor("w_in", [3 * D, D], F32, kind="ExternalInput").ap()
    w_out = nc.dram_tensor("w_out", [D, D], F32, kind="ExternalInput").ap()
    beta = nc.dram_tensor("beta", [NH, M], F32, kind="ExternalInput").ap()
    w_gate = nc.dram_tensor("w_gate", [I, D], F32, kind="ExternalInput").ap()
    w_up = nc.dram_tensor("w_up", [I, D], F32, kind="ExternalInput").ap()
    w_down = nc.dram_tensor("w_down", [D, I], F32, kind="ExternalInput").ap()
    y_out = nc.dram_tensor("y", [S, D], F32, kind="ExternalOutput").ap()

    xs = [nc.dram_tensor(f"xs{i}", [S, D], F32).ap() for i in range(2)]
    wgq = nc.dram_tensor("wgq", [8, 22, 128, 128], BF16).ap()
    wuq = nc.dram_tensor("wuq", [8, 22, 128, 128], BF16).ap()
    wdq = nc.dram_tensor("wdq", [22, 8, 128, 128], BF16).ap()
    winT = nc.dram_tensor("winT", [8, 24, 128, 128], BF16).ap()
    woutT = nc.dram_tensor("woutT", [8, 8, 128, 128], BF16).ap()
    xnT_h = nc.dram_tensor("xnT", [NCHUNK, D, 128], BF16).ap()

    st = ExitStack()
    sb = lambda name, shape, dt=F32, side="left": st.enter_context(
        nc.sbuf_tensor(name, shape, dt, side=side))

    xb = [sb(f"xb{i}", [128, D]) for i in range(4)]
    ab = sb("ab", [128, D])
    zb = sb("zb", [128, I])
    qq = sb("qq", [128, I], BF16)
    scr = sb("scr", [128, D])
    hb = sb("hb", [128, IH])
    msum = sb("msum", [128, NCHUNK])
    sqs = sb("sqs", [128, NCHUNK])
    mcol = sb("mcol", [128, NCHUNK])
    rstd = sb("rstd", [128, NCHUNK])
    rst6 = sb("rst6", [128, NCHUNK])
    wvar = sb("wvar", [128, NCHUNK])
    stmp = sb("stmp", [128, NCHUNK])
    maxa = sb("maxa", [128, I // 16])
    rm6 = sb("rm6", [128, I // 16])
    sca = sb("sca", [128, I // 16])
    ident = sb("ident", [128, 128], BF16)
    identf = sb("identf", [128, 128])
    betac = sb("betac", [128, NPAIR * M])

    ps_a = st.enter_context(nc.psum_tensor([128, 1536], F32))
    ps_b = st.enter_context(nc.psum_tensor([128, 1536], F32))
    psA_t = st.enter_context(nc.psum_tensor([128, 1024], BF16))
    psB_t = st.enter_context(nc.psum_tensor([128, 1024], BF16))
    slotA = psA_t[:, 0:128]
    slotB = psB_t[:, 0:128]

    names = ["dmx", "dmw", "vp", "sp", "tp", "gp", "stq", "svv", "ph"]
    _sems = {n: Sem(st.enter_context(nc.semaphore(n))) for n in names}
    dmx, dmw, vp, sp, tp, gp, stq, svv, ph = (_sems[n] for n in names)

    prog = {k: [] for k in ("sync", "vector", "scalar", "tensor", "gpsimd")}
    ENGS = list(prog.keys())

    def step(eng, waits, incs, body):
        w = [(s.h, v) for (s, v) in waits]
        ii = [(s.h, k) for (s, k) in incs]
        for s, k in incs:
            s.n += k

        def f(e, w=w, ii=ii, body=body):
            for h, v in w:
                if v > 0:
                    e.wait_ge(h, v)
            if body is None:
                assert not ii
                return
            ins = body(e)
            for h, k in ii:
                ins.then_inc(h, k)
        prog[eng].append(f)

    def barrier():
        tgt = ph.n + 5
        for eng in ENGS:
            def f(e, tgt=tgt):
                e.sem_inc(ph.h, 1)
                e.wait_ge(ph.h, tgt)
            prog[eng].append(f)
        ph.n = tgt

    slot_free = [0, 0]

    def xpose_evac(srcs, dsts, scales=None, evac_waits=()):
        n = len(srcs)
        for i in range(n):
            si = i % 2
            slot = slotA if si == 0 else slotB
            step("tensor",
                 [(sp, slot_free[si])],
                 [(tp, 1)],
                 lambda t, slot=slot, src=srcs[i]: t.transpose(slot, src, ident[:]))
            scl = scales[i] if scales else None
            ew = list(evac_waits) if i == 0 else []
            if scl is None:
                step("scalar", [(tp, tp.n)] + ew, [(sp, 1)],
                     lambda s, slot=slot, dst=dsts[i]: s.copy(dst, slot))
            else:
                step("scalar", [(tp, tp.n)] + ew, [(sp, 1)],
                     lambda s, slot=slot, dst=dsts[i], scl=scl:
                     s.activation(dst, slot, AF.Copy, scale=scl))
            slot_free[si] = sp.n

    step("gpsimd", [], [(gp, 1)], lambda g: g.memset(ident[:], 0.0))

    gwait = gp.n

    def g_pro2(g, gwait=gwait):
        g.wait_ge(gp.h, gwait)
        g.affine_select(out=ident[:], in_=ident[:],
                        compare_op=ALU.not_equal, fill=1.0, base=0,
                        pattern=[[-1, 128]], channel_multiplier=1)
        return g.memset(identf[:], 0.0)
    step("gpsimd", [], [(gp, 1)], g_pro2)

    gwait2 = gp.n

    def g_pro3(g, gwait2=gwait2):
        g.wait_ge(gp.h, gwait2)
        return g.affine_select(out=identf[:], in_=identf[:],
                               compare_op=ALU.not_equal, fill=1.0, base=0,
                               pattern=[[-1, 128]], channel_multiplier=1)
    step("gpsimd", [], [(gp, 1)], g_pro3)

    def sy_beta(sy):
        ins = None
        for h in range(NPAIR):
            if ins is not None:
                ins.then_inc(dmw.h, 16)
            ins = sy.dma_start(out=betac[0:64, h * M:(h + 1) * M],
                               in_=beta[2 * h:2 * h + 1, :].broadcast_to([64, M]))
            ins.then_inc(dmw.h, 16)
            ins = sy.dma_start(out=betac[64:128, h * M:(h + 1) * M],
                               in_=beta[2 * h + 1:2 * h + 2, :].broadcast_to([64, M]))
        return ins
    dmw.n += 15 * 16
    step("sync", [], [(dmw, 16)], sy_beta)

    def plan_quant(waits, incs, src_ap, ncols, fold_imm, fold_ap=None):
        sv0 = svv.n
        svv.n += 2

        def body(v, sv0=sv0):
            nblk = ncols // 16
            v.reduce_max(maxa[:, 0:nblk], r16(src_ap), axis=AXX,
                         apply_absolute_value=True).then_inc(svv.h, 1)
            v.wait_ge(svv.h, sv0 + 1)
            v.tensor_scalar(maxa[:, 0:nblk], maxa[:, 0:nblk], 6e-12, None, ALU.max)
            v.reciprocal(rm6[:, 0:nblk], maxa[:, 0:nblk]).then_inc(svv.h, 1)
            v.wait_ge(svv.h, sv0 + 2)
            v.tensor_scalar(rm6[:, 0:nblk], rm6[:, 0:nblk], 6.0, None, ALU.mult)
            if fold_ap is None:
                v.tensor_scalar(sca[:, 0:nblk], maxa[:, 0:nblk], fold_imm / 6.0,
                                None, ALU.mult)
            else:
                v.tensor_scalar(sca[:, 0:nblk], maxa[:, 0:nblk], fold_ap,
                                fold_imm / 6.0, ALU.mult, ALU.mult)
            v.tensor_tensor(r16(zb[:, 0:ncols]), r16(src_ap),
                            bc16(rm6[:, 0:nblk], nblk), ALU.mult)
            return v._custom_dve(FP4_QD, out=r16(qq[:, 0:ncols]),
                                 in0=r16(zb[:, 0:ncols]),
                                 in1=bc16(sca[:, 0:nblk], nblk),
                                 s0=FP4_C0, s1=FP4_C1)
        step("vector", waits, incs, body)

    def prep_weight(w_src, n_chunks, dst, quant, fold=1.0, wst=None):
        wst = _wst[0]
        ncols = w_src.shape[1]
        nkb = ncols // 128
        for r in range(n_chunks):
            step("sync", [(vp, vp.n)], [(dmw, 16)],
                 lambda sy, r=r, ncols=ncols: sy.dma_start(
                     out=zb[:, 0:ncols], in_=w_src[r * 128:(r + 1) * 128, :]))
            if quant:
                plan_quant([(dmw, dmw.n), (tp, tp.n)], [(vp, 1)],
                           zb[:, 0:ncols], ncols, fold)
            else:
                step("vector", [(dmw, dmw.n), (tp, tp.n)], [(vp, 1)],
                     lambda v, ncols=ncols: v.tensor_copy(qq[:, 0:ncols],
                                                          zb[:, 0:ncols]))
            step("tensor", [(vp, vp.n)], [], None)
            xpose_evac([qq[:, kb * 128:(kb + 1) * 128] for kb in range(nkb)],
                       [wst[:, kb * 128:(kb + 1) * 128] for kb in range(nkb)])
            step("gpsimd", [(sp, sp.n), (stq, stq.n)], [(stq, 16)],
                 lambda g, r=r, nkb=nkb: g.dma_start(
                     out=dst[:, r][0:nkb].rearrange("kb p t -> p kb t"),
                     in_=wst[:, 0:nkb * 128].rearrange("p (kb t) -> p kb t", kb=nkb)))
            step("scalar", [(stq, stq.n)], [], None)

    _wst = [None]
    with nc.sbuf_tensor("wst", [128, 22 * 128], BF16) as wst:
        _wst[0] = wst
        if "ffn1" in phases or "ffn2" in phases:
            prep_weight(w_gate, 22, wgq, True)
            prep_weight(w_up, 22, wuq, True)
            prep_weight(w_down, 8, wdq, True, fold=0.5)
        if "attn" in phases:
            prep_weight(w_in, 24, winT, False)
            prep_weight(w_out, 8, woutT, False)
        barrier()

    def ln_stats(x_hbm):
        for c in range(NCHUNK):
            step("sync", [(vp, vp.n - 1)] if c >= 2 else [], [(dmx, 16)],
                 lambda sy, c=c: sy.dma_start(out=xb[c % 2][:],
                                              in_=x_hbm[c * 128:(c + 1) * 128, :]))

            def stats_body(v, c=c):
                v.reduce_sum(msum[:, c:c + 1], xb[c % 2][:], axis=AXX)
                return v.scalar_tensor_tensor(scr[:], xb[c % 2][:], 1.0, xb[c % 2][:],
                                              ALU.mult, ALU.mult,
                                              accum_out=sqs[:, c:c + 1])
            step("vector", [(dmx, dmx.n)], [(vp, 1)], stats_body)

        sv0 = svv.n
        svv.n += 1

        def smalls(v, sv0=sv0):
            v.tensor_copy(stmp[:], msum[:]).then_inc(svv.h, 1)
            v.wait_ge(svv.h, sv0 + 1)
            v.tensor_scalar(mcol[:], msum[:], 1.0 / D, None, ALU.mult)
            v.tensor_scalar(wvar[:], sqs[:], 1.0 / D, None, ALU.mult)
            v.tensor_tensor(stmp[:], mcol[:], mcol[:], ALU.mult)
            v.tensor_tensor(wvar[:], wvar[:], stmp[:], ALU.subtract)
            return v.tensor_scalar(wvar[:], wvar[:], 1e-5, None, ALU.add)
        step("vector", [], [(vp, 1)], smalls)

        def sqrt_body(s):
            s.preload_act_table(AF.Sqrt)
            ins = s.activation(rstd[:], wvar[:], AF.Sqrt)
            s.preload_act_table(AF.Silu)
            return ins
        step("scalar", [(vp, vp.n)], [(sp, 1)], sqrt_body)

        sv0 = svv.n
        svv.n += 1

        def refine(v, sv0=sv0):
            v.reciprocal(rstd[:], rstd[:]).then_inc(svv.h, 1)
            v.wait_ge(svv.h, sv0 + 1)
            v.tensor_tensor(stmp[:], rstd[:], rstd[:], ALU.mult)
            v.tensor_tensor(stmp[:], stmp[:], wvar[:], ALU.mult)
            v.tensor_scalar(stmp[:], stmp[:], -0.5, 1.5, ALU.mult, ALU.add)
            v.tensor_tensor(rstd[:], rstd[:], stmp[:], ALU.mult)
            return v.tensor_scalar(rst6[:], rstd[:], 1.0 / 6.0, None, ALU.mult)
        step("vector", [(sp, sp.n)], [(vp, 1)], refine)

    _ffn_ctr = [0]

    def ffn_phase(x_src, x_dst):
        _ffn_ctr[0] += 1
        tag = f"_f{_ffn_ctr[0]}"
        ln_stats(x_src)
        with ExitStack() as fst:
            fsb = lambda name, shape, dt=F32: fst.enter_context(
                nc.sbuf_tensor(name + tag, shape, dt))
            wgh = fsb("wgh", [128, 8 * IH], BF16)
            wuh = fsb("wuh", [128, 8 * IH], BF16)
            wdn = fsb("wdn", [128, 22 * D], BF16)
            hqT = fsb("hqT", [128, 22 * GRP * 128], BF16)
            xqT = [fsb(f"xqT{i}", [128, D], BF16) for i in range(GRP)]
            sgb = fsb("sgb", [128, IH])
            sp_free_a = [0]
            vp_free_b = [0]
            sp_xq = [0]
            resid_done = [vp.n] * 4
            store_done = [0] * 4

            def load_wdn(sy):
                ins = None
                for ob in range(8):
                    if ins is not None:
                        ins.then_inc(dmw.h, 16)
                    ins = sy.dma_start(
                        out=wdn[:].rearrange("p (ib ob t) -> p ib ob t",
                                             ib=22, ob=8)[:, :, ob],
                        in_=wdq[:, ob].rearrange("ib p t -> p ib t"))
                return ins
            dmw.n += 7 * 16
            step("sync", [], [(dmw, 16)], load_wdn)

            for g in range(NGRP):
                for ci in range(GRP):
                    c = g * GRP + ci
                    step("sync", [(vp, resid_done[ci]), (stq, store_done[ci])],
                         [(dmx, 16)],
                         lambda sy, c=c, ci=ci: sy.dma_start(
                             out=xb[ci][:],
                             in_=x_src[c * 128:(c + 1) * 128, :]))
                    sv0 = svv.n
                    svv.n += 2

                    def xquant(v, c=c, ci=ci, sv0=sv0):
                        v.tensor_scalar(ab[:], xb[ci][:], mcol[:, c:c + 1],
                                        None, ALU.subtract)
                        nblk = D // 16
                        v.reduce_max(maxa[:, 0:nblk], r16(ab[:]), axis=AXX,
                                     apply_absolute_value=True).then_inc(svv.h, 1)
                        v.wait_ge(svv.h, sv0 + 1)
                        v.tensor_scalar(maxa[:, 0:nblk], maxa[:, 0:nblk],
                                        6e-12, None, ALU.max)
                        v.reciprocal(rm6[:, 0:nblk], maxa[:, 0:nblk]
                                     ).then_inc(svv.h, 1)
                        v.wait_ge(svv.h, sv0 + 2)
                        v.tensor_scalar(rm6[:, 0:nblk], rm6[:, 0:nblk], 6.0,
                                        None, ALU.mult)
                        v.tensor_scalar(sca[:, 0:nblk], maxa[:, 0:nblk],
                                        rst6[:, c:c + 1], None, ALU.mult)
                        v.tensor_tensor(r16(zb[:, 0:D]), r16(ab[:]),
                                        bc16(rm6[:, 0:nblk], nblk), ALU.mult)
                        return v._custom_dve(
                            FP4_QD, out=r16(qq[:, 0:D]), in0=r16(zb[:, 0:D]),
                            in1=bc16(sca[:, 0:nblk], nblk),
                            s0=FP4_C0, s1=FP4_C1)
                    step("vector", [(dmx, dmx.n), (tp, tp.n)], [(vp, 1)], xquant)
                    step("tensor", [(vp, vp.n)], [], None)
                    xpose_evac(
                        [qq[:, kb * 128:(kb + 1) * 128] for kb in range(8)],
                        [xqT[ci][:, kb * 128:(kb + 1) * 128] for kb in range(8)])
                sp_xq[0] = sp.n

                for ih in range(2):
                    def load_guh(sy, ih=ih):
                        ins = None
                        for kb in range(8):
                            if ins is not None:
                                ins.then_inc(dmw.h, 16)
                            ins = sy.dma_start(
                                out=wgh[:, kb * IH:(kb + 1) * IH].rearrange(
                                    "p (r t) -> p r t", r=11),
                                in_=wgq[kb, ih * 11:(ih + 1) * 11].rearrange(
                                    "r p t -> p r t"))
                            ins.then_inc(dmw.h, 16)
                            ins = sy.dma_start(
                                out=wuh[:, kb * IH:(kb + 1) * IH].rearrange(
                                    "p (r t) -> p r t", r=11),
                                in_=wuq[kb, ih * 11:(ih + 1) * 11].rearrange(
                                    "r p t -> p r t"))
                        return ins
                    dmw.n += 15 * 16
                    step("sync", [(tp, tp.n)], [(dmw, 16)], load_guh)

                    for ci in range(GRP):
                        c = g * GRP + ci

                        def gup(t, ci=ci):
                            for kb in range(8):
                                lhs = xqT[ci][:, kb * 128:(kb + 1) * 128]
                                first, last = kb == 0, kb == 7
                                for nw in range(3):
                                    n0, n1 = nw * 512, min((nw + 1) * 512, IH)
                                    t.matmul(ps_a[:, n0:n1], lhs,
                                             wgh[:, kb * IH + n0:kb * IH + n1],
                                             start=first, stop=last)
                                    ins = t.matmul(ps_b[:, n0:n1], lhs,
                                                   wuh[:, kb * IH + n0:kb * IH + n1],
                                                   start=first, stop=last)
                            return ins
                        step("tensor",
                             [(sp, max(sp_free_a[0], sp_xq[0])),
                              (vp, vp_free_b[0]), (dmw, dmw.n)], [(tp, 1)], gup)

                        step("scalar", [(tp, tp.n), (vp, vp_free_b[0])], [(sp, 1)],
                             lambda s: s.activation(sgb[:], ps_a[:, 0:IH], AF.Silu))
                        sp_free_a[0] = sp.n

                        def hmult(v):
                            return v.tensor_tensor(hb[:], sgb[:], ps_b[:, 0:IH],
                                                   ALU.mult)
                        step("vector", [(sp, sp.n), (tp, tp.n)], [(vp, 1)], hmult)
                        vp_free_b[0] = vp.n

                        sv0 = svv.n
                        svv.n += 2

                        def hquant(v, sv0=sv0):
                            nblk = IH // 16
                            v.reduce_max(maxa[:, 0:nblk], r16(hb[:]), axis=AXX,
                                         apply_absolute_value=True).then_inc(svv.h, 1)
                            v.wait_ge(svv.h, sv0 + 1)
                            v.tensor_scalar(maxa[:, 0:nblk], maxa[:, 0:nblk], 6e-12,
                                            None, ALU.max)
                            v.reciprocal(rm6[:, 0:nblk], maxa[:, 0:nblk]
                                         ).then_inc(svv.h, 1)
                            v.wait_ge(svv.h, sv0 + 2)
                            v.tensor_scalar(rm6[:, 0:nblk], rm6[:, 0:nblk], 6.0,
                                            None, ALU.mult)
                            v.tensor_scalar(sca[:, 0:nblk], maxa[:, 0:nblk],
                                            1.0 / 6.0, None, ALU.mult)
                            v.tensor_tensor(r16(zb[:, 0:IH]), r16(hb[:]),
                                            bc16(rm6[:, 0:nblk], nblk), ALU.mult)
                            return v._custom_dve(
                                FP4_QD, out=r16(qq[:, 0:IH]), in0=r16(zb[:, 0:IH]),
                                in1=bc16(sca[:, 0:nblk], nblk), s0=FP4_C0, s1=FP4_C1)
                        step("vector", [], [(vp, 1)], hquant)
                        step("tensor", [(vp, vp.n)], [], None)
                        xpose_evac(
                            [qq[:, ib * 128:(ib + 1) * 128] for ib in range(11)],
                            [hqT[:, ((ih * 11 + ib) * GRP + ci) * 128:
                                 ((ih * 11 + ib) * GRP + ci) * 128 + 128]
                             for ib in range(11)])

                for ci in range(GRP):
                    c = g * GRP + ci

                    def down(t, ci=ci):
                        for ib in range(22):
                            lhs = hqT[:, (ib * GRP + ci) * 128:
                                      (ib * GRP + ci) * 128 + 128]
                            for nw in range(2):
                                ins = t.matmul(
                                    ps_a[:, nw * 512:(nw + 1) * 512], lhs,
                                    wdn[:, ib * D + nw * 512:ib * D + (nw + 1) * 512],
                                    start=(ib == 0), stop=(ib == 21))
                        return ins
                    step("tensor", [(sp, sp.n)], [(tp, 1)], down)
                    step("vector", [(tp, tp.n)], [(vp, 1)],
                         lambda v, ci=ci: v.tensor_tensor(xb[ci][:], xb[ci][:],
                                                          ps_a[:, 0:D], ALU.add))
                    resid_done[ci] = vp.n
                    step("tensor", [(vp, vp.n)], [], None)
                    step("gpsimd", [(vp, vp.n)], [(stq, 16)],
                         lambda g2, c=c, ci=ci: g2.dma_start(
                             out=x_dst[c * 128:(c + 1) * 128, :], in_=xb[ci][:]))
                    store_done[ci] = stq.n

            step("gpsimd", [(stq, stq.n)], [], None)
        barrier()

    def attn_phase(x_src, x_dst):
        ln_stats(x_src)
        with ExitStack() as kst:
            kvacc = kst.enter_context(
                nc.sbuf_tensor("kvacc", [128, NPAIR * M * 128], F32))
            kvar = kst.enter_context(
                nc.sbuf_tensor("kvar", [128, NPAIR * M * 128], BF16, side="right"))
            with ExitStack() as ast:
                asb = lambda name, shape, dt=F32: ast.enter_context(
                    nc.sbuf_tensor(name, shape, dt))
                wkv = asb("wkv", [128, 8 * 2048], BF16)
                tkall = asb("tkall", [128, SUB * M * D], BF16)
                vgrp = asb("vgrp", [128, SUB * D], BF16)
                scr2b = asb("scr2b", [128, D])
                xqTa = [asb(f"xqTa{i}", [128, D], BF16) for i in range(2)]

                def load_wkv(sy):
                    ins = None
                    for kb in range(8):
                        if ins is not None:
                            ins.then_inc(dmw.h, 16)
                        ins = sy.dma_start(
                            out=wkv[:, kb * 2048:(kb + 1) * 2048].rearrange(
                                "p (r t) -> p r t", r=16),
                            in_=winT[kb, 8:24].rearrange("r p t -> p r t"))
                    return ins
                dmw.n += 7 * 16
                step("sync", [], [(dmw, 16)], load_wkv)

                for sg in range(NCHUNK // SUB):
                    kevac_done = [0, 0]
                    vevac_done = [0, 0]
                    pending_tk = []
                    for ci in range(SUB):
                        c = sg * SUB + ci
                        step("sync", [(vp, vp.n)], [(dmx, 16)],
                             lambda sy, c=c, ci=ci: sy.dma_start(
                                 out=xb[ci][:], in_=x_src[c * 128:(c + 1) * 128, :]))
                        step("vector", [(dmx, dmx.n), (tp, tp.n)], [(vp, 1)],
                             lambda v, c=c, ci=ci: v.tensor_scalar(
                                 qq[:, 0:D], xb[ci][:], mcol[:, c:c + 1],
                                 rstd[:, c:c + 1], ALU.subtract, ALU.mult))
                        step("tensor", [(vp, vp.n)], [], None)
                        xpose_evac(
                            [qq[:, kb * 128:(kb + 1) * 128] for kb in range(8)],
                            [xqTa[ci][:, kb * 128:(kb + 1) * 128] for kb in range(8)],
                            evac_waits=[(stq, stq.n)])
                        step("gpsimd", [(sp, sp.n)], [(stq, 16)],
                             lambda g2, c=c, ci=ci: g2.dma_start(
                                 out=xnT_h[c].rearrange("(kb kp) t -> kp kb t",
                                                        kp=128),
                                 in_=xqTa[ci][:].rearrange("p (kb t) -> p kb t",
                                                           kb=8)))

                        def kvproj(t, ci=ci):
                            for kb in range(8):
                                lhs = xqTa[ci][:, kb * 128:(kb + 1) * 128]
                                for nw in range(4):
                                    ps = ps_a if nw < 2 else ps_b
                                    ins = t.matmul(
                                        ps[:, (nw % 2) * 512:(nw % 2 + 1) * 512], lhs,
                                        wkv[:, kb * 2048 + nw * 512:
                                            kb * 2048 + (nw + 1) * 512],
                                        start=(kb == 0), stop=(kb == 7))
                            return ins
                        step("tensor",
                             [(dmw, dmw.n), (vp, kevac_done[0]),
                              (sp, vevac_done[0])],
                             [(tp, 1)], kvproj)
                        tp_proj = tp.n

                        def kevac(v):
                            v.tensor_scalar(ab[:], ps_a[:, 0:D], 0.125, 1.0,
                                            ALU.mult, ALU.min)
                            return v.tensor_scalar(ab[:], ab[:], -1.0, None, ALU.max)
                        step("vector", [(tp, tp_proj)], [(vp, 1)], kevac)
                        kevac_done[0] = vp.n
                        step("scalar", [(tp, tp_proj)], [(sp, 1)],
                             lambda s, ci=ci: s.copy(vgrp[:, ci * D:(ci + 1) * D],
                                                     ps_b[:, 0:D]))
                        vevac_done[0] = sp.n

                        def tksave(v, ci=ci):
                            base = ci * M * D
                            v.memset(tkall[:, base:base + D], 1.0)
                            v.tensor_copy(tkall[:, base + D:base + 2 * D], ab[:])
                            dstk = scr[:] if ci == 0 else scr2b[:]
                            return v.tensor_scalar(dstk, ab[:], 2.0, None, ALU.mult)
                        step("vector", [], [(vp, 1)], tksave)

                        def tkrec(v, ci=ci):
                            base = ci * M * D
                            k2 = scr[:] if ci == 0 else scr2b[:]
                            v.memset(zb[:, 0:D], 1.0)
                            v.tensor_scalar(zb[:, D:2 * D], k2, 0.5, None, ALU.mult)
                            for m in range(2, M):
                                cur, prv = (m % 2) * D, ((m - 1) % 2) * D
                                v.tensor_tensor(hb[:, 0:D], k2,
                                                zb[:, prv:prv + D], ALU.mult)
                                v.tensor_tensor(zb[:, cur:cur + D], hb[:, 0:D],
                                                zb[:, cur:cur + D], ALU.subtract)
                                ins = v.tensor_copy(
                                    tkall[:, base + m * D:base + (m + 1) * D],
                                    zb[:, cur:cur + D])
                            return ins
                        pending_tk.append(tkrec)
                        if ci == 0:
                            pass
                        else:
                            for f in pending_tk:
                                step("vector", [], [(vp, 1)], f)
                            pending_tk = []

                    for pp in range(NPAIR // 2):
                        def kvmm(t, sg=sg, pp=pp):
                            for half in range(2):
                                pair = pp * 2 + half
                                ps = ps_a if half == 0 else ps_b
                                for ci in range(SUB):
                                    lhs = vgrp[:, ci * D + pair * 128:
                                               ci * D + (pair + 1) * 128]
                                    for mg in range(3):
                                        m0 = mg * 4
                                        mN = min(4, M - m0)
                                        rhs = tkall[:].rearrange(
                                            "p (c m t) -> p c m t", c=SUB, m=M)[
                                            :, ci, m0:m0 + mN,
                                            pair * 128:(pair + 1) * 128]
                                        ins = t.matmul(
                                            ps[:, m0 * 128:(m0 + mN) * 128], lhs, rhs,
                                            start=(ci == 0), stop=(ci == SUB - 1))
                            return ins
                        step("tensor", [(vp, vp.n)], [(tp, 1)], kvmm)

                        def kvadd(v, sg=sg, pp=pp):
                            for half in range(2):
                                pair = pp * 2 + half
                                ps = ps_a if half == 0 else ps_b
                                dst = kvacc[:, pair * M * 128:(pair + 1) * M * 128]
                                if sg == 0:
                                    ins = v.tensor_copy(dst, ps[:, 0:M * 128])
                                else:
                                    ins = v.tensor_tensor(dst, dst, ps[:, 0:M * 128],
                                                          ALU.add)
                            return ins
                        step("vector", [(tp, tp.n)], [(vp, 1)], kvadd)
                        step("tensor", [(vp, vp.n)], [], None)

                step("gpsimd", [(stq, stq.n)], [], None)
                barrier()

                step("gpsimd", [], [(gp, 1)], lambda g2: g2.memset(kvar[:], 0.0))
                step("tensor", [(gp, gp.n)], [], None)
                arr_base = sp.n
                for pair in range(NPAIR):
                    for m in range(M):
                        j = pair * M + m
                        slot = ps_a[:, 0:128] if j % 2 == 0 else ps_b[:, 0:128]
                        step("tensor",
                             [(sp, arr_base + 2 * (j - 1))] if j >= 2 else [],
                             [(tp, 1)],
                             lambda t, pair=pair, m=m, slot=slot: t.transpose(
                                 slot,
                                 kvacc[:, (pair * M + m) * 128:
                                       (pair * M + m + 1) * 128],
                                 identf[:]))

                        def kvevac(s, pair=pair, m=m, slot=slot):
                            base = (pair * M + m) * 128
                            s.activation(kvar[0:64, base:base + 64], slot[0:64, 0:64],
                                         AF.Copy,
                                         scale=betac[0:64, pair * M + m:
                                                     pair * M + m + 1])
                            return s.activation(
                                kvar[64:128, base + 64:base + 128],
                                slot[64:128, 64:128], AF.Copy,
                                scale=betac[64:128, pair * M + m:pair * M + m + 1])
                        step("scalar", [(tp, tp.n)], [(sp, 2)], kvevac)
                step("tensor", [(sp, sp.n)], [], None)
                barrier()

            with ExitStack() as bst:
                bsb2 = lambda name, shape, dt=F32: bst.enter_context(
                    nc.sbuf_tensor(name, shape, dt))
                wq8 = bsb2("wq8", [128, 8 * D], BF16)
                woT = bsb2("woT", [128, 8 * D], BF16)
                xbT = bsb2("xbT", [128, 8 * D], BF16)
                tqa = bsb2("tqa", [128, M * D], BF16)
                aoT = bsb2("aoT", [128, 8 * D], BF16)
                qf = bsb2("qf", [128, D])

                def load_wb(sy):
                    ins = None
                    for kb in range(8):
                        if ins is not None:
                            ins.then_inc(dmw.h, 16)
                        ins = sy.dma_start(
                            out=wq8[:, kb * D:(kb + 1) * D].rearrange(
                                "p (r t) -> p r t", r=8),
                            in_=winT[kb, 0:8].rearrange("r p t -> p r t"))
                        ins.then_inc(dmw.h, 16)
                        ins = sy.dma_start(
                            out=woT[:, kb * D:(kb + 1) * D].rearrange(
                                "p (r t) -> p r t", r=8),
                            in_=woutT[kb, :].rearrange("r p t -> p r t"))
                    return ins
                dmw.n += 15 * 16
                step("sync", [], [(dmw, 16)], load_wb)

                bstore = [0] * 4
                for stile in range(4):
                    c0 = stile * 8

                    def load_x(sy, c0=c0):
                        ins = None
                        for kb in range(8):
                            if ins is not None:
                                ins.then_inc(dmx.h, 16)
                            ins = sy.dma_start(
                                out=xbT[:, kb * D:(kb + 1) * D].rearrange(
                                    "p (c t) -> p c t", c=8),
                                in_=xnT_h[c0:c0 + 8, kb * 128:(kb + 1) * 128]
                                .rearrange("c kp t -> kp c t"))
                        return ins
                    dmx.n += 7 * 16
                    step("sync", [(vp, vp.n)], [(dmx, 16)], load_x)

                    qf_done = [0]
                    ao_done = [0]
                    pending_out = [None]

                    def plan_outmm(pair):
                        def outmm(t, pair=pair):
                            for m in range(M):
                                lhs = kvar[:, (pair * M + m) * 128:
                                           (pair * M + m + 1) * 128]
                                for nw in range(2):
                                    ins = t.matmul(
                                        ps_b[:, nw * 512:(nw + 1) * 512], lhs,
                                        tqa[:, m * D + nw * 512:
                                            m * D + (nw + 1) * 512],
                                        start=(m == 0), stop=(m == M - 1))
                            return ins
                        step("tensor", [(vp, vp.n), (sp, ao_done[0])],
                             [(tp, 1)], outmm)
                        step("scalar", [(tp, tp.n)], [(sp, 1)],
                             lambda s, pair=pair: s.copy(
                                 aoT[:, pair * D:(pair + 1) * D], ps_b[:, 0:D]))
                        ao_done[0] = sp.n

                    for pair in range(NPAIR):
                        def qproj(t, pair=pair):
                            for kb in range(8):
                                lhs = wq8[:, kb * D + pair * 128:
                                          kb * D + (pair + 1) * 128]
                                for nw in range(2):
                                    ins = t.matmul(
                                        ps_a[:, nw * 512:(nw + 1) * 512], lhs,
                                        xbT[:, kb * D + nw * 512:
                                            kb * D + (nw + 1) * 512],
                                        start=(kb == 0), stop=(kb == 7))
                            return ins
                        step("tensor",
                             [(dmx, dmx.n), (dmw, dmw.n), (vp, qf_done[0])],
                             [(tp, 1)], qproj)
                        tp_q = tp.n

                        def qfevac(v):
                            v.tensor_scalar(qf[:], ps_a[:, 0:1024], 0.125, 1.0,
                                            ALU.mult, ALU.min)
                            return v.tensor_scalar(qf[:], qf[:], -1.0, None, ALU.max)
                        step("vector", [(tp, tp_q)], [(vp, 1)], qfevac)
                        qf_done[0] = vp.n

                        if pending_out[0] is not None:
                            plan_outmm(pending_out[0])
                        tp_out = tp.n

                        def tqrec(v):
                            v.memset(tqa[:, 0:D], 1.0)
                            v.tensor_copy(tqa[:, D:2 * D], qf[:])
                            v.tensor_scalar(scr[:], qf[:], 2.0, None, ALU.mult)
                            v.memset(zb[:, 0:D], 1.0)
                            v.tensor_copy(zb[:, D:2 * D], qf[:])
                            for m in range(2, M):
                                cur, prv = (m % 2) * D, ((m - 1) % 2) * D
                                v.tensor_tensor(hb[:, 0:D], scr[:],
                                                zb[:, prv:prv + D], ALU.mult)
                                v.tensor_tensor(zb[:, cur:cur + D], hb[:, 0:D],
                                                zb[:, cur:cur + D], ALU.subtract)
                                ins = v.tensor_copy(tqa[:, m * D:(m + 1) * D],
                                                    zb[:, cur:cur + D])
                            return ins
                        step("vector", [(tp, tp_out)], [(vp, 1)], tqrec)
                        pending_out[0] = pair
                    plan_outmm(pending_out[0])

                    for ci in range(8):
                        c = c0 + ci
                        step("sync", [(vp, vp.n), (stq, bstore[ci % 4])],
                             [(dmx, 16)],
                             lambda sy, c=c, ci=ci: sy.dma_start(
                                 out=xb[ci % 4][:],
                                 in_=x_src[c * 128:(c + 1) * 128, :]))

                        def woutmm(t, ci=ci):
                            for kb in range(8):
                                lhs = aoT[:, kb * D + ci * 128:kb * D + ci * 128 + 128]
                                for nw in range(2):
                                    ins = t.matmul(
                                        ps_a[:, nw * 512:(nw + 1) * 512], lhs,
                                        woT[:, kb * D + nw * 512:
                                            kb * D + (nw + 1) * 512],
                                        start=(kb == 0), stop=(kb == 7))
                            return ins
                        step("tensor", [(sp, sp.n)], [(tp, 1)], woutmm)
                        step("vector", [(tp, tp.n), (dmx, dmx.n)], [(vp, 1)],
                             lambda v, ci=ci: v.tensor_tensor(
                                 xb[ci % 4][:], xb[ci % 4][:], ps_a[:, 0:D], ALU.add))
                        step("tensor", [(vp, vp.n)], [], None)
                        step("gpsimd", [(vp, vp.n)], [(stq, 16)],
                             lambda g2, c=c, ci=ci: g2.dma_start(
                                 out=x_dst[c * 128:(c + 1) * 128, :],
                                 in_=xb[ci % 4][:]))
                        bstore[ci % 4] = stq.n

                step("gpsimd", [(stq, stq.n)], [], None)
        barrier()

    nph = [p for p in ("ffn1", "attn", "ffn2") if p in phases]
    cur = x_in
    for idx, p in enumerate(nph):
        dst = y_out if idx == len(nph) - 1 else xs[idx % 2]
        if p in ("ffn1", "ffn2"):
            ffn_phase(cur, dst)
        else:
            attn_phase(cur, dst)
        cur = dst
    if cur is not y_out:
        for c in range(NCHUNK):
            dmx_tgt = dmx.n + 16

            def cl(sy, c=c, cur=cur, dmx_tgt=dmx_tgt):
                ins = sy.dma_start(out=xb[0][:], in_=cur[c * 128:(c + 1) * 128, :])
                ins.then_inc(dmx.h, 16)
                sy.wait_ge(dmx.h, dmx_tgt)
                return sy.dma_start(out=y_out[c * 128:(c + 1) * 128, :], in_=xb[0][:])
            step("sync", [(stq, stq.n)], [(stq, 16)], cl)
            dmx.n += 16

    step("sync", [(stq, stq.n)], [], None)

    with nc.Block() as block:
        @block.sync
        def _(e):
            for f in prog["sync"]:
                f(e)

        @block.vector
        def _(e):
            for f in prog["vector"]:
                f(e)

        @block.scalar
        def _(e):
            for f in prog["scalar"]:
                f(e)

        @block.tensor
        def _(e):
            for f in prog["tensor"]:
                f(e)

        @block.gpsimd
        def _(e):
            for f in prog["gpsimd"]:
                f(e)

    st.close()
    mybir.codegen_inst_isa_subclasses(nc)
    return nc


_NC_CACHE = {}


def kernel(L0_attn_x, L0_attn_w_in, L0_attn_w_out, L0_attn_beta,
           L0_ffn_w_gate, L0_ffn_w_up, L0_ffn_w_down):
    x = np.ascontiguousarray(np.asarray(L0_attn_x, dtype=np.float32))
    if "nc" not in _NC_CACHE:
        _NC_CACHE["nc"] = build_program()
    nc = _NC_CACHE["nc"]
    w = {
        "w_in": np.ascontiguousarray(np.asarray(L0_attn_w_in, np.float32)),
        "w_out": np.ascontiguousarray(np.asarray(L0_attn_w_out, np.float32)),
        "beta": np.ascontiguousarray(np.asarray(L0_attn_beta, np.float32)),
        "w_gate": np.ascontiguousarray(np.asarray(L0_ffn_w_gate, np.float32)),
        "w_up": np.ascontiguousarray(np.asarray(L0_ffn_w_up, np.float32)),
        "w_down": np.ascontiguousarray(np.asarray(L0_ffn_w_down, np.float32)),
    }
    in_maps = [dict(w, x=np.ascontiguousarray(x[b])) for b in range(B)]
    res = run_bass_kernel_spmd(nc, in_maps, core_ids=list(range(B)))
    out = np.stack([np.asarray(res.results[b]["y"]) for b in range(B)], axis=0)
    return out.astype(np.float32)


# version 06
# speedup vs baseline: 1.1147x; 1.1147x over previous
import numpy as np

import concourse.bass as bass
import concourse.mybir as mybir
from concourse.bass_utils import run_bass_kernel_spmd
from concourse import dve_ops
from concourse.dve_ops import DveOp, OPS
from concourse.dve_spec import Spec, Src0, Src1, C0, C1, Bin, lower as dve_lower, maxx
from concourse.dve_uop import AluOp as DAlu
from concourse.dve_table_gen import dve_ver_for

F32 = mybir.dt.float32
BF16 = mybir.dt.bfloat16
AF = mybir.ActivationFunctionType
ALU = mybir.AluOpType
AXX = mybir.AxisListType.X

B, S, D, I = 8, 4096, 1024, 2816
NCHUNK = S // 128
GRP = 4
NGRP = NCHUNK // GRP
IH = I // 2
NH, DH, M = 16, 64, 11
NPAIR = NH // 2
SUB = 2


def _fp4_ref(in0, in1, s0, s1, imm2):
    z = np.asarray(in0)
    az = np.abs(z)
    s_ = np.maximum(np.minimum(az, 4.0), 1.0)
    s_ = np.exp2(np.floor(np.log2(s_)))
    h = s_ / 2
    return ((np.round(z / h) * h) * in1).astype(z.dtype)


def _register_fp4():
    for op in OPS:
        if op.name == "FP4_QD":
            return op
    c_ = maxx(Bin(DAlu.BITWISE_AND, Bin(DAlu.MULTIPLY, Src0, C1), C0), C1)
    X_ = Bin(DAlu.ADD, Bin(DAlu.ADD, c_, c_), c_)
    rq_ = Bin(DAlu.SUBTRACT, Bin(DAlu.ADD, Src0, X_), X_)
    spec = Spec(body=Bin(DAlu.MULTIPLY, rq_, Src1))
    ver = dve_ver_for("TRN2")
    sha = dve_lower(spec, ver=ver, name="FP4_QD").sha(ver)
    op = DveOp("FP4_QD", spec, subdim=False, uops_sha={ver: sha}, reference=_fp4_ref)
    OPS.append(op)
    dve_ops._SUB_OPCODE_FOR_NAME[op.name] = dve_ops._CUSTOM_DVE_ROW_BASE + len(OPS) - 1
    dve_ops.CUSTOM_DVE_SPECS[op.name] = spec
    dve_ops.CUSTOM_DVE_REFS[op.name] = _fp4_ref
    return op


FP4_QD = _register_fp4()
FP4_C0 = float(2 ** 24)
FP4_C1 = float(2 ** 21)


class Sem:
    def __init__(self, h):
        self.h = h
        self.n = 0


def bc16(t, nblk):
    return t[:, :, None].broadcast_to([128, nblk, 16])


def r16(ap):
    return ap.rearrange("p (b k) -> p b k", k=16)


def build_program(phases=("ffn1", "attn", "ffn2")):
    from contextlib import ExitStack

    nc = bass.Bass("TRN2", target_bir_lowering=False,
                   detect_race_conditions=False)

    x_in = nc.dram_tensor("x", [S, D], F32, kind="ExternalInput").ap()
    w_in = nc.dram_tensor("w_in", [3 * D, D], F32, kind="ExternalInput").ap()
    w_out = nc.dram_tensor("w_out", [D, D], F32, kind="ExternalInput").ap()
    beta = nc.dram_tensor("beta", [NH, M], F32, kind="ExternalInput").ap()
    w_gate = nc.dram_tensor("w_gate", [I, D], F32, kind="ExternalInput").ap()
    w_up = nc.dram_tensor("w_up", [I, D], F32, kind="ExternalInput").ap()
    w_down = nc.dram_tensor("w_down", [D, I], F32, kind="ExternalInput").ap()
    y_out = nc.dram_tensor("y", [S, D], F32, kind="ExternalOutput").ap()

    xs = [nc.dram_tensor(f"xs{i}", [S, D], F32).ap() for i in range(2)]
    wgq = nc.dram_tensor("wgq", [8, 22, 128, 128], BF16).ap()
    wuq = nc.dram_tensor("wuq", [8, 22, 128, 128], BF16).ap()
    wdq = nc.dram_tensor("wdq", [22, 8, 128, 128], BF16).ap()
    winT = nc.dram_tensor("winT", [8, 24, 128, 128], BF16).ap()
    woutT = nc.dram_tensor("woutT", [8, 8, 128, 128], BF16).ap()
    xnT_h = nc.dram_tensor("xnT", [NCHUNK, D, 128], BF16).ap()

    st = ExitStack()
    sb = lambda name, shape, dt=F32, side="left": st.enter_context(
        nc.sbuf_tensor(name, shape, dt, side=side))

    xb = [sb(f"xb{i}", [128, D]) for i in range(4)]
    ab = sb("ab", [128, D])
    zb = sb("zb", [128, I])
    qq = sb("qq", [128, I], BF16)
    scr = sb("scr", [128, D])
    hb = sb("hb", [128, IH])
    msum = sb("msum", [128, NCHUNK])
    sqs = sb("sqs", [128, NCHUNK])
    mcol = sb("mcol", [128, NCHUNK])
    rstd = sb("rstd", [128, NCHUNK])
    rst6 = sb("rst6", [128, NCHUNK])
    wvar = sb("wvar", [128, NCHUNK])
    stmp = sb("stmp", [128, NCHUNK])
    maxa = sb("maxa", [128, I // 16])
    rm6 = sb("rm6", [128, I // 16])
    sca = sb("sca", [128, I // 16])
    ident = sb("ident", [128, 128], BF16)
    identf = sb("identf", [128, 128])
    betac = sb("betac", [128, NPAIR * M])

    ps_a = st.enter_context(nc.psum_tensor([128, 1536], F32))
    ps_b = st.enter_context(nc.psum_tensor([128, 1536], F32))
    psA_t = st.enter_context(nc.psum_tensor([128, 1024], BF16))
    psB_t = st.enter_context(nc.psum_tensor([128, 1024], BF16))
    slotA = psA_t[:, 0:128]
    slotB = psB_t[:, 0:128]

    names = ["dmx", "dmw", "vp", "sp", "tp", "gp", "stq", "svv", "ph"]
    _sems = {n: Sem(st.enter_context(nc.semaphore(n))) for n in names}
    dmx, dmw, vp, sp, tp, gp, stq, svv, ph = (_sems[n] for n in names)

    prog = {k: [] for k in ("sync", "vector", "scalar", "tensor", "gpsimd")}
    ENGS = list(prog.keys())

    def step(eng, waits, incs, body):
        w = [(s.h, v) for (s, v) in waits]
        ii = [(s.h, k) for (s, k) in incs]
        for s, k in incs:
            s.n += k

        def f(e, w=w, ii=ii, body=body):
            for h, v in w:
                if v > 0:
                    e.wait_ge(h, v)
            if body is None:
                assert not ii
                return
            ins = body(e)
            for h, k in ii:
                ins.then_inc(h, k)
        prog[eng].append(f)

    def barrier():
        tgt = ph.n + 5
        for eng in ENGS:
            def f(e, tgt=tgt):
                e.sem_inc(ph.h, 1)
                e.wait_ge(ph.h, tgt)
            prog[eng].append(f)
        ph.n = tgt

    slot_free = [0, 0]

    def xpose_evac(srcs, dsts, scales=None, evac_waits=()):
        n = len(srcs)
        for i in range(n):
            si = i % 2
            slot = slotA if si == 0 else slotB
            step("tensor",
                 [(sp, slot_free[si])],
                 [(tp, 1)],
                 lambda t, slot=slot, src=srcs[i]: t.transpose(slot, src, ident[:]))
            scl = scales[i] if scales else None
            ew = list(evac_waits) if i == 0 else []
            if scl is None:
                step("scalar", [(tp, tp.n)] + ew, [(sp, 1)],
                     lambda s, slot=slot, dst=dsts[i]: s.copy(dst, slot))
            else:
                step("scalar", [(tp, tp.n)] + ew, [(sp, 1)],
                     lambda s, slot=slot, dst=dsts[i], scl=scl:
                     s.activation(dst, slot, AF.Copy, scale=scl))
            slot_free[si] = sp.n

    step("gpsimd", [], [(gp, 1)], lambda g: g.memset(ident[:], 0.0))

    gwait = gp.n

    def g_pro2(g, gwait=gwait):
        g.wait_ge(gp.h, gwait)
        g.affine_select(out=ident[:], in_=ident[:],
                        compare_op=ALU.not_equal, fill=1.0, base=0,
                        pattern=[[-1, 128]], channel_multiplier=1)
        return g.memset(identf[:], 0.0)
    step("gpsimd", [], [(gp, 1)], g_pro2)

    gwait2 = gp.n

    def g_pro3(g, gwait2=gwait2):
        g.wait_ge(gp.h, gwait2)
        return g.affine_select(out=identf[:], in_=identf[:],
                               compare_op=ALU.not_equal, fill=1.0, base=0,
                               pattern=[[-1, 128]], channel_multiplier=1)
    step("gpsimd", [], [(gp, 1)], g_pro3)

    def sy_beta(sy):
        ins = None
        for h in range(NPAIR):
            if ins is not None:
                ins.then_inc(dmw.h, 16)
            ins = sy.dma_start(out=betac[0:64, h * M:(h + 1) * M],
                               in_=beta[2 * h:2 * h + 1, :].broadcast_to([64, M]))
            ins.then_inc(dmw.h, 16)
            ins = sy.dma_start(out=betac[64:128, h * M:(h + 1) * M],
                               in_=beta[2 * h + 1:2 * h + 2, :].broadcast_to([64, M]))
        return ins
    dmw.n += 15 * 16
    step("sync", [], [(dmw, 16)], sy_beta)

    def plan_quant(waits, incs, src_ap, ncols, fold_imm, fold_ap=None):
        sv0 = svv.n
        svv.n += 2

        def body(v, sv0=sv0):
            nblk = ncols // 16
            v.reduce_max(maxa[:, 0:nblk], r16(src_ap), axis=AXX,
                         apply_absolute_value=True).then_inc(svv.h, 1)
            v.wait_ge(svv.h, sv0 + 1)
            v.tensor_scalar(maxa[:, 0:nblk], maxa[:, 0:nblk], 6e-12, None, ALU.max)
            v.reciprocal(rm6[:, 0:nblk], maxa[:, 0:nblk]).then_inc(svv.h, 1)
            v.wait_ge(svv.h, sv0 + 2)
            v.tensor_scalar(rm6[:, 0:nblk], rm6[:, 0:nblk], 6.0, None, ALU.mult)
            if fold_ap is None:
                v.tensor_scalar(sca[:, 0:nblk], maxa[:, 0:nblk], fold_imm / 6.0,
                                None, ALU.mult)
            else:
                v.tensor_scalar(sca[:, 0:nblk], maxa[:, 0:nblk], fold_ap,
                                fold_imm / 6.0, ALU.mult, ALU.mult)
            v.tensor_tensor(r16(zb[:, 0:ncols]), r16(src_ap),
                            bc16(rm6[:, 0:nblk], nblk), ALU.mult)
            return v._custom_dve(FP4_QD, out=r16(qq[:, 0:ncols]),
                                 in0=r16(zb[:, 0:ncols]),
                                 in1=bc16(sca[:, 0:nblk], nblk),
                                 s0=FP4_C0, s1=FP4_C1)
        step("vector", waits, incs, body)

    def prep_weight(w_src, n_chunks, dst, quant, fold=1.0, wst=None):
        wst = _wst[0]
        ncols = w_src.shape[1]
        nkb = ncols // 128
        for r in range(n_chunks):
            step("sync", [(vp, vp.n)], [(dmw, 16)],
                 lambda sy, r=r, ncols=ncols: sy.dma_start(
                     out=zb[:, 0:ncols], in_=w_src[r * 128:(r + 1) * 128, :]))
            if quant:
                plan_quant([(dmw, dmw.n), (tp, tp.n)], [(vp, 1)],
                           zb[:, 0:ncols], ncols, fold)
            else:
                step("vector", [(dmw, dmw.n), (tp, tp.n)], [(vp, 1)],
                     lambda v, ncols=ncols: v.tensor_copy(qq[:, 0:ncols],
                                                          zb[:, 0:ncols]))
            step("tensor", [(vp, vp.n)], [], None)
            xpose_evac([qq[:, kb * 128:(kb + 1) * 128] for kb in range(nkb)],
                       [wst[:, kb * 128:(kb + 1) * 128] for kb in range(nkb)])
            step("gpsimd", [(sp, sp.n), (stq, stq.n)], [(stq, 16)],
                 lambda g, r=r, nkb=nkb: g.dma_start(
                     out=dst[:, r][0:nkb].rearrange("kb p t -> p kb t"),
                     in_=wst[:, 0:nkb * 128].rearrange("p (kb t) -> p kb t", kb=nkb)))
            step("scalar", [(stq, stq.n)], [], None)

    _wst = [None]
    with nc.sbuf_tensor("wst", [128, 22 * 128], BF16) as wst:
        _wst[0] = wst
        if "ffn1" in phases or "ffn2" in phases:
            prep_weight(w_gate, 22, wgq, True)
            prep_weight(w_up, 22, wuq, True)
            prep_weight(w_down, 8, wdq, True, fold=0.5)
        if "attn" in phases:
            prep_weight(w_in, 24, winT, False)
            prep_weight(w_out, 8, woutT, False)
        barrier()

    def ln_stats(x_hbm):
        for c in range(NCHUNK):
            step("sync", [(vp, vp.n - 1)] if c >= 2 else [], [(dmx, 16)],
                 lambda sy, c=c: sy.dma_start(out=xb[c % 2][:],
                                              in_=x_hbm[c * 128:(c + 1) * 128, :]))

            def stats_body(v, c=c):
                v.reduce_sum(msum[:, c:c + 1], xb[c % 2][:], axis=AXX)
                return v.scalar_tensor_tensor(scr[:], xb[c % 2][:], 1.0, xb[c % 2][:],
                                              ALU.mult, ALU.mult,
                                              accum_out=sqs[:, c:c + 1])
            step("vector", [(dmx, dmx.n)], [(vp, 1)], stats_body)

        sv0 = svv.n
        svv.n += 1

        def smalls(v, sv0=sv0):
            v.tensor_copy(stmp[:], msum[:]).then_inc(svv.h, 1)
            v.wait_ge(svv.h, sv0 + 1)
            v.tensor_scalar(mcol[:], msum[:], 1.0 / D, None, ALU.mult)
            v.tensor_scalar(wvar[:], sqs[:], 1.0 / D, None, ALU.mult)
            v.tensor_tensor(stmp[:], mcol[:], mcol[:], ALU.mult)
            v.tensor_tensor(wvar[:], wvar[:], stmp[:], ALU.subtract)
            return v.tensor_scalar(wvar[:], wvar[:], 1e-5, None, ALU.add)
        step("vector", [], [(vp, 1)], smalls)

        def sqrt_body(s):
            s.preload_act_table(AF.Sqrt)
            ins = s.activation(rstd[:], wvar[:], AF.Sqrt)
            s.preload_act_table(AF.Silu)
            return ins
        step("scalar", [(vp, vp.n)], [(sp, 1)], sqrt_body)

        sv0 = svv.n
        svv.n += 1

        def refine(v, sv0=sv0):
            v.reciprocal(rstd[:], rstd[:]).then_inc(svv.h, 1)
            v.wait_ge(svv.h, sv0 + 1)
            v.tensor_tensor(stmp[:], rstd[:], rstd[:], ALU.mult)
            v.tensor_tensor(stmp[:], stmp[:], wvar[:], ALU.mult)
            v.tensor_scalar(stmp[:], stmp[:], -0.5, 1.5, ALU.mult, ALU.add)
            v.tensor_tensor(rstd[:], rstd[:], stmp[:], ALU.mult)
            return v.tensor_scalar(rst6[:], rstd[:], 1.0 / 6.0, None, ALU.mult)
        step("vector", [(sp, sp.n)], [(vp, 1)], refine)

    _ffn_ctr = [0]

    def ffn_phase(x_src, x_dst):
        _ffn_ctr[0] += 1
        tag = f"_f{_ffn_ctr[0]}"
        ln_stats(x_src)
        with ExitStack() as fst:
            fsb = lambda name, shape, dt=F32: fst.enter_context(
                nc.sbuf_tensor(name + tag, shape, dt))
            wgh = fsb("wgh", [128, 8 * IH], BF16)
            wuh = fsb("wuh", [128, 8 * IH], BF16)
            wdn = fsb("wdn", [128, 22 * D], BF16)
            hqT = fsb("hqT", [128, 22 * GRP * 128], BF16)
            xqT = [fsb(f"xqT{i}", [128, D], BF16) for i in range(GRP)]
            sgb = fsb("sgb", [128, IH])
            sp_free_a = [0]
            vp_free_b = [0]
            sp_xq = [0]
            resid_done = [vp.n] * 4
            store_done = [0] * 4

            def load_wdn(sy):
                ins = None
                for ob in range(8):
                    if ins is not None:
                        ins.then_inc(dmw.h, 16)
                    ins = sy.dma_start(
                        out=wdn[:].rearrange("p (ib ob t) -> p ib ob t",
                                             ib=22, ob=8)[:, :, ob],
                        in_=wdq[:, ob].rearrange("ib p t -> p ib t"))
                return ins
            dmw.n += 7 * 16
            step("sync", [], [(dmw, 16)], load_wdn)

            for g in range(NGRP):
                for ci in range(GRP):
                    c = g * GRP + ci
                    step("sync", [(vp, resid_done[ci]), (stq, store_done[ci])],
                         [(dmx, 16)],
                         lambda sy, c=c, ci=ci: sy.dma_start(
                             out=xb[ci][:],
                             in_=x_src[c * 128:(c + 1) * 128, :]))
                    sv0 = svv.n
                    svv.n += 2

                    def xquant(v, c=c, ci=ci, sv0=sv0):
                        v.tensor_scalar(ab[:], xb[ci][:], mcol[:, c:c + 1],
                                        None, ALU.subtract)
                        nblk = D // 16
                        v.reduce_max(maxa[:, 0:nblk], r16(ab[:]), axis=AXX,
                                     apply_absolute_value=True).then_inc(svv.h, 1)
                        v.wait_ge(svv.h, sv0 + 1)
                        v.tensor_scalar(maxa[:, 0:nblk], maxa[:, 0:nblk],
                                        6e-12, None, ALU.max)
                        v.reciprocal(rm6[:, 0:nblk], maxa[:, 0:nblk]
                                     ).then_inc(svv.h, 1)
                        v.wait_ge(svv.h, sv0 + 2)
                        v.tensor_scalar(rm6[:, 0:nblk], rm6[:, 0:nblk], 6.0,
                                        None, ALU.mult)
                        v.tensor_scalar(sca[:, 0:nblk], maxa[:, 0:nblk],
                                        rst6[:, c:c + 1], None, ALU.mult)
                        v.tensor_tensor(r16(zb[:, 0:D]), r16(ab[:]),
                                        bc16(rm6[:, 0:nblk], nblk), ALU.mult)
                        return v._custom_dve(
                            FP4_QD, out=r16(qq[:, 0:D]), in0=r16(zb[:, 0:D]),
                            in1=bc16(sca[:, 0:nblk], nblk),
                            s0=FP4_C0, s1=FP4_C1)
                    step("vector", [(dmx, dmx.n), (tp, tp.n)], [(vp, 1)], xquant)
                    step("tensor", [(vp, vp.n)], [], None)
                    xpose_evac(
                        [qq[:, kb * 128:(kb + 1) * 128] for kb in range(8)],
                        [xqT[ci][:, kb * 128:(kb + 1) * 128] for kb in range(8)])
                sp_xq[0] = sp.n

                pending_hq = [None]

                def flush_hq(vp_tgt):
                    if pending_hq[0] is None:
                        return
                    ih0, ci0 = pending_hq[0]
                    pending_hq[0] = None
                    step("tensor", [(vp, vp_tgt)], [], None)
                    xpose_evac(
                        [qq[:, ib * 128:(ib + 1) * 128] for ib in range(11)],
                        [hqT[:, ((ih0 * 11 + ib) * GRP + ci0) * 128:
                             ((ih0 * 11 + ib) * GRP + ci0) * 128 + 128]
                         for ib in range(11)])

                for ih in range(2):
                    def load_guh(sy, ih=ih):
                        ins = None
                        for kb in range(8):
                            if ins is not None:
                                ins.then_inc(dmw.h, 16)
                            ins = sy.dma_start(
                                out=wgh[:, kb * IH:(kb + 1) * IH].rearrange(
                                    "p (r t) -> p r t", r=11),
                                in_=wgq[kb, ih * 11:(ih + 1) * 11].rearrange(
                                    "r p t -> p r t"))
                            ins.then_inc(dmw.h, 16)
                            ins = sy.dma_start(
                                out=wuh[:, kb * IH:(kb + 1) * IH].rearrange(
                                    "p (r t) -> p r t", r=11),
                                in_=wuq[kb, ih * 11:(ih + 1) * 11].rearrange(
                                    "r p t -> p r t"))
                        return ins
                    dmw.n += 15 * 16
                    step("sync", [(tp, tp.n)], [(dmw, 16)], load_guh)

                    for ci in range(GRP):
                        c = g * GRP + ci

                        def gup(t, ci=ci):
                            for kb in range(8):
                                lhs = xqT[ci][:, kb * 128:(kb + 1) * 128]
                                first, last = kb == 0, kb == 7
                                for nw in range(3):
                                    n0, n1 = nw * 512, min((nw + 1) * 512, IH)
                                    t.matmul(ps_a[:, n0:n1], lhs,
                                             wgh[:, kb * IH + n0:kb * IH + n1],
                                             start=first, stop=last)
                                    ins = t.matmul(ps_b[:, n0:n1], lhs,
                                                   wuh[:, kb * IH + n0:kb * IH + n1],
                                                   start=first, stop=last)
                            return ins
                        step("tensor",
                             [(sp, max(sp_free_a[0], sp_xq[0])),
                              (vp, vp_free_b[0]), (dmw, dmw.n)], [(tp, 1)], gup)
                        tp_gup = tp.n

                        step("scalar", [(tp, tp_gup), (vp, vp_free_b[0])], [(sp, 1)],
                             lambda s: s.activation(sgb[:], ps_a[:, 0:IH], AF.Silu))
                        sp_free_a[0] = sp.n

                        hq_vp = vp.n
                        flush_hq(hq_vp)
                        tp_xp = tp.n

                        def hmult(v):
                            return v.tensor_tensor(hb[:], sgb[:], ps_b[:, 0:IH],
                                                   ALU.mult)
                        step("vector", [(sp, sp_free_a[0]), (tp, tp_gup)],
                             [(vp, 1)], hmult)
                        vp_free_b[0] = vp.n

                        sv0 = svv.n
                        svv.n += 2

                        def hquant(v, sv0=sv0):
                            nblk = IH // 16
                            v.reduce_max(maxa[:, 0:nblk], r16(hb[:]), axis=AXX,
                                         apply_absolute_value=True).then_inc(svv.h, 1)
                            v.wait_ge(svv.h, sv0 + 1)
                            v.tensor_scalar(maxa[:, 0:nblk], maxa[:, 0:nblk], 6e-12,
                                            None, ALU.max)
                            v.reciprocal(rm6[:, 0:nblk], maxa[:, 0:nblk]
                                         ).then_inc(svv.h, 1)
                            v.wait_ge(svv.h, sv0 + 2)
                            v.tensor_scalar(rm6[:, 0:nblk], rm6[:, 0:nblk], 6.0,
                                            None, ALU.mult)
                            v.tensor_scalar(sca[:, 0:nblk], maxa[:, 0:nblk],
                                            1.0 / 6.0, None, ALU.mult)
                            v.tensor_tensor(r16(zb[:, 0:IH]), r16(hb[:]),
                                            bc16(rm6[:, 0:nblk], nblk), ALU.mult)
                            return v._custom_dve(
                                FP4_QD, out=r16(qq[:, 0:IH]), in0=r16(zb[:, 0:IH]),
                                in1=bc16(sca[:, 0:nblk], nblk), s0=FP4_C0, s1=FP4_C1)
                        step("vector", [(tp, tp_xp)], [(vp, 1)], hquant)
                        pending_hq[0] = (ih, ci)

                flush_hq(vp.n)
                for ci in range(GRP):
                    c = g * GRP + ci

                    def down(t, ci=ci):
                        for ib in range(22):
                            lhs = hqT[:, (ib * GRP + ci) * 128:
                                      (ib * GRP + ci) * 128 + 128]
                            for nw in range(2):
                                ins = t.matmul(
                                    ps_a[:, nw * 512:(nw + 1) * 512], lhs,
                                    wdn[:, ib * D + nw * 512:ib * D + (nw + 1) * 512],
                                    start=(ib == 0), stop=(ib == 21))
                        return ins
                    step("tensor", [(sp, sp.n)], [(tp, 1)], down)
                    step("vector", [(tp, tp.n)], [(vp, 1)],
                         lambda v, ci=ci: v.tensor_tensor(xb[ci][:], xb[ci][:],
                                                          ps_a[:, 0:D], ALU.add))
                    resid_done[ci] = vp.n
                    step("tensor", [(vp, vp.n)], [], None)
                    step("gpsimd", [(vp, vp.n)], [(stq, 16)],
                         lambda g2, c=c, ci=ci: g2.dma_start(
                             out=x_dst[c * 128:(c + 1) * 128, :], in_=xb[ci][:]))
                    store_done[ci] = stq.n

            step("gpsimd", [(stq, stq.n)], [], None)
        barrier()

    def attn_phase(x_src, x_dst):
        ln_stats(x_src)
        with ExitStack() as kst:
            kvacc = kst.enter_context(
                nc.sbuf_tensor("kvacc", [128, NPAIR * M * 128], F32))
            kvar = kst.enter_context(
                nc.sbuf_tensor("kvar", [128, NPAIR * M * 128], BF16, side="right"))
            with ExitStack() as ast:
                asb = lambda name, shape, dt=F32: ast.enter_context(
                    nc.sbuf_tensor(name, shape, dt))
                wkv = asb("wkv", [128, 8 * 2048], BF16)
                tkall = asb("tkall", [128, SUB * M * D], BF16)
                vgrp = asb("vgrp", [128, SUB * D], BF16)
                scr2b = asb("scr2b", [128, D])
                xqTa = [asb(f"xqTa{i}", [128, D], BF16) for i in range(2)]

                def load_wkv(sy):
                    ins = None
                    for kb in range(8):
                        if ins is not None:
                            ins.then_inc(dmw.h, 16)
                        ins = sy.dma_start(
                            out=wkv[:, kb * 2048:(kb + 1) * 2048].rearrange(
                                "p (r t) -> p r t", r=16),
                            in_=winT[kb, 8:24].rearrange("r p t -> p r t"))
                    return ins
                dmw.n += 7 * 16
                step("sync", [], [(dmw, 16)], load_wkv)

                for sg in range(NCHUNK // SUB):
                    kevac_done = [0, 0]
                    vevac_done = [0, 0]
                    pending_tk = []
                    for ci in range(SUB):
                        c = sg * SUB + ci
                        step("sync", [(vp, vp.n)], [(dmx, 16)],
                             lambda sy, c=c, ci=ci: sy.dma_start(
                                 out=xb[ci][:], in_=x_src[c * 128:(c + 1) * 128, :]))
                        step("vector", [(dmx, dmx.n), (tp, tp.n)], [(vp, 1)],
                             lambda v, c=c, ci=ci: v.tensor_scalar(
                                 qq[:, 0:D], xb[ci][:], mcol[:, c:c + 1],
                                 rstd[:, c:c + 1], ALU.subtract, ALU.mult))
                        step("tensor", [(vp, vp.n)], [], None)
                        xpose_evac(
                            [qq[:, kb * 128:(kb + 1) * 128] for kb in range(8)],
                            [xqTa[ci][:, kb * 128:(kb + 1) * 128] for kb in range(8)],
                            evac_waits=[(stq, stq.n)])
                        step("gpsimd", [(sp, sp.n)], [(stq, 16)],
                             lambda g2, c=c, ci=ci: g2.dma_start(
                                 out=xnT_h[c].rearrange("(kb kp) t -> kp kb t",
                                                        kp=128),
                                 in_=xqTa[ci][:].rearrange("p (kb t) -> p kb t",
                                                           kb=8)))

                        def kvproj(t, ci=ci):
                            for kb in range(8):
                                lhs = xqTa[ci][:, kb * 128:(kb + 1) * 128]
                                for nw in range(4):
                                    ps = ps_a if nw < 2 else ps_b
                                    ins = t.matmul(
                                        ps[:, (nw % 2) * 512:(nw % 2 + 1) * 512], lhs,
                                        wkv[:, kb * 2048 + nw * 512:
                                            kb * 2048 + (nw + 1) * 512],
                                        start=(kb == 0), stop=(kb == 7))
                            return ins
                        step("tensor",
                             [(dmw, dmw.n), (vp, kevac_done[0]),
                              (sp, vevac_done[0])],
                             [(tp, 1)], kvproj)
                        tp_proj = tp.n

                        def kevac(v):
                            v.tensor_scalar(ab[:], ps_a[:, 0:D], 0.125, 1.0,
                                            ALU.mult, ALU.min)
                            return v.tensor_scalar(ab[:], ab[:], -1.0, None, ALU.max)
                        step("vector", [(tp, tp_proj)], [(vp, 1)], kevac)
                        kevac_done[0] = vp.n
                        step("scalar", [(tp, tp_proj)], [(sp, 1)],
                             lambda s, ci=ci: s.copy(vgrp[:, ci * D:(ci + 1) * D],
                                                     ps_b[:, 0:D]))
                        vevac_done[0] = sp.n

                        def tksave(v, ci=ci):
                            base = ci * M * D
                            v.memset(tkall[:, base:base + D], 1.0)
                            v.tensor_copy(tkall[:, base + D:base + 2 * D], ab[:])
                            dstk = scr[:] if ci == 0 else scr2b[:]
                            return v.tensor_scalar(dstk, ab[:], 2.0, None, ALU.mult)
                        step("vector", [], [(vp, 1)], tksave)

                        def tkrec(v, ci=ci):
                            base = ci * M * D
                            k2 = scr[:] if ci == 0 else scr2b[:]
                            v.memset(zb[:, 0:D], 1.0)
                            v.tensor_scalar(zb[:, D:2 * D], k2, 0.5, None, ALU.mult)
                            for m in range(2, M):
                                cur, prv = (m % 2) * D, ((m - 1) % 2) * D
                                v.tensor_tensor(hb[:, 0:D], k2,
                                                zb[:, prv:prv + D], ALU.mult)
                                v.tensor_tensor(zb[:, cur:cur + D], hb[:, 0:D],
                                                zb[:, cur:cur + D], ALU.subtract)
                                ins = v.tensor_copy(
                                    tkall[:, base + m * D:base + (m + 1) * D],
                                    zb[:, cur:cur + D])
                            return ins
                        pending_tk.append(tkrec)
                        if ci == 0:
                            pass
                        else:
                            for f in pending_tk:
                                step("vector", [], [(vp, 1)], f)
                            pending_tk = []

                    for pp in range(NPAIR // 2):
                        def kvmm(t, sg=sg, pp=pp):
                            for half in range(2):
                                pair = pp * 2 + half
                                ps = ps_a if half == 0 else ps_b
                                for ci in range(SUB):
                                    lhs = vgrp[:, ci * D + pair * 128:
                                               ci * D + (pair + 1) * 128]
                                    for mg in range(3):
                                        m0 = mg * 4
                                        mN = min(4, M - m0)
                                        rhs = tkall[:].rearrange(
                                            "p (c m t) -> p c m t", c=SUB, m=M)[
                                            :, ci, m0:m0 + mN,
                                            pair * 128:(pair + 1) * 128]
                                        ins = t.matmul(
                                            ps[:, m0 * 128:(m0 + mN) * 128], lhs, rhs,
                                            start=(ci == 0), stop=(ci == SUB - 1))
                            return ins
                        step("tensor", [(vp, vp.n)], [(tp, 1)], kvmm)

                        def kvadd(v, sg=sg, pp=pp):
                            for half in range(2):
                                pair = pp * 2 + half
                                ps = ps_a if half == 0 else ps_b
                                dst = kvacc[:, pair * M * 128:(pair + 1) * M * 128]
                                if sg == 0:
                                    ins = v.tensor_copy(dst, ps[:, 0:M * 128])
                                else:
                                    ins = v.tensor_tensor(dst, dst, ps[:, 0:M * 128],
                                                          ALU.add)
                            return ins
                        step("vector", [(tp, tp.n)], [(vp, 1)], kvadd)
                        step("tensor", [(vp, vp.n)], [], None)

                step("gpsimd", [(stq, stq.n)], [], None)
                barrier()

                step("gpsimd", [], [(gp, 1)], lambda g2: g2.memset(kvar[:], 0.0))
                step("tensor", [(gp, gp.n)], [], None)
                arr_base = sp.n
                for pair in range(NPAIR):
                    for m in range(M):
                        j = pair * M + m
                        slot = ps_a[:, 0:128] if j % 2 == 0 else ps_b[:, 0:128]
                        step("tensor",
                             [(sp, arr_base + 2 * (j - 1))] if j >= 2 else [],
                             [(tp, 1)],
                             lambda t, pair=pair, m=m, slot=slot: t.transpose(
                                 slot,
                                 kvacc[:, (pair * M + m) * 128:
                                       (pair * M + m + 1) * 128],
                                 identf[:]))

                        def kvevac(s, pair=pair, m=m, slot=slot):
                            base = (pair * M + m) * 128
                            s.activation(kvar[0:64, base:base + 64], slot[0:64, 0:64],
                                         AF.Copy,
                                         scale=betac[0:64, pair * M + m:
                                                     pair * M + m + 1])
                            return s.activation(
                                kvar[64:128, base + 64:base + 128],
                                slot[64:128, 64:128], AF.Copy,
                                scale=betac[64:128, pair * M + m:pair * M + m + 1])
                        step("scalar", [(tp, tp.n)], [(sp, 2)], kvevac)
                step("tensor", [(sp, sp.n)], [], None)
                barrier()

            with ExitStack() as bst:
                bsb2 = lambda name, shape, dt=F32: bst.enter_context(
                    nc.sbuf_tensor(name, shape, dt))
                wq8 = bsb2("wq8", [128, 8 * D], BF16)
                woT = bsb2("woT", [128, 8 * D], BF16)
                xbT = bsb2("xbT", [128, 8 * D], BF16)
                tqa = bsb2("tqa", [128, M * D], BF16)
                aoT = bsb2("aoT", [128, 8 * D], BF16)
                qf = bsb2("qf", [128, D])

                def load_wb(sy):
                    ins = None
                    for kb in range(8):
                        if ins is not None:
                            ins.then_inc(dmw.h, 16)
                        ins = sy.dma_start(
                            out=wq8[:, kb * D:(kb + 1) * D].rearrange(
                                "p (r t) -> p r t", r=8),
                            in_=winT[kb, 0:8].rearrange("r p t -> p r t"))
                        ins.then_inc(dmw.h, 16)
                        ins = sy.dma_start(
                            out=woT[:, kb * D:(kb + 1) * D].rearrange(
                                "p (r t) -> p r t", r=8),
                            in_=woutT[kb, :].rearrange("r p t -> p r t"))
                    return ins
                dmw.n += 15 * 16
                step("sync", [], [(dmw, 16)], load_wb)

                bstore = [0] * 4
                for stile in range(4):
                    c0 = stile * 8

                    def load_x(sy, c0=c0):
                        ins = None
                        for kb in range(8):
                            if ins is not None:
                                ins.then_inc(dmx.h, 16)
                            ins = sy.dma_start(
                                out=xbT[:, kb * D:(kb + 1) * D].rearrange(
                                    "p (c t) -> p c t", c=8),
                                in_=xnT_h[c0:c0 + 8, kb * 128:(kb + 1) * 128]
                                .rearrange("c kp t -> kp c t"))
                        return ins
                    dmx.n += 7 * 16
                    step("sync", [(vp, vp.n)], [(dmx, 16)], load_x)

                    qf_done = [0]
                    ao_done = [0]
                    pending_out = [None]

                    def plan_outmm(pair):
                        def outmm(t, pair=pair):
                            for m in range(M):
                                lhs = kvar[:, (pair * M + m) * 128:
                                           (pair * M + m + 1) * 128]
                                for nw in range(2):
                                    ins = t.matmul(
                                        ps_b[:, nw * 512:(nw + 1) * 512], lhs,
                                        tqa[:, m * D + nw * 512:
                                            m * D + (nw + 1) * 512],
                                        start=(m == 0), stop=(m == M - 1))
                            return ins
                        step("tensor", [(vp, vp.n), (sp, ao_done[0])],
                             [(tp, 1)], outmm)
                        step("scalar", [(tp, tp.n)], [(sp, 1)],
                             lambda s, pair=pair: s.copy(
                                 aoT[:, pair * D:(pair + 1) * D], ps_b[:, 0:D]))
                        ao_done[0] = sp.n

                    for pair in range(NPAIR):
                        def qproj(t, pair=pair):
                            for kb in range(8):
                                lhs = wq8[:, kb * D + pair * 128:
                                          kb * D + (pair + 1) * 128]
                                for nw in range(2):
                                    ins = t.matmul(
                                        ps_a[:, nw * 512:(nw + 1) * 512], lhs,
                                        xbT[:, kb * D + nw * 512:
                                            kb * D + (nw + 1) * 512],
                                        start=(kb == 0), stop=(kb == 7))
                            return ins
                        step("tensor",
                             [(dmx, dmx.n), (dmw, dmw.n), (vp, qf_done[0])],
                             [(tp, 1)], qproj)
                        tp_q = tp.n

                        def qfevac(v):
                            v.tensor_scalar(qf[:], ps_a[:, 0:1024], 0.125, 1.0,
                                            ALU.mult, ALU.min)
                            return v.tensor_scalar(qf[:], qf[:], -1.0, None, ALU.max)
                        step("vector", [(tp, tp_q)], [(vp, 1)], qfevac)
                        qf_done[0] = vp.n

                        if pending_out[0] is not None:
                            plan_outmm(pending_out[0])
                        tp_out = tp.n

                        def tqrec(v):
                            v.memset(tqa[:, 0:D], 1.0)
                            v.tensor_copy(tqa[:, D:2 * D], qf[:])
                            v.tensor_scalar(scr[:], qf[:], 2.0, None, ALU.mult)
                            v.memset(zb[:, 0:D], 1.0)
                            v.tensor_copy(zb[:, D:2 * D], qf[:])
                            for m in range(2, M):
                                cur, prv = (m % 2) * D, ((m - 1) % 2) * D
                                v.tensor_tensor(hb[:, 0:D], scr[:],
                                                zb[:, prv:prv + D], ALU.mult)
                                v.tensor_tensor(zb[:, cur:cur + D], hb[:, 0:D],
                                                zb[:, cur:cur + D], ALU.subtract)
                                ins = v.tensor_copy(tqa[:, m * D:(m + 1) * D],
                                                    zb[:, cur:cur + D])
                            return ins
                        step("vector", [(tp, tp_out)], [(vp, 1)], tqrec)
                        pending_out[0] = pair
                    plan_outmm(pending_out[0])

                    for ci in range(8):
                        c = c0 + ci
                        step("sync", [(vp, vp.n), (stq, bstore[ci % 4])],
                             [(dmx, 16)],
                             lambda sy, c=c, ci=ci: sy.dma_start(
                                 out=xb[ci % 4][:],
                                 in_=x_src[c * 128:(c + 1) * 128, :]))

                        def woutmm(t, ci=ci):
                            for kb in range(8):
                                lhs = aoT[:, kb * D + ci * 128:kb * D + ci * 128 + 128]
                                for nw in range(2):
                                    ins = t.matmul(
                                        ps_a[:, nw * 512:(nw + 1) * 512], lhs,
                                        woT[:, kb * D + nw * 512:
                                            kb * D + (nw + 1) * 512],
                                        start=(kb == 0), stop=(kb == 7))
                            return ins
                        step("tensor", [(sp, sp.n)], [(tp, 1)], woutmm)
                        step("vector", [(tp, tp.n), (dmx, dmx.n)], [(vp, 1)],
                             lambda v, ci=ci: v.tensor_tensor(
                                 xb[ci % 4][:], xb[ci % 4][:], ps_a[:, 0:D], ALU.add))
                        step("tensor", [(vp, vp.n)], [], None)
                        step("gpsimd", [(vp, vp.n)], [(stq, 16)],
                             lambda g2, c=c, ci=ci: g2.dma_start(
                                 out=x_dst[c * 128:(c + 1) * 128, :],
                                 in_=xb[ci % 4][:]))
                        bstore[ci % 4] = stq.n

                step("gpsimd", [(stq, stq.n)], [], None)
        barrier()

    nph = [p for p in ("ffn1", "attn", "ffn2") if p in phases]
    cur = x_in
    for idx, p in enumerate(nph):
        dst = y_out if idx == len(nph) - 1 else xs[idx % 2]
        if p in ("ffn1", "ffn2"):
            ffn_phase(cur, dst)
        else:
            attn_phase(cur, dst)
        cur = dst
    if cur is not y_out:
        for c in range(NCHUNK):
            dmx_tgt = dmx.n + 16

            def cl(sy, c=c, cur=cur, dmx_tgt=dmx_tgt):
                ins = sy.dma_start(out=xb[0][:], in_=cur[c * 128:(c + 1) * 128, :])
                ins.then_inc(dmx.h, 16)
                sy.wait_ge(dmx.h, dmx_tgt)
                return sy.dma_start(out=y_out[c * 128:(c + 1) * 128, :], in_=xb[0][:])
            step("sync", [(stq, stq.n)], [(stq, 16)], cl)
            dmx.n += 16

    step("sync", [(stq, stq.n)], [], None)

    with nc.Block() as block:
        @block.sync
        def _(e):
            for f in prog["sync"]:
                f(e)

        @block.vector
        def _(e):
            for f in prog["vector"]:
                f(e)

        @block.scalar
        def _(e):
            for f in prog["scalar"]:
                f(e)

        @block.tensor
        def _(e):
            for f in prog["tensor"]:
                f(e)

        @block.gpsimd
        def _(e):
            for f in prog["gpsimd"]:
                f(e)

    st.close()
    mybir.codegen_inst_isa_subclasses(nc)
    return nc


_NC_CACHE = {}


def kernel(L0_attn_x, L0_attn_w_in, L0_attn_w_out, L0_attn_beta,
           L0_ffn_w_gate, L0_ffn_w_up, L0_ffn_w_down):
    x = np.ascontiguousarray(np.asarray(L0_attn_x, dtype=np.float32))
    if "nc" not in _NC_CACHE:
        _NC_CACHE["nc"] = build_program()
    nc = _NC_CACHE["nc"]
    w = {
        "w_in": np.ascontiguousarray(np.asarray(L0_attn_w_in, np.float32)),
        "w_out": np.ascontiguousarray(np.asarray(L0_attn_w_out, np.float32)),
        "beta": np.ascontiguousarray(np.asarray(L0_attn_beta, np.float32)),
        "w_gate": np.ascontiguousarray(np.asarray(L0_ffn_w_gate, np.float32)),
        "w_up": np.ascontiguousarray(np.asarray(L0_ffn_w_up, np.float32)),
        "w_down": np.ascontiguousarray(np.asarray(L0_ffn_w_down, np.float32)),
    }
    in_maps = [dict(w, x=np.ascontiguousarray(x[b])) for b in range(B)]
    res = run_bass_kernel_spmd(nc, in_maps, core_ids=list(range(B)))
    out = np.stack([np.asarray(res.results[b]["y"]) for b in range(B)], axis=0)
    return out.astype(np.float32)


# version 09
# speedup vs baseline: 1.2622x; 1.1324x over previous
import numpy as np

import concourse.bass as bass
import concourse.mybir as mybir
from concourse.bass_utils import run_bass_kernel_spmd
from concourse import dve_ops
from concourse.dve_ops import DveOp, OPS
from concourse.dve_spec import Spec, Src0, Src1, C0, C1, Bin, lower as dve_lower, maxx
from concourse.dve_uop import AluOp as DAlu
from concourse.dve_table_gen import dve_ver_for

F32 = mybir.dt.float32
BF16 = mybir.dt.bfloat16
AF = mybir.ActivationFunctionType
ALU = mybir.AluOpType
AXX = mybir.AxisListType.X

B, S, D, I = 8, 4096, 1024, 2816
NCHUNK = S // 128
GRP = 4
NGRP = NCHUNK // GRP
IH = I // 2
NH, DH, M = 16, 64, 11
NPAIR = NH // 2
SUB = 2


def _fp4_ref(in0, in1, s0, s1, imm2):
    z = np.asarray(in0)
    az = np.abs(z)
    s_ = np.maximum(np.minimum(az, 4.0), 1.0)
    s_ = np.exp2(np.floor(np.log2(s_)))
    h = s_ / 2
    return ((np.round(z / h) * h) * in1).astype(z.dtype)


def _register_fp4():
    for op in OPS:
        if op.name == "FP4_QD":
            return op
    c_ = maxx(Bin(DAlu.BITWISE_AND, Bin(DAlu.MULTIPLY, Src0, C1), C0), C1)
    X_ = Bin(DAlu.ADD, Bin(DAlu.ADD, c_, c_), c_)
    rq_ = Bin(DAlu.SUBTRACT, Bin(DAlu.ADD, Src0, X_), X_)
    spec = Spec(body=Bin(DAlu.MULTIPLY, rq_, Src1))
    ver = dve_ver_for("TRN2")
    sha = dve_lower(spec, ver=ver, name="FP4_QD").sha(ver)
    op = DveOp("FP4_QD", spec, subdim=False, uops_sha={ver: sha}, reference=_fp4_ref)
    OPS.append(op)
    dve_ops._SUB_OPCODE_FOR_NAME[op.name] = dve_ops._CUSTOM_DVE_ROW_BASE + len(OPS) - 1
    dve_ops.CUSTOM_DVE_SPECS[op.name] = spec
    dve_ops.CUSTOM_DVE_REFS[op.name] = _fp4_ref
    return op


FP4_QD = _register_fp4()
FP4_C0 = float(2 ** 24)
FP4_C1 = float(2 ** 21)


class Sem:
    def __init__(self, h):
        self.h = h
        self.n = 0


def bc16(t, nblk):
    return t[:, :, None].broadcast_to([128, nblk, 16])


def r16(ap):
    return ap.rearrange("p (b k) -> p b k", k=16)


def build_program(phases=("ffn1", "attn", "ffn2")):
    from contextlib import ExitStack

    nc = bass.Bass("TRN2", target_bir_lowering=False,
                   detect_race_conditions=False)

    x_in = nc.dram_tensor("x", [S, D], F32, kind="ExternalInput").ap()
    w_in = nc.dram_tensor("w_in", [3 * D, D], F32, kind="ExternalInput").ap()
    w_out = nc.dram_tensor("w_out", [D, D], F32, kind="ExternalInput").ap()
    beta = nc.dram_tensor("beta", [NH, M], F32, kind="ExternalInput").ap()
    w_gate = nc.dram_tensor("w_gate", [I, D], F32, kind="ExternalInput").ap()
    w_up = nc.dram_tensor("w_up", [I, D], F32, kind="ExternalInput").ap()
    w_down = nc.dram_tensor("w_down", [D, I], F32, kind="ExternalInput").ap()
    y_out = nc.dram_tensor("y", [S, D], F32, kind="ExternalOutput").ap()

    xs = [nc.dram_tensor(f"xs{i}", [S, D], F32).ap() for i in range(2)]
    wgq = nc.dram_tensor("wgq", [8, 22, 128, 128], BF16).ap()
    wuq = nc.dram_tensor("wuq", [8, 22, 128, 128], BF16).ap()
    wdq = nc.dram_tensor("wdq", [22, 8, 128, 128], BF16).ap()
    winT = nc.dram_tensor("winT", [8, 24, 128, 128], BF16).ap()
    woutT = nc.dram_tensor("woutT", [8, 8, 128, 128], BF16).ap()
    xnT_h = nc.dram_tensor("xnT", [NCHUNK, D, 128], BF16).ap()

    st = ExitStack()
    sb = lambda name, shape, dt=F32, side="left": st.enter_context(
        nc.sbuf_tensor(name, shape, dt, side=side))

    xb = [sb(f"xb{i}", [128, D]) for i in range(4)]
    ab = sb("ab", [128, D])
    zb = sb("zb", [128, max(I, 3 * 1024)])
    qq = sb("qq", [128, I], BF16)
    scr = sb("scr", [128, D])
    hb = sb("hb", [128, IH])
    msum = sb("msum", [128, NCHUNK])
    sqs = sb("sqs", [128, NCHUNK])
    mcol = sb("mcol", [128, NCHUNK])
    rstd = sb("rstd", [128, NCHUNK])
    rst6 = sb("rst6", [128, NCHUNK])
    wvar = sb("wvar", [128, NCHUNK])
    stmp = sb("stmp", [128, NCHUNK])
    maxa = sb("maxa", [128, I // 16])
    rm6 = sb("rm6", [128, I // 16])
    sca = sb("sca", [128, I // 16])
    ident = sb("ident", [128, 128], BF16)
    identf = sb("identf", [128, 128])
    betac = sb("betac", [128, NPAIR * M])

    ps_a = st.enter_context(nc.psum_tensor([128, 1536], F32))
    ps_b = st.enter_context(nc.psum_tensor([128, 1536], F32))
    psA_t = st.enter_context(nc.psum_tensor([128, 1024], BF16))
    psB_t = st.enter_context(nc.psum_tensor([128, 1024], BF16))
    slotA = psA_t[:, 0:128]
    slotB = psB_t[:, 0:128]

    names = ["dmx", "dmw", "dmu", "vp", "sp", "tp", "gp", "stq", "svv", "ph"]
    _sems = {n: Sem(st.enter_context(nc.semaphore(n))) for n in names}
    dmx, dmw, dmu, vp, sp, tp, gp, stq, svv, ph = (_sems[n] for n in names)

    prog = {k: [] for k in ("sync", "vector", "scalar", "tensor", "gpsimd")}
    ENGS = list(prog.keys())

    def step(eng, waits, incs, body):
        w = [(s.h, v) for (s, v) in waits]
        ii = [(s.h, k) for (s, k) in incs]
        for s, k in incs:
            s.n += k

        def f(e, w=w, ii=ii, body=body):
            for h, v in w:
                if v > 0:
                    e.wait_ge(h, v)
            if body is None:
                assert not ii
                return
            ins = body(e)
            for h, k in ii:
                ins.then_inc(h, k)
        prog[eng].append(f)

    def barrier():
        tgt = ph.n + 5
        for eng in ENGS:
            def f(e, tgt=tgt):
                e.sem_inc(ph.h, 1)
                e.wait_ge(ph.h, tgt)
            prog[eng].append(f)
        ph.n = tgt

    slot_free = [0, 0]

    def xpose_evac(srcs, dsts, scales=None, evac_waits=()):
        n = len(srcs)
        for i in range(n):
            si = i % 2
            slot = slotA if si == 0 else slotB
            step("tensor",
                 [(sp, slot_free[si])],
                 [(tp, 1)],
                 lambda t, slot=slot, src=srcs[i]: t.transpose(slot, src, ident[:]))
            scl = scales[i] if scales else None
            ew = list(evac_waits) if i == 0 else []
            if scl is None:
                step("scalar", [(tp, tp.n)] + ew, [(sp, 1)],
                     lambda s, slot=slot, dst=dsts[i]: s.copy(dst, slot))
            else:
                step("scalar", [(tp, tp.n)] + ew, [(sp, 1)],
                     lambda s, slot=slot, dst=dsts[i], scl=scl:
                     s.activation(dst, slot, AF.Copy, scale=scl))
            slot_free[si] = sp.n

    step("gpsimd", [], [(gp, 1)], lambda g: g.memset(ident[:], 0.0))

    gwait = gp.n

    def g_pro2(g, gwait=gwait):
        g.wait_ge(gp.h, gwait)
        g.affine_select(out=ident[:], in_=ident[:],
                        compare_op=ALU.not_equal, fill=1.0, base=0,
                        pattern=[[-1, 128]], channel_multiplier=1)
        return g.memset(identf[:], 0.0)
    step("gpsimd", [], [(gp, 1)], g_pro2)

    gwait2 = gp.n

    def g_pro3(g, gwait2=gwait2):
        g.wait_ge(gp.h, gwait2)
        return g.affine_select(out=identf[:], in_=identf[:],
                               compare_op=ALU.not_equal, fill=1.0, base=0,
                               pattern=[[-1, 128]], channel_multiplier=1)
    step("gpsimd", [], [(gp, 1)], g_pro3)

    def sy_beta(sy):
        ins = None
        for h in range(NPAIR):
            if ins is not None:
                ins.then_inc(dmw.h, 16)
            ins = sy.dma_start(out=betac[0:64, h * M:(h + 1) * M],
                               in_=beta[2 * h:2 * h + 1, :].broadcast_to([64, M]))
            ins.then_inc(dmw.h, 16)
            ins = sy.dma_start(out=betac[64:128, h * M:(h + 1) * M],
                               in_=beta[2 * h + 1:2 * h + 2, :].broadcast_to([64, M]))
        return ins
    dmw.n += 15 * 16
    step("sync", [], [(dmw, 16)], sy_beta)

    def plan_quant(waits, incs, src_ap, ncols, fold_imm, fold_ap=None):
        sv0 = svv.n
        svv.n += 2

        def body(v, sv0=sv0):
            nblk = ncols // 16
            v.reduce_max(maxa[:, 0:nblk], r16(src_ap), axis=AXX,
                         apply_absolute_value=True).then_inc(svv.h, 1)
            v.wait_ge(svv.h, sv0 + 1)
            v.tensor_scalar(maxa[:, 0:nblk], maxa[:, 0:nblk], 6e-12, None, ALU.max)
            v.reciprocal(rm6[:, 0:nblk], maxa[:, 0:nblk]).then_inc(svv.h, 1)
            v.wait_ge(svv.h, sv0 + 2)
            v.tensor_scalar(rm6[:, 0:nblk], rm6[:, 0:nblk], 6.0, None, ALU.mult)
            if fold_ap is None:
                v.tensor_scalar(sca[:, 0:nblk], maxa[:, 0:nblk], fold_imm / 6.0,
                                None, ALU.mult)
            else:
                v.tensor_scalar(sca[:, 0:nblk], maxa[:, 0:nblk], fold_ap,
                                fold_imm / 6.0, ALU.mult, ALU.mult)
            v.tensor_tensor(r16(zb[:, 0:ncols]), r16(src_ap),
                            bc16(rm6[:, 0:nblk], nblk), ALU.mult)
            return v._custom_dve(FP4_QD, out=r16(qq[:, 0:ncols]),
                                 in0=r16(zb[:, 0:ncols]),
                                 in1=bc16(sca[:, 0:nblk], nblk),
                                 s0=FP4_C0, s1=FP4_C1)
        step("vector", waits, incs, body)

    def prep_weight(w_src, n_chunks, dst, quant, fold=1.0, wst=None):
        wst = _wst[0]
        ncols = w_src.shape[1]
        nkb = ncols // 128
        for r in range(n_chunks):
            step("sync", [(vp, vp.n)], [(dmw, 16)],
                 lambda sy, r=r, ncols=ncols: sy.dma_start(
                     out=zb[:, 0:ncols], in_=w_src[r * 128:(r + 1) * 128, :]))
            if quant:
                plan_quant([(dmw, dmw.n), (tp, tp.n)], [(vp, 1)],
                           zb[:, 0:ncols], ncols, fold)
            else:
                step("vector", [(dmw, dmw.n), (tp, tp.n)], [(vp, 1)],
                     lambda v, ncols=ncols: v.tensor_copy(qq[:, 0:ncols],
                                                          zb[:, 0:ncols]))
            step("tensor", [(vp, vp.n)], [], None)
            xpose_evac([qq[:, kb * 128:(kb + 1) * 128] for kb in range(nkb)],
                       [wst[:, kb * 128:(kb + 1) * 128] for kb in range(nkb)])
            step("gpsimd", [(sp, sp.n), (stq, stq.n)], [(stq, 16)],
                 lambda g, r=r, nkb=nkb: g.dma_start(
                     out=dst[:, r][0:nkb].rearrange("kb p t -> p kb t"),
                     in_=wst[:, 0:nkb * 128].rearrange("p (kb t) -> p kb t", kb=nkb)))
            step("scalar", [(stq, stq.n)], [], None)

    _wst = [None]
    with nc.sbuf_tensor("wst", [128, 22 * 128], BF16) as wst:
        _wst[0] = wst
        if "ffn1" in phases or "ffn2" in phases:
            prep_weight(w_gate, 22, wgq, True)
            prep_weight(w_up, 22, wuq, True)
            prep_weight(w_down, 8, wdq, True, fold=0.5)
        if "attn" in phases:
            prep_weight(w_in, 24, winT, False)
            prep_weight(w_out, 8, woutT, False)
        barrier()

    def ln_stats(x_hbm):
        for c in range(NCHUNK):
            step("sync", [(vp, vp.n - 1)] if c >= 2 else [], [(dmx, 16)],
                 lambda sy, c=c: sy.dma_start(out=xb[c % 2][:],
                                              in_=x_hbm[c * 128:(c + 1) * 128, :]))

            def stats_body(v, c=c):
                v.reduce_sum(msum[:, c:c + 1], xb[c % 2][:], axis=AXX)
                return v.scalar_tensor_tensor(scr[:], xb[c % 2][:], 1.0, xb[c % 2][:],
                                              ALU.mult, ALU.mult,
                                              accum_out=sqs[:, c:c + 1])
            step("vector", [(dmx, dmx.n)], [(vp, 1)], stats_body)

        sv0 = svv.n
        svv.n += 1

        def smalls(v, sv0=sv0):
            v.tensor_copy(stmp[:], msum[:]).then_inc(svv.h, 1)
            v.wait_ge(svv.h, sv0 + 1)
            v.tensor_scalar(mcol[:], msum[:], 1.0 / D, None, ALU.mult)
            v.tensor_scalar(wvar[:], sqs[:], 1.0 / D, None, ALU.mult)
            v.tensor_tensor(stmp[:], mcol[:], mcol[:], ALU.mult)
            v.tensor_tensor(wvar[:], wvar[:], stmp[:], ALU.subtract)
            return v.tensor_scalar(wvar[:], wvar[:], 1e-5, None, ALU.add)
        step("vector", [], [(vp, 1)], smalls)

        def sqrt_body(s):
            s.preload_act_table(AF.Sqrt)
            ins = s.activation(rstd[:], wvar[:], AF.Sqrt)
            s.preload_act_table(AF.Silu)
            return ins
        step("scalar", [(vp, vp.n)], [(sp, 1)], sqrt_body)

        sv0 = svv.n
        svv.n += 1

        def refine(v, sv0=sv0):
            v.reciprocal(rstd[:], rstd[:]).then_inc(svv.h, 1)
            v.wait_ge(svv.h, sv0 + 1)
            v.tensor_tensor(stmp[:], rstd[:], rstd[:], ALU.mult)
            v.tensor_tensor(stmp[:], stmp[:], wvar[:], ALU.mult)
            v.tensor_scalar(stmp[:], stmp[:], -0.5, 1.5, ALU.mult, ALU.add)
            v.tensor_tensor(rstd[:], rstd[:], stmp[:], ALU.mult)
            return v.tensor_scalar(rst6[:], rstd[:], 1.0 / 6.0, None, ALU.mult)
        step("vector", [(sp, sp.n)], [(vp, 1)], refine)

    _ffn_ctr = [0]

    def ffn_phase(x_src, x_dst):
        _ffn_ctr[0] += 1
        tag = f"_f{_ffn_ctr[0]}"
        ln_stats(x_src)
        with ExitStack() as fst:
            fsb = lambda name, shape, dt=F32: fst.enter_context(
                nc.sbuf_tensor(name + tag, shape, dt))
            wgh = fsb("wgh", [128, 8 * IH], BF16)
            wuh = fsb("wuh", [128, 8 * IH], BF16)
            wdn = fsb("wdn", [128, 22 * D], BF16)
            hqT = fsb("hqT", [128, 22 * GRP * 128], BF16)
            xqT = [fsb(f"xqT{i}", [128, D], BF16) for i in range(GRP)]
            sgb = fsb("sgb", [128, IH])
            sp_free_a = [0]
            vp_free_b = [0]
            sp_xq = [0]
            resid_done = [vp.n] * 4
            resid_done_ps = [0, 0]
            store_done = [0] * 4

            def load_wdn(sy):
                ins = None
                for ob in range(8):
                    if ins is not None:
                        ins.then_inc(dmw.h, 16)
                    ins = sy.dma_start(
                        out=wdn[:].rearrange("p (ib ob t) -> p ib ob t",
                                             ib=22, ob=8)[:, :, ob],
                        in_=wdq[:, ob].rearrange("ib p t -> p ib t"))
                return ins
            dmw.n += 7 * 16
            step("sync", [], [(dmw, 16)], load_wdn)

            tp_kb_prev = [tp.n] * 8
            for g in range(NGRP):
                for ci in range(GRP):
                    c = g * GRP + ci
                    step("sync", [(vp, resid_done[ci]), (stq, store_done[ci])],
                         [(dmx, 16)],
                         lambda sy, c=c, ci=ci: sy.dma_start(
                             out=xb[ci][:],
                             in_=x_src[c * 128:(c + 1) * 128, :]))
                    sv0 = svv.n
                    svv.n += 2

                    def xquant(v, c=c, ci=ci, sv0=sv0):
                        v.tensor_scalar(ab[:], xb[ci][:], mcol[:, c:c + 1],
                                        None, ALU.subtract)
                        nblk = D // 16
                        v.reduce_max(maxa[:, 0:nblk], r16(ab[:]), axis=AXX,
                                     apply_absolute_value=True).then_inc(svv.h, 1)
                        v.wait_ge(svv.h, sv0 + 1)
                        v.tensor_scalar(maxa[:, 0:nblk], maxa[:, 0:nblk],
                                        6e-12, None, ALU.max)
                        v.reciprocal(rm6[:, 0:nblk], maxa[:, 0:nblk]
                                     ).then_inc(svv.h, 1)
                        v.wait_ge(svv.h, sv0 + 2)
                        v.tensor_scalar(rm6[:, 0:nblk], rm6[:, 0:nblk], 6.0,
                                        None, ALU.mult)
                        v.tensor_scalar(sca[:, 0:nblk], maxa[:, 0:nblk],
                                        rst6[:, c:c + 1], None, ALU.mult)
                        v.tensor_tensor(r16(zb[:, 0:D]), r16(ab[:]),
                                        bc16(rm6[:, 0:nblk], nblk), ALU.mult)
                        return v._custom_dve(
                            FP4_QD, out=r16(qq[:, 0:D]), in0=r16(zb[:, 0:D]),
                            in1=bc16(sca[:, 0:nblk], nblk),
                            s0=FP4_C0, s1=FP4_C1)
                    step("vector", [(dmx, dmx.n), (tp, tp.n)], [(vp, 1)], xquant)
                    step("tensor", [(vp, vp.n)], [], None)
                    xpose_evac(
                        [qq[:, kb * 128:(kb + 1) * 128] for kb in range(8)],
                        [xqT[ci][:, kb * 128:(kb + 1) * 128] for kb in range(8)])
                sp_xq[0] = sp.n

                pending_hq = [None]

                def flush_hq(vp_tgt):
                    if pending_hq[0] is None:
                        return
                    ih0, ci0 = pending_hq[0]
                    pending_hq[0] = None
                    step("tensor", [(vp, vp_tgt)], [], None)
                    xpose_evac(
                        [qq[:, ib * 128:(ib + 1) * 128] for ib in range(11)],
                        [hqT[:, ((ih0 * 11 + ib) * GRP + ci0) * 128:
                             ((ih0 * 11 + ib) * GRP + ci0) * 128 + 128]
                         for ib in range(11)])

                for ih in range(2):
                    w_kb = [0] * 8
                    u_kb = [0] * 8
                    for kb in range(8):
                        step("sync", [(tp, tp_kb_prev[kb])], [(dmw, 16)],
                             lambda sy, kb=kb, ih=ih: sy.dma_start(
                                 out=wgh[:, kb * IH:(kb + 1) * IH].rearrange(
                                     "p (r t) -> p r t", r=11),
                                 in_=wgq[kb, ih * 11:(ih + 1) * 11].rearrange(
                                     "r p t -> p r t")))
                        w_kb[kb] = dmw.n
                        step("sync", [(tp, tp_kb_prev[kb])], [(dmu, 16)],
                             lambda sy, kb=kb, ih=ih: sy.dma_start(
                                 out=wuh[:, kb * IH:(kb + 1) * IH].rearrange(
                                     "p (r t) -> p r t", r=11),
                                 in_=wuq[kb, ih * 11:(ih + 1) * 11].rearrange(
                                     "r p t -> p r t")))
                        u_kb[kb] = dmu.n

                    for ci in range(GRP):
                        c = g * GRP + ci
                        tp_kb_cur = [0] * 8
                        for kb in range(8):
                            def gup_kb(t, ci=ci, kb=kb):
                                lhs = xqT[ci][:, kb * 128:(kb + 1) * 128]
                                first, last = kb == 0, kb == 7
                                for nw in range(3):
                                    n0, n1 = nw * 512, min((nw + 1) * 512, IH)
                                    t.matmul(ps_a[:, n0:n1], lhs,
                                             wgh[:, kb * IH + n0:kb * IH + n1],
                                             start=first, stop=last)
                                    ins = t.matmul(
                                        ps_b[:, n0:n1], lhs,
                                        wuh[:, kb * IH + n0:kb * IH + n1],
                                        start=first, stop=last)
                                return ins
                            w = [(dmw, w_kb[kb]), (dmu, u_kb[kb])]
                            if kb == 0:
                                w += [(sp, max(sp_free_a[0], sp_xq[0])),
                                      (vp, vp_free_b[0])]
                            step("tensor", w, [(tp, 1)], gup_kb)
                            tp_kb_cur[kb] = tp.n
                        if ci == GRP - 1:
                            tp_kb_prev = tp_kb_cur
                        tp_gup = tp.n

                        step("scalar", [(tp, tp_gup), (vp, vp_free_b[0])], [(sp, 1)],
                             lambda s: s.activation(sgb[:], ps_a[:, 0:IH], AF.Silu))
                        sp_free_a[0] = sp.n

                        hq_vp = vp.n
                        flush_hq(hq_vp)
                        tp_xp = tp.n

                        def hmult(v):
                            return v.tensor_tensor(hb[:], sgb[:], ps_b[:, 0:IH],
                                                   ALU.mult)
                        step("vector", [(sp, sp_free_a[0]), (tp, tp_gup)],
                             [(vp, 1)], hmult)
                        vp_free_b[0] = vp.n

                        sv0 = svv.n
                        svv.n += 2

                        def hquant(v, sv0=sv0):
                            nblk = IH // 16
                            v.reduce_max(maxa[:, 0:nblk], r16(hb[:]), axis=AXX,
                                         apply_absolute_value=True).then_inc(svv.h, 1)
                            v.wait_ge(svv.h, sv0 + 1)
                            v.tensor_scalar(maxa[:, 0:nblk], maxa[:, 0:nblk], 6e-12,
                                            None, ALU.max)
                            v.reciprocal(rm6[:, 0:nblk], maxa[:, 0:nblk]
                                         ).then_inc(svv.h, 1)
                            v.wait_ge(svv.h, sv0 + 2)
                            v.tensor_scalar(rm6[:, 0:nblk], rm6[:, 0:nblk], 6.0,
                                            None, ALU.mult)
                            v.tensor_scalar(sca[:, 0:nblk], maxa[:, 0:nblk],
                                            1.0 / 6.0, None, ALU.mult)
                            v.tensor_tensor(r16(zb[:, 0:IH]), r16(hb[:]),
                                            bc16(rm6[:, 0:nblk], nblk), ALU.mult)
                            return v._custom_dve(
                                FP4_QD, out=r16(qq[:, 0:IH]), in0=r16(zb[:, 0:IH]),
                                in1=bc16(sca[:, 0:nblk], nblk), s0=FP4_C0, s1=FP4_C1)
                        step("vector", [(tp, tp_xp)], [(vp, 1)], hquant)
                        pending_hq[0] = (ih, ci)

                flush_hq(vp.n)
                for ci in range(GRP):
                    c = g * GRP + ci
                    ps = ps_a if ci % 2 == 0 else ps_b

                    def down(t, ci=ci, ps=ps):
                        for ib in range(22):
                            lhs = hqT[:, (ib * GRP + ci) * 128:
                                      (ib * GRP + ci) * 128 + 128]
                            for nw in range(2):
                                ins = t.matmul(
                                    ps[:, nw * 512:(nw + 1) * 512], lhs,
                                    wdn[:, ib * D + nw * 512:ib * D + (nw + 1) * 512],
                                    start=(ib == 0), stop=(ib == 21))
                        return ins
                    w = [(sp, sp.n)]
                    if ci >= 2:
                        w.append((vp, resid_done_ps[ci % 2]))
                    elif ci == 1:
                        w.append((vp, vp_free_b[0]))
                    step("tensor", w, [(tp, 1)], down)
                    step("vector", [(tp, tp.n)], [(vp, 1)],
                         lambda v, ci=ci, ps=ps: v.tensor_tensor(
                             xb[ci][:], xb[ci][:], ps[:, 0:D], ALU.add))
                    resid_done[ci] = vp.n
                    resid_done_ps[ci % 2] = vp.n
                    step("gpsimd", [(vp, vp.n)], [(stq, 16)],
                         lambda g2, c=c, ci=ci: g2.dma_start(
                             out=x_dst[c * 128:(c + 1) * 128, :], in_=xb[ci][:]))
                    store_done[ci] = stq.n
                vp_free_b[0] = vp.n
                sp_free_a[0] = max(sp_free_a[0], sp.n)

            step("gpsimd", [(stq, stq.n)], [], None)
        barrier()

    rec_st = {"sV": 0, "v10": 0}

    def rec_split(k2ap, dst_fn):
        s_scr = sp.n
        sV = {}
        vV = {}
        for m in range(2, M):
            cur, p1, p2 = (m % 3) * D, ((m - 1) % 3) * D, ((m - 2) % 3) * D

            def vbody(v, m=m, cur=cur, p1=p1, p2=p2, k2ap=k2ap):
                if m == 2:
                    v.memset(zb[:, 0:D], 1.0)
                    v.tensor_scalar(zb[:, D:2 * D], k2ap, 0.5, None, ALU.mult)
                v.tensor_tensor(zb[:, cur:cur + D], k2ap,
                                zb[:, p1:p1 + D], ALU.mult)
                return v.tensor_tensor(zb[:, cur:cur + D], zb[:, cur:cur + D],
                                       zb[:, p2:p2 + D], ALU.subtract)
            if m == 2:
                w = [(sp, max(s_scr, rec_st["sV"]))]
            elif m >= 5:
                w = [(sp, sV[m - 3])]
            else:
                w = []
            step("vector", w, [(vp, 1)], vbody)
            vV[m] = vp.n

            step("scalar", [(vp, vV[m])], [(sp, 1)],
                 lambda s, m=m, cur=cur: s.copy(dst_fn(m, 0, D),
                                                zb[:, cur:cur + D]))
            sV[m] = sp.n
        rec_st["sV"] = sV[M - 1]
        rec_st["v10"] = vV[M - 1]

    def attn_phase(x_src, x_dst):
        ln_stats(x_src)
        with ExitStack() as kst:
            kvacc = kst.enter_context(
                nc.sbuf_tensor("kvacc", [128, NPAIR * M * 128], F32))
            kvar = kst.enter_context(
                nc.sbuf_tensor("kvar", [128, NPAIR * M * 128], BF16, side="right"))
            with ExitStack() as ast:
                asb = lambda name, shape, dt=F32: ast.enter_context(
                    nc.sbuf_tensor(name, shape, dt))
                wkv = asb("wkv", [128, 8 * 2048], BF16)
                tkall = asb("tkall", [128, SUB * M * D], BF16)
                vgrp = asb("vgrp", [128, SUB * D], BF16)
                scr2b = asb("scr2b", [128, D])
                xqTa = [asb(f"xqTa{i}", [128, D], BF16) for i in range(2)]

                def load_wkv(sy):
                    ins = None
                    for kb in range(8):
                        if ins is not None:
                            ins.then_inc(dmw.h, 16)
                        ins = sy.dma_start(
                            out=wkv[:, kb * 2048:(kb + 1) * 2048].rearrange(
                                "p (r t) -> p r t", r=16),
                            in_=winT[kb, 8:24].rearrange("r p t -> p r t"))
                    return ins
                dmw.n += 7 * 16
                step("sync", [], [(dmw, 16)], load_wkv)

                def tk0(g2):
                    g2.memset(tkall[:, 0:D], 1.0)
                    return g2.memset(tkall[:, M * D:M * D + D], 1.0)
                step("gpsimd", [], [(gp, 1)], tk0)

                ab_free = [0]
                for sg in range(NCHUNK // SUB):
                    kevac_done = [0, 0]
                    vevac_done = [0, 0]
                    s_scr_c = [0, 0]
                    for ci in range(SUB):
                        c = sg * SUB + ci
                        step("sync", [(vp, vp.n)], [(dmx, 16)],
                             lambda sy, c=c, ci=ci: sy.dma_start(
                                 out=xb[ci][:], in_=x_src[c * 128:(c + 1) * 128, :]))
                        step("vector", [(dmx, dmx.n), (tp, tp.n)], [(vp, 1)],
                             lambda v, c=c, ci=ci: v.tensor_scalar(
                                 qq[:, 0:D], xb[ci][:], mcol[:, c:c + 1],
                                 rstd[:, c:c + 1], ALU.subtract, ALU.mult))
                        step("tensor", [(vp, vp.n)], [], None)
                        xpose_evac(
                            [qq[:, kb * 128:(kb + 1) * 128] for kb in range(8)],
                            [xqTa[ci][:, kb * 128:(kb + 1) * 128] for kb in range(8)],
                            evac_waits=[(stq, stq.n)])
                        step("gpsimd", [(sp, sp.n)], [(stq, 16)],
                             lambda g2, c=c, ci=ci: g2.dma_start(
                                 out=xnT_h[c].rearrange("(kb kp) t -> kp kb t",
                                                        kp=128),
                                 in_=xqTa[ci][:].rearrange("p (kb t) -> p kb t",
                                                           kb=8)))

                        def kvproj(t, ci=ci):
                            for kb in range(8):
                                lhs = xqTa[ci][:, kb * 128:(kb + 1) * 128]
                                for nw in range(4):
                                    ps = ps_a if nw < 2 else ps_b
                                    ins = t.matmul(
                                        ps[:, (nw % 2) * 512:(nw % 2 + 1) * 512], lhs,
                                        wkv[:, kb * 2048 + nw * 512:
                                            kb * 2048 + (nw + 1) * 512],
                                        start=(kb == 0), stop=(kb == 7))
                            return ins
                        step("tensor",
                             [(dmw, dmw.n), (vp, kevac_done[0]),
                              (sp, vevac_done[0])],
                             [(tp, 1)], kvproj)
                        tp_proj = tp.n

                        def kevac(v):
                            v.tensor_scalar(ab[:], ps_a[:, 0:D], 0.125, 1.0,
                                            ALU.mult, ALU.min)
                            return v.tensor_scalar(ab[:], ab[:], -1.0, None, ALU.max)
                        step("vector", [(tp, tp_proj), (sp, ab_free[0])],
                             [(vp, 1)], kevac)
                        kevac_done[0] = vp.n
                        step("scalar", [(tp, tp_proj)], [(sp, 1)],
                             lambda s, ci=ci: s.copy(vgrp[:, ci * D:(ci + 1) * D],
                                                     ps_b[:, 0:D]))
                        vevac_done[0] = sp.n

                        k2dst = scr[:] if ci == 0 else scr2b[:]

                        def sinit(s, ci=ci, k2dst=k2dst):
                            s.copy(tkall[:, ci * M * D + D:ci * M * D + 2 * D],
                                   ab[:])
                            return s.activation(k2dst, ab[:], AF.Copy, scale=2.0)
                        step("scalar",
                             [(vp, max(kevac_done[0], rec_st["v10"]))],
                             [(sp, 1)], sinit)
                        ab_free[0] = sp.n
                        s_scr_c[ci] = sp.n

                    for ci in range(SUB):
                        rec_split(scr[:] if ci == 0 else scr2b[:],
                                  lambda m, lo, hi, ci=ci:
                                  tkall[:, ci * M * D + m * D + lo:
                                        ci * M * D + m * D + hi])

                    for pp in range(NPAIR // 2):
                        def kvmm(t, sg=sg, pp=pp):
                            for half in range(2):
                                pair = pp * 2 + half
                                ps = ps_a if half == 0 else ps_b
                                for ci in range(SUB):
                                    lhs = vgrp[:, ci * D + pair * 128:
                                               ci * D + (pair + 1) * 128]
                                    for mg in range(3):
                                        m0 = mg * 4
                                        mN = min(4, M - m0)
                                        rhs = tkall[:].rearrange(
                                            "p (c m t) -> p c m t", c=SUB, m=M)[
                                            :, ci, m0:m0 + mN,
                                            pair * 128:(pair + 1) * 128]
                                        ins = t.matmul(
                                            ps[:, m0 * 128:(m0 + mN) * 128], lhs, rhs,
                                            start=(ci == 0), stop=(ci == SUB - 1))
                            return ins
                        step("tensor", [(sp, sp.n), (vp, kevac_done[0])],
                             [(tp, 1)], kvmm)

                        def kvadd(v, sg=sg, pp=pp):
                            for half in range(2):
                                pair = pp * 2 + half
                                ps = ps_a if half == 0 else ps_b
                                dst = kvacc[:, pair * M * 128:(pair + 1) * M * 128]
                                if sg == 0:
                                    ins = v.tensor_copy(dst, ps[:, 0:M * 128])
                                else:
                                    ins = v.tensor_tensor(dst, dst, ps[:, 0:M * 128],
                                                          ALU.add)
                            return ins
                        step("vector", [(tp, tp.n)], [(vp, 1)], kvadd)
                        step("tensor", [(vp, vp.n)], [], None)

                step("gpsimd", [(stq, stq.n)], [], None)
                barrier()

                step("gpsimd", [], [(gp, 1)], lambda g2: g2.memset(kvar[:], 0.0))
                step("tensor", [(gp, gp.n)], [], None)
                arr_base = sp.n
                for pair in range(NPAIR):
                    for m in range(M):
                        j = pair * M + m
                        slot = ps_a[:, 0:128] if j % 2 == 0 else ps_b[:, 0:128]
                        step("tensor",
                             [(sp, arr_base + 2 * (j - 1))] if j >= 2 else [],
                             [(tp, 1)],
                             lambda t, pair=pair, m=m, slot=slot: t.transpose(
                                 slot,
                                 kvacc[:, (pair * M + m) * 128:
                                       (pair * M + m + 1) * 128],
                                 identf[:]))

                        def kvevac(s, pair=pair, m=m, slot=slot):
                            base = (pair * M + m) * 128
                            s.activation(kvar[0:64, base:base + 64], slot[0:64, 0:64],
                                         AF.Copy,
                                         scale=betac[0:64, pair * M + m:
                                                     pair * M + m + 1])
                            return s.activation(
                                kvar[64:128, base + 64:base + 128],
                                slot[64:128, 64:128], AF.Copy,
                                scale=betac[64:128, pair * M + m:pair * M + m + 1])
                        step("scalar", [(tp, tp.n)], [(sp, 2)], kvevac)
                step("tensor", [(sp, sp.n)], [], None)
                barrier()

            with ExitStack() as bst:
                bsb2 = lambda name, shape, dt=F32: bst.enter_context(
                    nc.sbuf_tensor(name, shape, dt))
                wq8 = bsb2("wq8", [128, 8 * D], BF16)
                woT = bsb2("woT", [128, 8 * D], BF16)
                xbT = bsb2("xbT", [128, 8 * D], BF16)
                tqa = bsb2("tqa", [128, M * D], BF16)
                aoT = bsb2("aoT", [128, 8 * D], BF16)
                qf = bsb2("qf", [128, D])

                def load_wb(sy):
                    ins = None
                    for kb in range(8):
                        if ins is not None:
                            ins.then_inc(dmw.h, 16)
                        ins = sy.dma_start(
                            out=wq8[:, kb * D:(kb + 1) * D].rearrange(
                                "p (r t) -> p r t", r=8),
                            in_=winT[kb, 0:8].rearrange("r p t -> p r t"))
                        ins.then_inc(dmw.h, 16)
                        ins = sy.dma_start(
                            out=woT[:, kb * D:(kb + 1) * D].rearrange(
                                "p (r t) -> p r t", r=8),
                            in_=woutT[kb, :].rearrange("r p t -> p r t"))
                    return ins
                dmw.n += 15 * 16
                step("sync", [], [(dmw, 16)], load_wb)

                step("gpsimd", [], [(gp, 1)],
                     lambda g2: g2.memset(tqa[:, 0:D], 1.0))

                bstore = [0] * 4
                for stile in range(4):
                    c0 = stile * 8

                    def load_x(sy, c0=c0):
                        ins = None
                        for kb in range(8):
                            if ins is not None:
                                ins.then_inc(dmx.h, 16)
                            ins = sy.dma_start(
                                out=xbT[:, kb * D:(kb + 1) * D].rearrange(
                                    "p (c t) -> p c t", c=8),
                                in_=xnT_h[c0:c0 + 8, kb * 128:(kb + 1) * 128]
                                .rearrange("c kp t -> kp c t"))
                        return ins
                    dmx.n += 7 * 16
                    step("sync", [(vp, vp.n)], [(dmx, 16)], load_x)

                    qf_done = [0]
                    qf_free = [0]
                    ao_done = [0]
                    tq_done = {}
                    pending_out = [None]

                    def plan_outmm(pair):
                        def outmm(t, pair=pair):
                            for m in range(M):
                                lhs = kvar[:, (pair * M + m) * 128:
                                           (pair * M + m + 1) * 128]
                                for nw in range(2):
                                    ins = t.matmul(
                                        ps_b[:, nw * 512:(nw + 1) * 512], lhs,
                                        tqa[:, m * D + nw * 512:
                                            m * D + (nw + 1) * 512],
                                        start=(m == 0), stop=(m == M - 1))
                            return ins
                        step("tensor",
                             [(sp, max(tq_done[pair], ao_done[0]))],
                             [(tp, 1)], outmm)
                        step("scalar", [(tp, tp.n)], [(sp, 1)],
                             lambda s, pair=pair: s.copy(
                                 aoT[:, pair * D:(pair + 1) * D], ps_b[:, 0:D]))
                        ao_done[0] = sp.n

                    for pair in range(NPAIR):
                        def qproj(t, pair=pair):
                            for kb in range(8):
                                lhs = wq8[:, kb * D + pair * 128:
                                          kb * D + (pair + 1) * 128]
                                for nw in range(2):
                                    ins = t.matmul(
                                        ps_a[:, nw * 512:(nw + 1) * 512], lhs,
                                        xbT[:, kb * D + nw * 512:
                                            kb * D + (nw + 1) * 512],
                                        start=(kb == 0), stop=(kb == 7))
                            return ins
                        step("tensor",
                             [(dmx, dmx.n), (dmw, dmw.n), (vp, qf_done[0])],
                             [(tp, 1)], qproj)
                        tp_q = tp.n

                        def qfevac(v):
                            v.tensor_scalar(qf[:], ps_a[:, 0:1024], 0.125, 1.0,
                                            ALU.mult, ALU.min)
                            return v.tensor_scalar(qf[:], qf[:], -1.0, None, ALU.max)
                        step("vector", [(tp, tp_q), (sp, qf_free[0])],
                             [(vp, 1)], qfevac)
                        qf_done[0] = vp.n

                        if pending_out[0] is not None:
                            plan_outmm(pending_out[0])
                        tp_out = tp.n

                        def sinit(s):
                            s.copy(tqa[:, D:2 * D], qf[:])
                            return s.activation(scr[:], qf[:], AF.Copy, scale=2.0)
                        step("scalar",
                             [(vp, max(qf_done[0], rec_st["v10"])),
                              (tp, tp_out)], [(sp, 1)], sinit)
                        qf_free[0] = sp.n

                        rec_split(scr[:],
                                  lambda m, lo, hi: tqa[:, m * D + lo:m * D + hi])
                        tq_done[pair] = sp.n
                        pending_out[0] = pair
                    plan_outmm(pending_out[0])

                    wout_resid = [0, 0]
                    for ci in range(8):
                        c = c0 + ci
                        ps = ps_a if ci % 2 == 0 else ps_b
                        step("sync", [(vp, vp.n), (stq, bstore[ci % 4])],
                             [(dmx, 16)],
                             lambda sy, c=c, ci=ci: sy.dma_start(
                                 out=xb[ci % 4][:],
                                 in_=x_src[c * 128:(c + 1) * 128, :]))

                        def woutmm(t, ci=ci, ps=ps):
                            for kb in range(8):
                                lhs = aoT[:, kb * D + ci * 128:kb * D + ci * 128 + 128]
                                for nw in range(2):
                                    ins = t.matmul(
                                        ps[:, nw * 512:(nw + 1) * 512], lhs,
                                        woT[:, kb * D + nw * 512:
                                            kb * D + (nw + 1) * 512],
                                        start=(kb == 0), stop=(kb == 7))
                            return ins
                        w = [(sp, sp.n)]
                        if ci >= 2:
                            w.append((vp, wout_resid[ci % 2]))
                        step("tensor", w, [(tp, 1)], woutmm)
                        step("vector", [(tp, tp.n), (dmx, dmx.n)], [(vp, 1)],
                             lambda v, ci=ci, ps=ps: v.tensor_tensor(
                                 xb[ci % 4][:], xb[ci % 4][:], ps[:, 0:D], ALU.add))
                        wout_resid[ci % 2] = vp.n
                        step("gpsimd", [(vp, vp.n)], [(stq, 16)],
                             lambda g2, c=c, ci=ci: g2.dma_start(
                                 out=x_dst[c * 128:(c + 1) * 128, :],
                                 in_=xb[ci % 4][:]))
                        bstore[ci % 4] = stq.n

                step("gpsimd", [(stq, stq.n)], [], None)
        barrier()

    nph = [p for p in ("ffn1", "attn", "ffn2") if p in phases]
    cur = x_in
    for idx, p in enumerate(nph):
        dst = y_out if idx == len(nph) - 1 else xs[idx % 2]
        if p in ("ffn1", "ffn2"):
            ffn_phase(cur, dst)
        else:
            attn_phase(cur, dst)
        cur = dst
    if cur is not y_out:
        for c in range(NCHUNK):
            dmx_tgt = dmx.n + 16

            def cl(sy, c=c, cur=cur, dmx_tgt=dmx_tgt):
                ins = sy.dma_start(out=xb[0][:], in_=cur[c * 128:(c + 1) * 128, :])
                ins.then_inc(dmx.h, 16)
                sy.wait_ge(dmx.h, dmx_tgt)
                return sy.dma_start(out=y_out[c * 128:(c + 1) * 128, :], in_=xb[0][:])
            step("sync", [(stq, stq.n)], [(stq, 16)], cl)
            dmx.n += 16

    step("sync", [(stq, stq.n)], [], None)

    with nc.Block() as block:
        @block.sync
        def _(e):
            for f in prog["sync"]:
                f(e)

        @block.vector
        def _(e):
            for f in prog["vector"]:
                f(e)

        @block.scalar
        def _(e):
            for f in prog["scalar"]:
                f(e)

        @block.tensor
        def _(e):
            for f in prog["tensor"]:
                f(e)

        @block.gpsimd
        def _(e):
            for f in prog["gpsimd"]:
                f(e)

    st.close()
    mybir.codegen_inst_isa_subclasses(nc)
    return nc


_NC_CACHE = {}


def kernel(L0_attn_x, L0_attn_w_in, L0_attn_w_out, L0_attn_beta,
           L0_ffn_w_gate, L0_ffn_w_up, L0_ffn_w_down):
    x = np.ascontiguousarray(np.asarray(L0_attn_x, dtype=np.float32))
    if "nc" not in _NC_CACHE:
        _NC_CACHE["nc"] = build_program()
    nc = _NC_CACHE["nc"]
    w = {
        "w_in": np.ascontiguousarray(np.asarray(L0_attn_w_in, np.float32)),
        "w_out": np.ascontiguousarray(np.asarray(L0_attn_w_out, np.float32)),
        "beta": np.ascontiguousarray(np.asarray(L0_attn_beta, np.float32)),
        "w_gate": np.ascontiguousarray(np.asarray(L0_ffn_w_gate, np.float32)),
        "w_up": np.ascontiguousarray(np.asarray(L0_ffn_w_up, np.float32)),
        "w_down": np.ascontiguousarray(np.asarray(L0_ffn_w_down, np.float32)),
    }
    in_maps = [dict(w, x=np.ascontiguousarray(x[b])) for b in range(B)]
    res = run_bass_kernel_spmd(nc, in_maps, core_ids=list(range(B)))
    out = np.stack([np.asarray(res.results[b]["y"]) for b in range(B)], axis=0)
    return out.astype(np.float32)


# version 10
# speedup vs baseline: 1.3199x; 1.0457x over previous
import numpy as np

import concourse.bass as bass
import concourse.mybir as mybir
from concourse.bass_utils import run_bass_kernel_spmd
from concourse import dve_ops
from concourse.dve_ops import DveOp, OPS
from concourse.dve_spec import Spec, Src0, Src1, C0, C1, Bin, lower as dve_lower, maxx
from concourse.dve_uop import AluOp as DAlu
from concourse.dve_table_gen import dve_ver_for

F32 = mybir.dt.float32
BF16 = mybir.dt.bfloat16
AF = mybir.ActivationFunctionType
ALU = mybir.AluOpType
AXX = mybir.AxisListType.X

B, S, D, I = 8, 4096, 1024, 2816
NCHUNK = S // 128
GRP = 4
NGRP = NCHUNK // GRP
IH = I // 2
NH, DH, M = 16, 64, 11
NPAIR = NH // 2
SUB = 2


def _fp4_ref(in0, in1, s0, s1, imm2):
    z = np.asarray(in0)
    az = np.abs(z)
    s_ = np.maximum(np.minimum(az, 4.0), 1.0)
    s_ = np.exp2(np.floor(np.log2(s_)))
    h = s_ / 2
    return ((np.round(z / h) * h) * in1).astype(z.dtype)


def _register_fp4():
    for op in OPS:
        if op.name == "FP4_QD":
            return op
    c_ = maxx(Bin(DAlu.BITWISE_AND, Bin(DAlu.MULTIPLY, Src0, C1), C0), C1)
    X_ = Bin(DAlu.ADD, Bin(DAlu.ADD, c_, c_), c_)
    rq_ = Bin(DAlu.SUBTRACT, Bin(DAlu.ADD, Src0, X_), X_)
    spec = Spec(body=Bin(DAlu.MULTIPLY, rq_, Src1))
    ver = dve_ver_for("TRN2")
    sha = dve_lower(spec, ver=ver, name="FP4_QD").sha(ver)
    op = DveOp("FP4_QD", spec, subdim=False, uops_sha={ver: sha}, reference=_fp4_ref)
    OPS.append(op)
    dve_ops._SUB_OPCODE_FOR_NAME[op.name] = dve_ops._CUSTOM_DVE_ROW_BASE + len(OPS) - 1
    dve_ops.CUSTOM_DVE_SPECS[op.name] = spec
    dve_ops.CUSTOM_DVE_REFS[op.name] = _fp4_ref
    return op


FP4_QD = _register_fp4()
FP4_C0 = float(2 ** 24)
FP4_C1 = float(2 ** 21)


class Sem:
    def __init__(self, h):
        self.h = h
        self.n = 0


def bc16(t, nblk):
    return t[:, :, None].broadcast_to([128, nblk, 16])


def r16(ap):
    return ap.rearrange("p (b k) -> p b k", k=16)


def build_program(phases=("ffn1", "attn", "ffn2")):
    from contextlib import ExitStack

    nc = bass.Bass("TRN2", target_bir_lowering=False,
                   detect_race_conditions=False)

    x_in = nc.dram_tensor("x", [S, D], F32, kind="ExternalInput").ap()
    w_in = nc.dram_tensor("w_in", [3 * D, D], F32, kind="ExternalInput").ap()
    w_out = nc.dram_tensor("w_out", [D, D], F32, kind="ExternalInput").ap()
    beta = nc.dram_tensor("beta", [NH, M], F32, kind="ExternalInput").ap()
    w_gate = nc.dram_tensor("w_gate", [I, D], F32, kind="ExternalInput").ap()
    w_up = nc.dram_tensor("w_up", [I, D], F32, kind="ExternalInput").ap()
    w_down = nc.dram_tensor("w_down", [D, I], F32, kind="ExternalInput").ap()
    y_out = nc.dram_tensor("y", [S, D], F32, kind="ExternalOutput").ap()

    xs = [nc.dram_tensor(f"xs{i}", [S, D], F32).ap() for i in range(2)]
    wgq = nc.dram_tensor("wgq", [8, 22, 128, 128], BF16).ap()
    wuq = nc.dram_tensor("wuq", [8, 22, 128, 128], BF16).ap()
    wdq = nc.dram_tensor("wdq", [22, 8, 128, 128], BF16).ap()
    winT = nc.dram_tensor("winT", [8, 24, 128, 128], BF16).ap()
    woutT = nc.dram_tensor("woutT", [8, 8, 128, 128], BF16).ap()
    xnT_h = nc.dram_tensor("xnT", [NCHUNK, D, 128], BF16).ap()

    st = ExitStack()
    sb = lambda name, shape, dt=F32, side="left": st.enter_context(
        nc.sbuf_tensor(name, shape, dt, side=side))

    xb = [sb(f"xb{i}", [128, D]) for i in range(4)]
    ab = sb("ab", [128, D])
    zb = sb("zb", [128, max(I, 3 * 1024)])
    qq = sb("qq", [128, I], BF16)
    scr = sb("scr", [128, D])
    hb = sb("hb", [128, IH])
    msum = sb("msum", [128, NCHUNK])
    sqs = sb("sqs", [128, NCHUNK])
    mcol = sb("mcol", [128, NCHUNK])
    rstd = sb("rstd", [128, NCHUNK])
    rst6 = sb("rst6", [128, NCHUNK])
    wvar = sb("wvar", [128, NCHUNK])
    stmp = sb("stmp", [128, NCHUNK])
    maxa = sb("maxa", [128, I // 16])
    rm6 = sb("rm6", [128, I // 16])
    sca = sb("sca", [128, I // 16])
    ident = sb("ident", [128, 128], BF16)
    identf = sb("identf", [128, 128])
    betac = sb("betac", [128, NPAIR * M])

    ps_a = st.enter_context(nc.psum_tensor([128, 1536], F32))
    ps_b = st.enter_context(nc.psum_tensor([128, 1536], F32))
    psA_t = st.enter_context(nc.psum_tensor([128, 1024], BF16))
    psB_t = st.enter_context(nc.psum_tensor([128, 1024], BF16))
    slotA = psA_t[:, 0:128]
    slotB = psB_t[:, 0:128]

    names = ["dmx", "dmw", "dmu", "vp", "sp", "tp", "gp", "stq", "svv", "ph"]
    _sems = {n: Sem(st.enter_context(nc.semaphore(n))) for n in names}
    dmx, dmw, dmu, vp, sp, tp, gp, stq, svv, ph = (_sems[n] for n in names)

    prog = {k: [] for k in ("sync", "vector", "scalar", "tensor", "gpsimd")}
    ENGS = list(prog.keys())

    def step(eng, waits, incs, body):
        w = [(s.h, v) for (s, v) in waits]
        ii = [(s.h, k) for (s, k) in incs]
        for s, k in incs:
            s.n += k

        def f(e, w=w, ii=ii, body=body):
            for h, v in w:
                if v > 0:
                    e.wait_ge(h, v)
            if body is None:
                assert not ii
                return
            ins = body(e)
            for h, k in ii:
                ins.then_inc(h, k)
        prog[eng].append(f)

    def barrier():
        tgt = ph.n + 5
        for eng in ENGS:
            def f(e, tgt=tgt):
                e.sem_inc(ph.h, 1)
                e.wait_ge(ph.h, tgt)
            prog[eng].append(f)
        ph.n = tgt

    slot_free = [0, 0]

    def xpose_evac(srcs, dsts, scales=None, evac_waits=()):
        n = len(srcs)
        for i in range(n):
            si = i % 2
            slot = slotA if si == 0 else slotB
            step("tensor",
                 [(sp, slot_free[si])],
                 [(tp, 1)],
                 lambda t, slot=slot, src=srcs[i]: t.transpose(slot, src, ident[:]))
            scl = scales[i] if scales else None
            ew = list(evac_waits) if i == 0 else []
            if scl is None:
                step("scalar", [(tp, tp.n)] + ew, [(sp, 1)],
                     lambda s, slot=slot, dst=dsts[i]: s.copy(dst, slot))
            else:
                step("scalar", [(tp, tp.n)] + ew, [(sp, 1)],
                     lambda s, slot=slot, dst=dsts[i], scl=scl:
                     s.activation(dst, slot, AF.Copy, scale=scl))
            slot_free[si] = sp.n

    step("gpsimd", [], [(gp, 1)], lambda g: g.memset(ident[:], 0.0))

    gwait = gp.n

    def g_pro2(g, gwait=gwait):
        g.wait_ge(gp.h, gwait)
        g.affine_select(out=ident[:], in_=ident[:],
                        compare_op=ALU.not_equal, fill=1.0, base=0,
                        pattern=[[-1, 128]], channel_multiplier=1)
        return g.memset(identf[:], 0.0)
    step("gpsimd", [], [(gp, 1)], g_pro2)

    gwait2 = gp.n

    def g_pro3(g, gwait2=gwait2):
        g.wait_ge(gp.h, gwait2)
        return g.affine_select(out=identf[:], in_=identf[:],
                               compare_op=ALU.not_equal, fill=1.0, base=0,
                               pattern=[[-1, 128]], channel_multiplier=1)
    step("gpsimd", [], [(gp, 1)], g_pro3)

    def sy_beta(sy):
        ins = None
        for h in range(NPAIR):
            if ins is not None:
                ins.then_inc(dmw.h, 16)
            ins = sy.dma_start(out=betac[0:64, h * M:(h + 1) * M],
                               in_=beta[2 * h:2 * h + 1, :].broadcast_to([64, M]))
            ins.then_inc(dmw.h, 16)
            ins = sy.dma_start(out=betac[64:128, h * M:(h + 1) * M],
                               in_=beta[2 * h + 1:2 * h + 2, :].broadcast_to([64, M]))
        return ins
    dmw.n += 15 * 16
    step("sync", [], [(dmw, 16)], sy_beta)

    def plan_quant(waits, incs, src_ap, ncols, fold_imm, fold_ap=None):
        sv0 = svv.n
        svv.n += 2

        def body(v, sv0=sv0):
            nblk = ncols // 16
            v.reduce_max(maxa[:, 0:nblk], r16(src_ap), axis=AXX,
                         apply_absolute_value=True).then_inc(svv.h, 1)
            v.wait_ge(svv.h, sv0 + 1)
            v.tensor_scalar(maxa[:, 0:nblk], maxa[:, 0:nblk], 6e-12, None, ALU.max)
            v.reciprocal(rm6[:, 0:nblk], maxa[:, 0:nblk]).then_inc(svv.h, 1)
            v.wait_ge(svv.h, sv0 + 2)
            v.tensor_scalar(rm6[:, 0:nblk], rm6[:, 0:nblk], 6.0, None, ALU.mult)
            if fold_ap is None:
                v.tensor_scalar(sca[:, 0:nblk], maxa[:, 0:nblk], fold_imm / 6.0,
                                None, ALU.mult)
            else:
                v.tensor_scalar(sca[:, 0:nblk], maxa[:, 0:nblk], fold_ap,
                                fold_imm / 6.0, ALU.mult, ALU.mult)
            v.tensor_tensor(r16(zb[:, 0:ncols]), r16(src_ap),
                            bc16(rm6[:, 0:nblk], nblk), ALU.mult)
            return v._custom_dve(FP4_QD, out=r16(qq[:, 0:ncols]),
                                 in0=r16(zb[:, 0:ncols]),
                                 in1=bc16(sca[:, 0:nblk], nblk),
                                 s0=FP4_C0, s1=FP4_C1)
        step("vector", waits, incs, body)

    def prep_weight(w_src, n_chunks, dst, quant, fold=1.0, wst=None):
        wst = _wst[0]
        ncols = w_src.shape[1]
        nkb = ncols // 128
        for r in range(n_chunks):
            step("sync", [(vp, vp.n)], [(dmw, 16)],
                 lambda sy, r=r, ncols=ncols: sy.dma_start(
                     out=zb[:, 0:ncols], in_=w_src[r * 128:(r + 1) * 128, :]))
            if quant:
                plan_quant([(dmw, dmw.n), (tp, tp.n)], [(vp, 1)],
                           zb[:, 0:ncols], ncols, fold)
            else:
                step("vector", [(dmw, dmw.n), (tp, tp.n)], [(vp, 1)],
                     lambda v, ncols=ncols: v.tensor_copy(qq[:, 0:ncols],
                                                          zb[:, 0:ncols]))
            step("tensor", [(vp, vp.n)], [], None)
            xpose_evac([qq[:, kb * 128:(kb + 1) * 128] for kb in range(nkb)],
                       [wst[:, kb * 128:(kb + 1) * 128] for kb in range(nkb)])
            step("gpsimd", [(sp, sp.n), (stq, stq.n)], [(stq, 16)],
                 lambda g, r=r, nkb=nkb: g.dma_start(
                     out=dst[:, r][0:nkb].rearrange("kb p t -> p kb t"),
                     in_=wst[:, 0:nkb * 128].rearrange("p (kb t) -> p kb t", kb=nkb)))
            step("scalar", [(stq, stq.n)], [], None)

    _wst = [None]
    with nc.sbuf_tensor("wst", [128, 22 * 128], BF16) as wst:
        _wst[0] = wst
        if "ffn1" in phases or "ffn2" in phases:
            prep_weight(w_gate, 22, wgq, True)
            prep_weight(w_up, 22, wuq, True)
            prep_weight(w_down, 8, wdq, True, fold=0.5)
        if "attn" in phases:
            prep_weight(w_in, 24, winT, False)
            prep_weight(w_out, 8, woutT, False)
        barrier()

    def ln_stats(x_hbm):
        for c in range(NCHUNK):
            step("sync", [(vp, vp.n - 1)] if c >= 2 else [], [(dmx, 16)],
                 lambda sy, c=c: sy.dma_start(out=xb[c % 2][:],
                                              in_=x_hbm[c * 128:(c + 1) * 128, :]))

            def stats_body(v, c=c):
                v.reduce_sum(msum[:, c:c + 1], xb[c % 2][:], axis=AXX)
                return v.scalar_tensor_tensor(scr[:], xb[c % 2][:], 1.0, xb[c % 2][:],
                                              ALU.mult, ALU.mult,
                                              accum_out=sqs[:, c:c + 1])
            step("vector", [(dmx, dmx.n)], [(vp, 1)], stats_body)

        sv0 = svv.n
        svv.n += 1

        def smalls(v, sv0=sv0):
            v.tensor_copy(stmp[:], msum[:]).then_inc(svv.h, 1)
            v.wait_ge(svv.h, sv0 + 1)
            v.tensor_scalar(mcol[:], msum[:], 1.0 / D, None, ALU.mult)
            v.tensor_scalar(wvar[:], sqs[:], 1.0 / D, None, ALU.mult)
            v.tensor_tensor(stmp[:], mcol[:], mcol[:], ALU.mult)
            v.tensor_tensor(wvar[:], wvar[:], stmp[:], ALU.subtract)
            return v.tensor_scalar(wvar[:], wvar[:], 1e-5, None, ALU.add)
        step("vector", [], [(vp, 1)], smalls)

        def sqrt_body(s):
            s.preload_act_table(AF.Sqrt)
            ins = s.activation(rstd[:], wvar[:], AF.Sqrt)
            s.preload_act_table(AF.Silu)
            return ins
        step("scalar", [(vp, vp.n)], [(sp, 1)], sqrt_body)

        sv0 = svv.n
        svv.n += 1

        def refine(v, sv0=sv0):
            v.reciprocal(rstd[:], rstd[:]).then_inc(svv.h, 1)
            v.wait_ge(svv.h, sv0 + 1)
            v.tensor_tensor(stmp[:], rstd[:], rstd[:], ALU.mult)
            v.tensor_tensor(stmp[:], stmp[:], wvar[:], ALU.mult)
            v.tensor_scalar(stmp[:], stmp[:], -0.5, 1.5, ALU.mult, ALU.add)
            v.tensor_tensor(rstd[:], rstd[:], stmp[:], ALU.mult)
            return v.tensor_scalar(rst6[:], rstd[:], 1.0 / 6.0, None, ALU.mult)
        step("vector", [(sp, sp.n)], [(vp, 1)], refine)

    _ffn_ctr = [0]

    def ffn_phase(x_src, x_dst):
        _ffn_ctr[0] += 1
        tag = f"_f{_ffn_ctr[0]}"
        ln_stats(x_src)
        with ExitStack() as fst:
            fsb = lambda name, shape, dt=F32: fst.enter_context(
                nc.sbuf_tensor(name + tag, shape, dt))
            wgh = fsb("wgh", [128, 8 * IH], BF16)
            wuh = fsb("wuh", [128, 8 * IH], BF16)
            wdn = fsb("wdn", [128, 22 * D], BF16)
            hqT = fsb("hqT", [128, 22 * GRP * 128], BF16)
            xqT = [fsb(f"xqT{i}", [128, D], BF16) for i in range(8)]
            xb2 = [fsb(f"xb2{i}", [128, D]) for i in range(4)]
            xbr = xb + xb2
            qx = fsb("qx", [128, 4 * D], BF16)
            sgb = fsb("sgb", [128, IH])
            sp_free_a = [0]
            vp_free_b = [0]
            sp_xq = [0]
            qx_free = [0]
            resid_done = [vp.n] * 8
            resid_done_ps = [0, 0]
            store_done = [0] * 8

            def load_wdn(sy):
                ins = None
                for ob in range(8):
                    if ins is not None:
                        ins.then_inc(dmw.h, 16)
                    ins = sy.dma_start(
                        out=wdn[:].rearrange("p (ib ob t) -> p ib ob t",
                                             ib=22, ob=8)[:, :, ob],
                        in_=wdq[:, ob].rearrange("ib p t -> p ib t"))
                return ins
            dmw.n += 7 * 16
            step("sync", [], [(dmw, 16)], load_wdn)

            def plan_xquant(c, src, dst, extra_w):
                sv0 = svv.n
                svv.n += 2

                def xquant(v, c=c, src=src, dst=dst, sv0=sv0):
                    v.tensor_scalar(ab[:], src, mcol[:, c:c + 1],
                                    None, ALU.subtract)
                    nblk = D // 16
                    v.reduce_max(maxa[:, 0:nblk], r16(ab[:]), axis=AXX,
                                 apply_absolute_value=True).then_inc(svv.h, 1)
                    v.wait_ge(svv.h, sv0 + 1)
                    v.tensor_scalar(maxa[:, 0:nblk], maxa[:, 0:nblk],
                                    6e-12, None, ALU.max)
                    v.reciprocal(rm6[:, 0:nblk], maxa[:, 0:nblk]
                                 ).then_inc(svv.h, 1)
                    v.wait_ge(svv.h, sv0 + 2)
                    v.tensor_scalar(rm6[:, 0:nblk], rm6[:, 0:nblk], 6.0,
                                    None, ALU.mult)
                    v.tensor_scalar(sca[:, 0:nblk], maxa[:, 0:nblk],
                                    rst6[:, c:c + 1], None, ALU.mult)
                    v.tensor_tensor(r16(zb[:, 0:D]), r16(ab[:]),
                                    bc16(rm6[:, 0:nblk], nblk), ALU.mult)
                    return v._custom_dve(
                        FP4_QD, out=r16(dst), in0=r16(zb[:, 0:D]),
                        in1=bc16(sca[:, 0:nblk], nblk),
                        s0=FP4_C0, s1=FP4_C1)
                step("vector", [(dmx, dmx.n)] + extra_w, [(vp, 1)], xquant)

            tp_kb_prev = [tp.n] * 8
            for g in range(NGRP):
                if g == 0:
                    for ci in range(GRP):
                        c = g * GRP + ci
                        step("sync",
                             [(vp, resid_done[ci]), (stq, store_done[ci])],
                             [(dmx, 16)],
                             lambda sy, c=c, ci=ci: sy.dma_start(
                                 out=xbr[ci][:],
                                 in_=x_src[c * 128:(c + 1) * 128, :]))
                        plan_xquant(c, xbr[ci][:], qq[:, 0:D], [(tp, tp.n)])
                        step("tensor", [(vp, vp.n)], [], None)
                        xpose_evac(
                            [qq[:, kb * 128:(kb + 1) * 128] for kb in range(8)],
                            [xqT[ci][:, kb * 128:(kb + 1) * 128]
                             for kb in range(8)])
                    sp_xq[0] = sp.n

                pending_hq = [None]

                def flush_hq(vp_tgt):
                    if pending_hq[0] is None:
                        return
                    ih0, ci0 = pending_hq[0]
                    pending_hq[0] = None
                    step("tensor", [(vp, vp_tgt)], [], None)
                    xpose_evac(
                        [qq[:, ib * 128:(ib + 1) * 128] for ib in range(11)],
                        [hqT[:, ((ih0 * 11 + ib) * GRP + ci0) * 128:
                             ((ih0 * 11 + ib) * GRP + ci0) * 128 + 128]
                         for ib in range(11)])

                for ih in range(2):
                    w_kb = [0] * 8
                    u_kb = [0] * 8
                    for kb in range(8):
                        step("sync", [(tp, tp_kb_prev[kb])], [(dmw, 16)],
                             lambda sy, kb=kb, ih=ih: sy.dma_start(
                                 out=wgh[:, kb * IH:(kb + 1) * IH].rearrange(
                                     "p (r t) -> p r t", r=11),
                                 in_=wgq[kb, ih * 11:(ih + 1) * 11].rearrange(
                                     "r p t -> p r t")))
                        w_kb[kb] = dmw.n
                        step("sync", [(tp, tp_kb_prev[kb])], [(dmu, 16)],
                             lambda sy, kb=kb, ih=ih: sy.dma_start(
                                 out=wuh[:, kb * IH:(kb + 1) * IH].rearrange(
                                     "p (r t) -> p r t", r=11),
                                 in_=wuq[kb, ih * 11:(ih + 1) * 11].rearrange(
                                     "r p t -> p r t")))
                        u_kb[kb] = dmu.n

                    for ci in range(GRP):
                        c = g * GRP + ci
                        tp_kb_cur = [0] * 8
                        for kb in range(8):
                            def gup_kb(t, ci=ci, kb=kb, g=g):
                                lhs = xqT[(g * GRP + ci) % 8][
                                    :, kb * 128:(kb + 1) * 128]
                                first, last = kb == 0, kb == 7
                                for nw in range(3):
                                    n0, n1 = nw * 512, min((nw + 1) * 512, IH)
                                    t.matmul(ps_a[:, n0:n1], lhs,
                                             wgh[:, kb * IH + n0:kb * IH + n1],
                                             start=first, stop=last)
                                    ins = t.matmul(
                                        ps_b[:, n0:n1], lhs,
                                        wuh[:, kb * IH + n0:kb * IH + n1],
                                        start=first, stop=last)
                                return ins
                            w = [(dmw, w_kb[kb]), (dmu, u_kb[kb])]
                            if kb == 0:
                                w += [(sp, max(sp_free_a[0], sp_xq[0])),
                                      (vp, vp_free_b[0])]
                            step("tensor", w, [(tp, 1)], gup_kb)
                            tp_kb_cur[kb] = tp.n
                        if ci == GRP - 1:
                            tp_kb_prev = tp_kb_cur
                        tp_gup = tp.n

                        step("scalar", [(tp, tp_gup), (vp, vp_free_b[0])], [(sp, 1)],
                             lambda s: s.activation(sgb[:], ps_a[:, 0:IH], AF.Silu))
                        sp_free_a[0] = sp.n

                        hq_vp = vp.n
                        flush_hq(hq_vp)
                        tp_xp = tp.n

                        def hmult(v):
                            return v.tensor_tensor(hb[:], sgb[:], ps_b[:, 0:IH],
                                                   ALU.mult)
                        step("vector", [(sp, sp_free_a[0]), (tp, tp_gup)],
                             [(vp, 1)], hmult)
                        vp_free_b[0] = vp.n

                        sv0 = svv.n
                        svv.n += 2

                        def hquant(v, sv0=sv0):
                            nblk = IH // 16
                            v.reduce_max(maxa[:, 0:nblk], r16(hb[:]), axis=AXX,
                                         apply_absolute_value=True).then_inc(svv.h, 1)
                            v.wait_ge(svv.h, sv0 + 1)
                            v.tensor_scalar(maxa[:, 0:nblk], maxa[:, 0:nblk], 6e-12,
                                            None, ALU.max)
                            v.reciprocal(rm6[:, 0:nblk], maxa[:, 0:nblk]
                                         ).then_inc(svv.h, 1)
                            v.wait_ge(svv.h, sv0 + 2)
                            v.tensor_scalar(rm6[:, 0:nblk], rm6[:, 0:nblk], 6.0,
                                            None, ALU.mult)
                            v.tensor_scalar(sca[:, 0:nblk], maxa[:, 0:nblk],
                                            1.0 / 6.0, None, ALU.mult)
                            v.tensor_tensor(r16(zb[:, 0:IH]), r16(hb[:]),
                                            bc16(rm6[:, 0:nblk], nblk), ALU.mult)
                            return v._custom_dve(
                                FP4_QD, out=r16(qq[:, 0:IH]), in0=r16(zb[:, 0:IH]),
                                in1=bc16(sca[:, 0:nblk], nblk), s0=FP4_C0, s1=FP4_C1)
                        step("vector", [(tp, tp_xp)], [(vp, 1)], hquant)
                        pending_hq[0] = (ih, ci)

                        if ih == 1 and g < NGRP - 1:
                            c_n = (g + 1) * GRP + ci
                            sl = c_n % 8
                            step("sync",
                                 [(vp, resid_done[sl]), (stq, store_done[sl])],
                                 [(dmx, 16)],
                                 lambda sy, c_n=c_n, sl=sl: sy.dma_start(
                                     out=xbr[sl][:],
                                     in_=x_src[c_n * 128:(c_n + 1) * 128, :]))
                            plan_xquant(c_n, xbr[sl][:],
                                        qx[:, ci * D:(ci + 1) * D],
                                        [(tp, qx_free[0])])
                            vp_xq = vp.n

                flush_hq(vp.n)
                for ci in range(GRP):
                    c = g * GRP + ci
                    sl = c % 8
                    ps = ps_a if ci % 2 == 0 else ps_b

                    def down(t, ci=ci, ps=ps):
                        for ib in range(22):
                            lhs = hqT[:, (ib * GRP + ci) * 128:
                                      (ib * GRP + ci) * 128 + 128]
                            for nw in range(2):
                                ins = t.matmul(
                                    ps[:, nw * 512:(nw + 1) * 512], lhs,
                                    wdn[:, ib * D + nw * 512:ib * D + (nw + 1) * 512],
                                    start=(ib == 0), stop=(ib == 21))
                        return ins
                    w = [(sp, sp.n)]
                    if ci >= 2:
                        w.append((vp, resid_done_ps[ci % 2]))
                    elif ci == 1:
                        w.append((vp, vp_free_b[0]))
                    step("tensor", w, [(tp, 1)], down)
                    step("vector", [(tp, tp.n)], [(vp, 1)],
                         lambda v, sl=sl, ps=ps: v.tensor_tensor(
                             xbr[sl][:], xbr[sl][:], ps[:, 0:D], ALU.add))
                    resid_done[sl] = vp.n
                    resid_done_ps[ci % 2] = vp.n
                    step("gpsimd", [(vp, vp.n)], [(stq, 16)],
                         lambda g2, c=c, sl=sl: g2.dma_start(
                             out=x_dst[c * 128:(c + 1) * 128, :], in_=xbr[sl][:]))
                    store_done[sl] = stq.n
                vp_free_b[0] = vp.n
                sp_free_a[0] = max(sp_free_a[0], sp.n)

                if g < NGRP - 1:
                    step("tensor", [(vp, vp_xq)], [], None)
                    for ci in range(GRP):
                        sl = ((g + 1) * GRP + ci) % 8
                        xpose_evac(
                            [qx[:, ci * D + kb * 128:ci * D + (kb + 1) * 128]
                             for kb in range(8)],
                            [xqT[sl][:, kb * 128:(kb + 1) * 128]
                             for kb in range(8)])
                    qx_free[0] = tp.n
                    sp_xq[0] = sp.n

            step("gpsimd", [(stq, stq.n)], [], None)
        barrier()

    rec_st = {"sV": 0, "v10": 0}

    def rec_split(k2ap, dst_fn):
        s_scr = sp.n
        sV = {}
        vV = {}
        for m in range(2, M):
            cur, p1, p2 = (m % 3) * D, ((m - 1) % 3) * D, ((m - 2) % 3) * D

            def vbody(v, m=m, cur=cur, p1=p1, p2=p2, k2ap=k2ap):
                if m == 2:
                    v.memset(zb[:, 0:D], 1.0)
                    v.tensor_scalar(zb[:, D:2 * D], k2ap, 0.5, None, ALU.mult)
                v.tensor_tensor(zb[:, cur:cur + D], k2ap,
                                zb[:, p1:p1 + D], ALU.mult)
                return v.tensor_tensor(zb[:, cur:cur + D], zb[:, cur:cur + D],
                                       zb[:, p2:p2 + D], ALU.subtract)
            if m == 2:
                w = [(sp, max(s_scr, rec_st["sV"]))]
            elif m >= 5:
                w = [(sp, sV[m - 3])]
            else:
                w = []
            step("vector", w, [(vp, 1)], vbody)
            vV[m] = vp.n

            step("scalar", [(vp, vV[m])], [(sp, 1)],
                 lambda s, m=m, cur=cur: s.copy(dst_fn(m, 0, D),
                                                zb[:, cur:cur + D]))
            sV[m] = sp.n
        rec_st["sV"] = sV[M - 1]
        rec_st["v10"] = vV[M - 1]

    def attn_phase(x_src, x_dst):
        ln_stats(x_src)
        with ExitStack() as kst:
            kvacc = kst.enter_context(
                nc.sbuf_tensor("kvacc", [128, NPAIR * M * 128], F32))
            kvar = kst.enter_context(
                nc.sbuf_tensor("kvar", [128, NPAIR * M * 128], BF16, side="right"))
            with ExitStack() as ast:
                asb = lambda name, shape, dt=F32: ast.enter_context(
                    nc.sbuf_tensor(name, shape, dt))
                wkv = asb("wkv", [128, 8 * 2048], BF16)
                tkall = asb("tkall", [128, SUB * M * D], BF16)
                vgrp = asb("vgrp", [128, SUB * D], BF16)
                scr2b = asb("scr2b", [128, D])
                xqTa = [asb(f"xqTa{i}", [128, D], BF16) for i in range(2)]

                def load_wkv(sy):
                    ins = None
                    for kb in range(8):
                        if ins is not None:
                            ins.then_inc(dmw.h, 16)
                        ins = sy.dma_start(
                            out=wkv[:, kb * 2048:(kb + 1) * 2048].rearrange(
                                "p (r t) -> p r t", r=16),
                            in_=winT[kb, 8:24].rearrange("r p t -> p r t"))
                    return ins
                dmw.n += 7 * 16
                step("sync", [], [(dmw, 16)], load_wkv)

                def tk0(g2):
                    g2.memset(tkall[:, 0:D], 1.0)
                    return g2.memset(tkall[:, M * D:M * D + D], 1.0)
                step("gpsimd", [], [(gp, 1)], tk0)

                ab_free = [0]
                for sg in range(NCHUNK // SUB):
                    kevac_done = [0, 0]
                    vevac_done = [0, 0]
                    s_scr_c = [0, 0]
                    for ci in range(SUB):
                        c = sg * SUB + ci
                        step("sync", [(vp, vp.n)], [(dmx, 16)],
                             lambda sy, c=c, ci=ci: sy.dma_start(
                                 out=xb[ci][:], in_=x_src[c * 128:(c + 1) * 128, :]))
                        step("vector", [(dmx, dmx.n), (tp, tp.n)], [(vp, 1)],
                             lambda v, c=c, ci=ci: v.tensor_scalar(
                                 qq[:, 0:D], xb[ci][:], mcol[:, c:c + 1],
                                 rstd[:, c:c + 1], ALU.subtract, ALU.mult))
                        step("tensor", [(vp, vp.n)], [], None)
                        xpose_evac(
                            [qq[:, kb * 128:(kb + 1) * 128] for kb in range(8)],
                            [xqTa[ci][:, kb * 128:(kb + 1) * 128] for kb in range(8)],
                            evac_waits=[(stq, stq.n)])
                        step("gpsimd", [(sp, sp.n)], [(stq, 16)],
                             lambda g2, c=c, ci=ci: g2.dma_start(
                                 out=xnT_h[c].rearrange("(kb kp) t -> kp kb t",
                                                        kp=128),
                                 in_=xqTa[ci][:].rearrange("p (kb t) -> p kb t",
                                                           kb=8)))

                        def kvproj(t, ci=ci):
                            for kb in range(8):
                                lhs = xqTa[ci][:, kb * 128:(kb + 1) * 128]
                                for nw in range(4):
                                    ps = ps_a if nw < 2 else ps_b
                                    ins = t.matmul(
                                        ps[:, (nw % 2) * 512:(nw % 2 + 1) * 512], lhs,
                                        wkv[:, kb * 2048 + nw * 512:
                                            kb * 2048 + (nw + 1) * 512],
                                        start=(kb == 0), stop=(kb == 7))
                            return ins
                        step("tensor",
                             [(dmw, dmw.n), (vp, kevac_done[0]),
                              (sp, vevac_done[0])],
                             [(tp, 1)], kvproj)
                        tp_proj = tp.n

                        def kevac(v):
                            v.tensor_scalar(ab[:], ps_a[:, 0:D], 0.125, 1.0,
                                            ALU.mult, ALU.min)
                            return v.tensor_scalar(ab[:], ab[:], -1.0, None, ALU.max)
                        step("vector", [(tp, tp_proj), (sp, ab_free[0])],
                             [(vp, 1)], kevac)
                        kevac_done[0] = vp.n
                        step("scalar", [(tp, tp_proj)], [(sp, 1)],
                             lambda s, ci=ci: s.copy(vgrp[:, ci * D:(ci + 1) * D],
                                                     ps_b[:, 0:D]))
                        vevac_done[0] = sp.n

                        k2dst = scr[:] if ci == 0 else scr2b[:]

                        def sinit(s, ci=ci, k2dst=k2dst):
                            s.copy(tkall[:, ci * M * D + D:ci * M * D + 2 * D],
                                   ab[:])
                            return s.activation(k2dst, ab[:], AF.Copy, scale=2.0)
                        step("scalar",
                             [(vp, max(kevac_done[0], rec_st["v10"]))],
                             [(sp, 1)], sinit)
                        ab_free[0] = sp.n
                        s_scr_c[ci] = sp.n

                    for ci in range(SUB):
                        rec_split(scr[:] if ci == 0 else scr2b[:],
                                  lambda m, lo, hi, ci=ci:
                                  tkall[:, ci * M * D + m * D + lo:
                                        ci * M * D + m * D + hi])

                    for pp in range(NPAIR // 2):
                        def kvmm(t, sg=sg, pp=pp):
                            for half in range(2):
                                pair = pp * 2 + half
                                ps = ps_a if half == 0 else ps_b
                                for ci in range(SUB):
                                    lhs = vgrp[:, ci * D + pair * 128:
                                               ci * D + (pair + 1) * 128]
                                    for mg in range(3):
                                        m0 = mg * 4
                                        mN = min(4, M - m0)
                                        rhs = tkall[:].rearrange(
                                            "p (c m t) -> p c m t", c=SUB, m=M)[
                                            :, ci, m0:m0 + mN,
                                            pair * 128:(pair + 1) * 128]
                                        ins = t.matmul(
                                            ps[:, m0 * 128:(m0 + mN) * 128], lhs, rhs,
                                            start=(ci == 0), stop=(ci == SUB - 1))
                            return ins
                        step("tensor", [(sp, sp.n), (vp, kevac_done[0])],
                             [(tp, 1)], kvmm)

                        def kvadd(v, sg=sg, pp=pp):
                            for half in range(2):
                                pair = pp * 2 + half
                                ps = ps_a if half == 0 else ps_b
                                dst = kvacc[:, pair * M * 128:(pair + 1) * M * 128]
                                if sg == 0:
                                    ins = v.tensor_copy(dst, ps[:, 0:M * 128])
                                else:
                                    ins = v.tensor_tensor(dst, dst, ps[:, 0:M * 128],
                                                          ALU.add)
                            return ins
                        step("vector", [(tp, tp.n)], [(vp, 1)], kvadd)
                        step("tensor", [(vp, vp.n)], [], None)

                step("gpsimd", [(stq, stq.n)], [], None)
                barrier()

                step("gpsimd", [], [(gp, 1)], lambda g2: g2.memset(kvar[:], 0.0))
                step("tensor", [(gp, gp.n)], [], None)
                arr_base = sp.n
                for pair in range(NPAIR):
                    for m in range(M):
                        j = pair * M + m
                        slot = ps_a[:, 0:128] if j % 2 == 0 else ps_b[:, 0:128]
                        step("tensor",
                             [(sp, arr_base + 2 * (j - 1))] if j >= 2 else [],
                             [(tp, 1)],
                             lambda t, pair=pair, m=m, slot=slot: t.transpose(
                                 slot,
                                 kvacc[:, (pair * M + m) * 128:
                                       (pair * M + m + 1) * 128],
                                 identf[:]))

                        def kvevac(s, pair=pair, m=m, slot=slot):
                            base = (pair * M + m) * 128
                            s.activation(kvar[0:64, base:base + 64], slot[0:64, 0:64],
                                         AF.Copy,
                                         scale=betac[0:64, pair * M + m:
                                                     pair * M + m + 1])
                            return s.activation(
                                kvar[64:128, base + 64:base + 128],
                                slot[64:128, 64:128], AF.Copy,
                                scale=betac[64:128, pair * M + m:pair * M + m + 1])
                        step("scalar", [(tp, tp.n)], [(sp, 2)], kvevac)
                step("tensor", [(sp, sp.n)], [], None)
                barrier()

            with ExitStack() as bst:
                bsb2 = lambda name, shape, dt=F32: bst.enter_context(
                    nc.sbuf_tensor(name, shape, dt))
                wq8 = bsb2("wq8", [128, 8 * D], BF16)
                woT = bsb2("woT", [128, 8 * D], BF16)
                xbT = bsb2("xbT", [128, 8 * D], BF16)
                tqa = bsb2("tqa", [128, M * D], BF16)
                aoT = bsb2("aoT", [128, 8 * D], BF16)
                qf = bsb2("qf", [128, D])

                def load_wb(sy):
                    ins = None
                    for kb in range(8):
                        if ins is not None:
                            ins.then_inc(dmw.h, 16)
                        ins = sy.dma_start(
                            out=wq8[:, kb * D:(kb + 1) * D].rearrange(
                                "p (r t) -> p r t", r=8),
                            in_=winT[kb, 0:8].rearrange("r p t -> p r t"))
                        ins.then_inc(dmw.h, 16)
                        ins = sy.dma_start(
                            out=woT[:, kb * D:(kb + 1) * D].rearrange(
                                "p (r t) -> p r t", r=8),
                            in_=woutT[kb, :].rearrange("r p t -> p r t"))
                    return ins
                dmw.n += 15 * 16
                step("sync", [], [(dmw, 16)], load_wb)

                step("gpsimd", [], [(gp, 1)],
                     lambda g2: g2.memset(tqa[:, 0:D], 1.0))

                bstore = [0] * 4
                for stile in range(4):
                    c0 = stile * 8

                    def load_x(sy, c0=c0):
                        ins = None
                        for kb in range(8):
                            if ins is not None:
                                ins.then_inc(dmx.h, 16)
                            ins = sy.dma_start(
                                out=xbT[:, kb * D:(kb + 1) * D].rearrange(
                                    "p (c t) -> p c t", c=8),
                                in_=xnT_h[c0:c0 + 8, kb * 128:(kb + 1) * 128]
                                .rearrange("c kp t -> kp c t"))
                        return ins
                    dmx.n += 7 * 16
                    step("sync", [(vp, vp.n)], [(dmx, 16)], load_x)

                    qf_done = [0]
                    qf_free = [0]
                    ao_done = [0]
                    tq_done = {}
                    pending_out = [None]

                    def plan_outmm(pair):
                        def outmm(t, pair=pair):
                            for m in range(M):
                                lhs = kvar[:, (pair * M + m) * 128:
                                           (pair * M + m + 1) * 128]
                                for nw in range(2):
                                    ins = t.matmul(
                                        ps_b[:, nw * 512:(nw + 1) * 512], lhs,
                                        tqa[:, m * D + nw * 512:
                                            m * D + (nw + 1) * 512],
                                        start=(m == 0), stop=(m == M - 1))
                            return ins
                        step("tensor",
                             [(sp, max(tq_done[pair], ao_done[0]))],
                             [(tp, 1)], outmm)
                        step("scalar", [(tp, tp.n)], [(sp, 1)],
                             lambda s, pair=pair: s.copy(
                                 aoT[:, pair * D:(pair + 1) * D], ps_b[:, 0:D]))
                        ao_done[0] = sp.n

                    for pair in range(NPAIR):
                        def qproj(t, pair=pair):
                            for kb in range(8):
                                lhs = wq8[:, kb * D + pair * 128:
                                          kb * D + (pair + 1) * 128]
                                for nw in range(2):
                                    ins = t.matmul(
                                        ps_a[:, nw * 512:(nw + 1) * 512], lhs,
                                        xbT[:, kb * D + nw * 512:
                                            kb * D + (nw + 1) * 512],
                                        start=(kb == 0), stop=(kb == 7))
                            return ins
                        step("tensor",
                             [(dmx, dmx.n), (dmw, dmw.n), (vp, qf_done[0])],
                             [(tp, 1)], qproj)
                        tp_q = tp.n

                        def qfevac(v):
                            v.tensor_scalar(qf[:], ps_a[:, 0:1024], 0.125, 1.0,
                                            ALU.mult, ALU.min)
                            return v.tensor_scalar(qf[:], qf[:], -1.0, None, ALU.max)
                        step("vector", [(tp, tp_q), (sp, qf_free[0])],
                             [(vp, 1)], qfevac)
                        qf_done[0] = vp.n

                        if pending_out[0] is not None:
                            plan_outmm(pending_out[0])
                        tp_out = tp.n

                        def sinit(s):
                            s.copy(tqa[:, D:2 * D], qf[:])
                            return s.activation(scr[:], qf[:], AF.Copy, scale=2.0)
                        step("scalar",
                             [(vp, max(qf_done[0], rec_st["v10"])),
                              (tp, tp_out)], [(sp, 1)], sinit)
                        qf_free[0] = sp.n

                        rec_split(scr[:],
                                  lambda m, lo, hi: tqa[:, m * D + lo:m * D + hi])
                        tq_done[pair] = sp.n
                        pending_out[0] = pair
                    plan_outmm(pending_out[0])

                    wout_resid = [0, 0]
                    for ci in range(8):
                        c = c0 + ci
                        ps = ps_a if ci % 2 == 0 else ps_b
                        step("sync", [(vp, vp.n), (stq, bstore[ci % 4])],
                             [(dmx, 16)],
                             lambda sy, c=c, ci=ci: sy.dma_start(
                                 out=xb[ci % 4][:],
                                 in_=x_src[c * 128:(c + 1) * 128, :]))

                        def woutmm(t, ci=ci, ps=ps):
                            for kb in range(8):
                                lhs = aoT[:, kb * D + ci * 128:kb * D + ci * 128 + 128]
                                for nw in range(2):
                                    ins = t.matmul(
                                        ps[:, nw * 512:(nw + 1) * 512], lhs,
                                        woT[:, kb * D + nw * 512:
                                            kb * D + (nw + 1) * 512],
                                        start=(kb == 0), stop=(kb == 7))
                            return ins
                        w = [(sp, sp.n)]
                        if ci >= 2:
                            w.append((vp, wout_resid[ci % 2]))
                        step("tensor", w, [(tp, 1)], woutmm)
                        step("vector", [(tp, tp.n), (dmx, dmx.n)], [(vp, 1)],
                             lambda v, ci=ci, ps=ps: v.tensor_tensor(
                                 xb[ci % 4][:], xb[ci % 4][:], ps[:, 0:D], ALU.add))
                        wout_resid[ci % 2] = vp.n
                        step("gpsimd", [(vp, vp.n)], [(stq, 16)],
                             lambda g2, c=c, ci=ci: g2.dma_start(
                                 out=x_dst[c * 128:(c + 1) * 128, :],
                                 in_=xb[ci % 4][:]))
                        bstore[ci % 4] = stq.n

                step("gpsimd", [(stq, stq.n)], [], None)
        barrier()

    nph = [p for p in ("ffn1", "attn", "ffn2") if p in phases]
    cur = x_in
    for idx, p in enumerate(nph):
        dst = y_out if idx == len(nph) - 1 else xs[idx % 2]
        if p in ("ffn1", "ffn2"):
            ffn_phase(cur, dst)
        else:
            attn_phase(cur, dst)
        cur = dst
    if cur is not y_out:
        for c in range(NCHUNK):
            dmx_tgt = dmx.n + 16

            def cl(sy, c=c, cur=cur, dmx_tgt=dmx_tgt):
                ins = sy.dma_start(out=xb[0][:], in_=cur[c * 128:(c + 1) * 128, :])
                ins.then_inc(dmx.h, 16)
                sy.wait_ge(dmx.h, dmx_tgt)
                return sy.dma_start(out=y_out[c * 128:(c + 1) * 128, :], in_=xb[0][:])
            step("sync", [(stq, stq.n)], [(stq, 16)], cl)
            dmx.n += 16

    step("sync", [(stq, stq.n)], [], None)

    with nc.Block() as block:
        @block.sync
        def _(e):
            for f in prog["sync"]:
                f(e)

        @block.vector
        def _(e):
            for f in prog["vector"]:
                f(e)

        @block.scalar
        def _(e):
            for f in prog["scalar"]:
                f(e)

        @block.tensor
        def _(e):
            for f in prog["tensor"]:
                f(e)

        @block.gpsimd
        def _(e):
            for f in prog["gpsimd"]:
                f(e)

    st.close()
    mybir.codegen_inst_isa_subclasses(nc)
    return nc


_NC_CACHE = {}


def kernel(L0_attn_x, L0_attn_w_in, L0_attn_w_out, L0_attn_beta,
           L0_ffn_w_gate, L0_ffn_w_up, L0_ffn_w_down):
    x = np.ascontiguousarray(np.asarray(L0_attn_x, dtype=np.float32))
    if "nc" not in _NC_CACHE:
        _NC_CACHE["nc"] = build_program()
    nc = _NC_CACHE["nc"]
    w = {
        "w_in": np.ascontiguousarray(np.asarray(L0_attn_w_in, np.float32)),
        "w_out": np.ascontiguousarray(np.asarray(L0_attn_w_out, np.float32)),
        "beta": np.ascontiguousarray(np.asarray(L0_attn_beta, np.float32)),
        "w_gate": np.ascontiguousarray(np.asarray(L0_ffn_w_gate, np.float32)),
        "w_up": np.ascontiguousarray(np.asarray(L0_ffn_w_up, np.float32)),
        "w_down": np.ascontiguousarray(np.asarray(L0_ffn_w_down, np.float32)),
    }
    in_maps = [dict(w, x=np.ascontiguousarray(x[b])) for b in range(B)]
    res = run_bass_kernel_spmd(nc, in_maps, core_ids=list(range(B)))
    out = np.stack([np.asarray(res.results[b]["y"]) for b in range(B)], axis=0)
    return out.astype(np.float32)


# version 11
# speedup vs baseline: 1.3224x; 1.0019x over previous
import numpy as np

import concourse.bass as bass
import concourse.mybir as mybir
from concourse.bass_utils import run_bass_kernel_spmd
from concourse import dve_ops
from concourse.dve_ops import DveOp, OPS
from concourse.dve_spec import Spec, Src0, Src1, C0, C1, Bin, lower as dve_lower, maxx
from concourse.dve_uop import AluOp as DAlu
from concourse.dve_table_gen import dve_ver_for

F32 = mybir.dt.float32
BF16 = mybir.dt.bfloat16
AF = mybir.ActivationFunctionType
ALU = mybir.AluOpType
AXX = mybir.AxisListType.X

B, S, D, I = 8, 4096, 1024, 2816
NCHUNK = S // 128
GRP = 4
NGRP = NCHUNK // GRP
IH = I // 2
NH, DH, M = 16, 64, 11
NPAIR = NH // 2
SUB = 2


def _fp4_ref(in0, in1, s0, s1, imm2):
    z = np.asarray(in0)
    az = np.abs(z)
    s_ = np.maximum(np.minimum(az, 4.0), 1.0)
    s_ = np.exp2(np.floor(np.log2(s_)))
    h = s_ / 2
    return ((np.round(z / h) * h) * in1).astype(z.dtype)


def _register_fp4():
    for op in OPS:
        if op.name == "FP4_QD":
            return op
    c_ = maxx(Bin(DAlu.BITWISE_AND, Bin(DAlu.MULTIPLY, Src0, C1), C0), C1)
    X_ = Bin(DAlu.ADD, Bin(DAlu.ADD, c_, c_), c_)
    rq_ = Bin(DAlu.SUBTRACT, Bin(DAlu.ADD, Src0, X_), X_)
    spec = Spec(body=Bin(DAlu.MULTIPLY, rq_, Src1))
    ver = dve_ver_for("TRN2")
    sha = dve_lower(spec, ver=ver, name="FP4_QD").sha(ver)
    op = DveOp("FP4_QD", spec, subdim=False, uops_sha={ver: sha}, reference=_fp4_ref)
    OPS.append(op)
    dve_ops._SUB_OPCODE_FOR_NAME[op.name] = dve_ops._CUSTOM_DVE_ROW_BASE + len(OPS) - 1
    dve_ops.CUSTOM_DVE_SPECS[op.name] = spec
    dve_ops.CUSTOM_DVE_REFS[op.name] = _fp4_ref
    return op


FP4_QD = _register_fp4()
FP4_C0 = float(2 ** 24)
FP4_C1 = float(2 ** 21)


class Sem:
    def __init__(self, h):
        self.h = h
        self.n = 0


def bc16(t, nblk):
    return t[:, :, None].broadcast_to([128, nblk, 16])


def r16(ap):
    return ap.rearrange("p (b k) -> p b k", k=16)


def build_program(phases=("ffn1", "attn", "ffn2")):
    from contextlib import ExitStack

    nc = bass.Bass("TRN2", target_bir_lowering=False,
                   detect_race_conditions=False)

    x_in = nc.dram_tensor("x", [S, D], F32, kind="ExternalInput").ap()
    w_in = nc.dram_tensor("w_in", [3 * D, D], F32, kind="ExternalInput").ap()
    w_out = nc.dram_tensor("w_out", [D, D], F32, kind="ExternalInput").ap()
    beta = nc.dram_tensor("beta", [NH, M], F32, kind="ExternalInput").ap()
    w_gate = nc.dram_tensor("w_gate", [I, D], F32, kind="ExternalInput").ap()
    w_up = nc.dram_tensor("w_up", [I, D], F32, kind="ExternalInput").ap()
    w_down = nc.dram_tensor("w_down", [D, I], F32, kind="ExternalInput").ap()
    y_out = nc.dram_tensor("y", [S, D], F32, kind="ExternalOutput").ap()

    xs = [nc.dram_tensor(f"xs{i}", [S, D], F32).ap() for i in range(2)]
    wgq = nc.dram_tensor("wgq", [8, 22, 128, 128], BF16).ap()
    wuq = nc.dram_tensor("wuq", [8, 22, 128, 128], BF16).ap()
    wdq = nc.dram_tensor("wdq", [22, 8, 128, 128], BF16).ap()
    winT = nc.dram_tensor("winT", [8, 24, 128, 128], BF16).ap()
    woutT = nc.dram_tensor("woutT", [8, 8, 128, 128], BF16).ap()
    xnT_h = nc.dram_tensor("xnT", [NCHUNK, D, 128], BF16).ap()

    st = ExitStack()
    sb = lambda name, shape, dt=F32, side="left": st.enter_context(
        nc.sbuf_tensor(name, shape, dt, side=side))

    xb = [sb(f"xb{i}", [128, D]) for i in range(4)]
    ab = sb("ab", [128, D])
    zb = sb("zb", [128, max(I, 3 * 1024)])
    qq = sb("qq", [128, I], BF16)
    scr = sb("scr", [128, D])
    hb = sb("hb", [128, IH])
    msum = sb("msum", [128, NCHUNK])
    sqs = sb("sqs", [128, NCHUNK])
    mcol = sb("mcol", [128, NCHUNK])
    rstd = sb("rstd", [128, NCHUNK])
    rst6 = sb("rst6", [128, NCHUNK])
    wvar = sb("wvar", [128, NCHUNK])
    stmp = sb("stmp", [128, NCHUNK])
    maxa = sb("maxa", [128, I // 16])
    rm6 = sb("rm6", [128, I // 16])
    sca = sb("sca", [128, I // 16])
    ident = sb("ident", [128, 128], BF16)
    identf = sb("identf", [128, 128])
    betac = sb("betac", [128, NPAIR * M])

    ps_a = st.enter_context(nc.psum_tensor([128, 1536], F32))
    ps_b = st.enter_context(nc.psum_tensor([128, 1536], F32))
    psA_t = st.enter_context(nc.psum_tensor([128, 1024], BF16))
    psB_t = st.enter_context(nc.psum_tensor([128, 1024], BF16))
    slotA = psA_t[:, 0:128]
    slotB = psB_t[:, 0:128]

    names = ["dmx", "dmw", "dmu", "vp", "sp", "tp", "gp", "stq", "svv", "ph"]
    _sems = {n: Sem(st.enter_context(nc.semaphore(n))) for n in names}
    dmx, dmw, dmu, vp, sp, tp, gp, stq, svv, ph = (_sems[n] for n in names)

    prog = {k: [] for k in ("sync", "vector", "scalar", "tensor", "gpsimd")}
    ENGS = list(prog.keys())

    def step(eng, waits, incs, body):
        w = [(s.h, v) for (s, v) in waits]
        ii = [(s.h, k) for (s, k) in incs]
        for s, k in incs:
            s.n += k

        def f(e, w=w, ii=ii, body=body):
            for h, v in w:
                if v > 0:
                    e.wait_ge(h, v)
            if body is None:
                assert not ii
                return
            ins = body(e)
            for h, k in ii:
                ins.then_inc(h, k)
        prog[eng].append(f)

    def barrier():
        tgt = ph.n + 5
        for eng in ENGS:
            def f(e, tgt=tgt):
                e.sem_inc(ph.h, 1)
                e.wait_ge(ph.h, tgt)
            prog[eng].append(f)
        ph.n = tgt

    slot_free = [0, 0]

    def xpose_evac(srcs, dsts, scales=None, evac_waits=()):
        n = len(srcs)
        for i in range(n):
            si = i % 2
            slot = slotA if si == 0 else slotB
            step("tensor",
                 [(sp, slot_free[si])],
                 [(tp, 1)],
                 lambda t, slot=slot, src=srcs[i]: t.transpose(slot, src, ident[:]))
            scl = scales[i] if scales else None
            ew = list(evac_waits) if i == 0 else []
            if scl is None:
                step("scalar", [(tp, tp.n)] + ew, [(sp, 1)],
                     lambda s, slot=slot, dst=dsts[i]: s.copy(dst, slot))
            else:
                step("scalar", [(tp, tp.n)] + ew, [(sp, 1)],
                     lambda s, slot=slot, dst=dsts[i], scl=scl:
                     s.activation(dst, slot, AF.Copy, scale=scl))
            slot_free[si] = sp.n

    step("gpsimd", [], [(gp, 1)], lambda g: g.memset(ident[:], 0.0))

    gwait = gp.n

    def g_pro2(g, gwait=gwait):
        g.wait_ge(gp.h, gwait)
        g.affine_select(out=ident[:], in_=ident[:],
                        compare_op=ALU.not_equal, fill=1.0, base=0,
                        pattern=[[-1, 128]], channel_multiplier=1)
        return g.memset(identf[:], 0.0)
    step("gpsimd", [], [(gp, 1)], g_pro2)

    gwait2 = gp.n

    def g_pro3(g, gwait2=gwait2):
        g.wait_ge(gp.h, gwait2)
        return g.affine_select(out=identf[:], in_=identf[:],
                               compare_op=ALU.not_equal, fill=1.0, base=0,
                               pattern=[[-1, 128]], channel_multiplier=1)
    step("gpsimd", [], [(gp, 1)], g_pro3)

    def sy_beta(sy):
        ins = None
        for h in range(NPAIR):
            if ins is not None:
                ins.then_inc(dmw.h, 16)
            ins = sy.dma_start(out=betac[0:64, h * M:(h + 1) * M],
                               in_=beta[2 * h:2 * h + 1, :].broadcast_to([64, M]))
            ins.then_inc(dmw.h, 16)
            ins = sy.dma_start(out=betac[64:128, h * M:(h + 1) * M],
                               in_=beta[2 * h + 1:2 * h + 2, :].broadcast_to([64, M]))
        return ins
    dmw.n += 15 * 16
    step("sync", [], [(dmw, 16)], sy_beta)

    def plan_quant(waits, incs, src_ap, ncols, fold_imm, fold_ap=None):
        sv0 = svv.n
        svv.n += 2

        def body(v, sv0=sv0):
            nblk = ncols // 16
            v.reduce_max(maxa[:, 0:nblk], r16(src_ap), axis=AXX,
                         apply_absolute_value=True).then_inc(svv.h, 1)
            v.wait_ge(svv.h, sv0 + 1)
            v.tensor_scalar(maxa[:, 0:nblk], maxa[:, 0:nblk], 6e-12, None, ALU.max)
            v.reciprocal(rm6[:, 0:nblk], maxa[:, 0:nblk]).then_inc(svv.h, 1)
            v.wait_ge(svv.h, sv0 + 2)
            v.tensor_scalar(rm6[:, 0:nblk], rm6[:, 0:nblk], 6.0, None, ALU.mult)
            if fold_ap is None:
                v.tensor_scalar(sca[:, 0:nblk], maxa[:, 0:nblk], fold_imm / 6.0,
                                None, ALU.mult)
            else:
                v.tensor_scalar(sca[:, 0:nblk], maxa[:, 0:nblk], fold_ap,
                                fold_imm / 6.0, ALU.mult, ALU.mult)
            v.tensor_tensor(r16(zb[:, 0:ncols]), r16(src_ap),
                            bc16(rm6[:, 0:nblk], nblk), ALU.mult)
            return v._custom_dve(FP4_QD, out=r16(qq[:, 0:ncols]),
                                 in0=r16(zb[:, 0:ncols]),
                                 in1=bc16(sca[:, 0:nblk], nblk),
                                 s0=FP4_C0, s1=FP4_C1)
        step("vector", waits, incs, body)

    def prep_weight(w_src, n_chunks, dst, quant, fold=1.0, wst=None):
        wst = _wst[0]
        ncols = w_src.shape[1]
        nkb = ncols // 128
        for r in range(n_chunks):
            step("sync", [(vp, vp.n)], [(dmw, 16)],
                 lambda sy, r=r, ncols=ncols: sy.dma_start(
                     out=zb[:, 0:ncols], in_=w_src[r * 128:(r + 1) * 128, :]))
            if quant:
                plan_quant([(dmw, dmw.n), (tp, tp.n)], [(vp, 1)],
                           zb[:, 0:ncols], ncols, fold)
            else:
                step("vector", [(dmw, dmw.n), (tp, tp.n)], [(vp, 1)],
                     lambda v, ncols=ncols: v.tensor_copy(qq[:, 0:ncols],
                                                          zb[:, 0:ncols]))
            step("tensor", [(vp, vp.n)], [], None)
            xpose_evac([qq[:, kb * 128:(kb + 1) * 128] for kb in range(nkb)],
                       [wst[:, kb * 128:(kb + 1) * 128] for kb in range(nkb)])
            step("gpsimd", [(sp, sp.n), (stq, stq.n)], [(stq, 16)],
                 lambda g, r=r, nkb=nkb: g.dma_start(
                     out=dst[:, r][0:nkb].rearrange("kb p t -> p kb t"),
                     in_=wst[:, 0:nkb * 128].rearrange("p (kb t) -> p kb t", kb=nkb)))
            step("scalar", [(stq, stq.n)], [], None)

    _wst = [None]
    with nc.sbuf_tensor("wst", [128, 22 * 128], BF16) as wst:
        _wst[0] = wst
        if "ffn1" in phases or "ffn2" in phases:
            prep_weight(w_gate, 22, wgq, True)
            prep_weight(w_up, 22, wuq, True)
            prep_weight(w_down, 8, wdq, True, fold=0.5)
        if "attn" in phases:
            prep_weight(w_in, 24, winT, False)
            prep_weight(w_out, 8, woutT, False)
        barrier()

    def ln_stats(x_hbm, stats_ready=False):
        for c in range(NCHUNK if not stats_ready else 0):
            step("sync", [(vp, vp.n - 1)] if c >= 2 else [], [(dmx, 16)],
                 lambda sy, c=c: sy.dma_start(out=xb[c % 2][:],
                                              in_=x_hbm[c * 128:(c + 1) * 128, :]))

            def stats_body(v, c=c):
                v.reduce_sum(msum[:, c:c + 1], xb[c % 2][:], axis=AXX)
                return v.scalar_tensor_tensor(scr[:], xb[c % 2][:], 1.0, xb[c % 2][:],
                                              ALU.mult, ALU.mult,
                                              accum_out=sqs[:, c:c + 1])
            step("vector", [(dmx, dmx.n)], [(vp, 1)], stats_body)

        sv0 = svv.n
        svv.n += 1

        def smalls(v, sv0=sv0):
            v.tensor_copy(stmp[:], msum[:]).then_inc(svv.h, 1)
            v.wait_ge(svv.h, sv0 + 1)
            v.tensor_scalar(mcol[:], msum[:], 1.0 / D, None, ALU.mult)
            v.tensor_scalar(wvar[:], sqs[:], 1.0 / D, None, ALU.mult)
            v.tensor_tensor(stmp[:], mcol[:], mcol[:], ALU.mult)
            v.tensor_tensor(wvar[:], wvar[:], stmp[:], ALU.subtract)
            return v.tensor_scalar(wvar[:], wvar[:], 1e-5, None, ALU.add)
        step("vector", [], [(vp, 1)], smalls)

        def sqrt_body(s):
            s.preload_act_table(AF.Sqrt)
            ins = s.activation(rstd[:], wvar[:], AF.Sqrt)
            s.preload_act_table(AF.Silu)
            return ins
        step("scalar", [(vp, vp.n)], [(sp, 1)], sqrt_body)

        sv0 = svv.n
        svv.n += 1

        def refine(v, sv0=sv0):
            v.reciprocal(rstd[:], rstd[:]).then_inc(svv.h, 1)
            v.wait_ge(svv.h, sv0 + 1)
            v.tensor_tensor(stmp[:], rstd[:], rstd[:], ALU.mult)
            v.tensor_tensor(stmp[:], stmp[:], wvar[:], ALU.mult)
            v.tensor_scalar(stmp[:], stmp[:], -0.5, 1.5, ALU.mult, ALU.add)
            v.tensor_tensor(rstd[:], rstd[:], stmp[:], ALU.mult)
            return v.tensor_scalar(rst6[:], rstd[:], 1.0 / 6.0, None, ALU.mult)
        step("vector", [(sp, sp.n)], [(vp, 1)], refine)

    _ffn_ctr = [0]

    def ffn_phase(x_src, x_dst, stats_ready=False, fold_out=True):
        _ffn_ctr[0] += 1
        tag = f"_f{_ffn_ctr[0]}"
        ln_stats(x_src, stats_ready)
        with ExitStack() as fst:
            fsb = lambda name, shape, dt=F32: fst.enter_context(
                nc.sbuf_tensor(name + tag, shape, dt))
            wgh = fsb("wgh", [128, 8 * IH], BF16)
            wuh = fsb("wuh", [128, 8 * IH], BF16)
            wdn = fsb("wdn", [128, 22 * D], BF16)
            hqT = fsb("hqT", [128, 22 * GRP * 128], BF16)
            xqT = [fsb(f"xqT{i}", [128, D], BF16) for i in range(8)]
            xb2 = [fsb(f"xb2{i}", [128, D]) for i in range(4)]
            xbr = xb + xb2
            qx = fsb("qx", [128, 4 * D], BF16)
            sgb = fsb("sgb", [128, IH])
            sp_free_a = [0]
            vp_free_b = [0]
            sp_xq = [0]
            qx_free = [0]
            resid_done = [vp.n] * 8
            resid_done_ps = [0, 0]
            store_done = [0] * 8

            def load_wdn(sy):
                ins = None
                for ob in range(8):
                    if ins is not None:
                        ins.then_inc(dmw.h, 16)
                    ins = sy.dma_start(
                        out=wdn[:].rearrange("p (ib ob t) -> p ib ob t",
                                             ib=22, ob=8)[:, :, ob],
                        in_=wdq[:, ob].rearrange("ib p t -> p ib t"))
                return ins
            dmw.n += 7 * 16
            step("sync", [], [(dmw, 16)], load_wdn)

            def plan_xquant(c, src, dst, extra_w):
                sv0 = svv.n
                svv.n += 2

                def xquant(v, c=c, src=src, dst=dst, sv0=sv0):
                    v.tensor_scalar(ab[:], src, mcol[:, c:c + 1],
                                    None, ALU.subtract)
                    nblk = D // 16
                    v.reduce_max(maxa[:, 0:nblk], r16(ab[:]), axis=AXX,
                                 apply_absolute_value=True).then_inc(svv.h, 1)
                    v.wait_ge(svv.h, sv0 + 1)
                    v.tensor_scalar(maxa[:, 0:nblk], maxa[:, 0:nblk],
                                    6e-12, None, ALU.max)
                    v.reciprocal(rm6[:, 0:nblk], maxa[:, 0:nblk]
                                 ).then_inc(svv.h, 1)
                    v.wait_ge(svv.h, sv0 + 2)
                    v.tensor_scalar(rm6[:, 0:nblk], rm6[:, 0:nblk], 6.0,
                                    None, ALU.mult)
                    v.tensor_scalar(sca[:, 0:nblk], maxa[:, 0:nblk],
                                    rst6[:, c:c + 1], None, ALU.mult)
                    v.tensor_tensor(r16(zb[:, 0:D]), r16(ab[:]),
                                    bc16(rm6[:, 0:nblk], nblk), ALU.mult)
                    return v._custom_dve(
                        FP4_QD, out=r16(dst), in0=r16(zb[:, 0:D]),
                        in1=bc16(sca[:, 0:nblk], nblk),
                        s0=FP4_C0, s1=FP4_C1)
                step("vector", [(dmx, dmx.n)] + extra_w, [(vp, 1)], xquant)

            tp_kb_prev = [tp.n] * 8
            for g in range(NGRP):
                if g == 0:
                    for ci in range(GRP):
                        c = g * GRP + ci
                        step("sync",
                             [(vp, resid_done[ci]), (stq, store_done[ci])],
                             [(dmx, 16)],
                             lambda sy, c=c, ci=ci: sy.dma_start(
                                 out=xbr[ci][:],
                                 in_=x_src[c * 128:(c + 1) * 128, :]))
                        plan_xquant(c, xbr[ci][:], qq[:, 0:D], [(tp, tp.n)])
                        step("tensor", [(vp, vp.n)], [], None)
                        xpose_evac(
                            [qq[:, kb * 128:(kb + 1) * 128] for kb in range(8)],
                            [xqT[ci][:, kb * 128:(kb + 1) * 128]
                             for kb in range(8)])
                    sp_xq[0] = sp.n

                pending_hq = [None]

                def flush_hq(vp_tgt):
                    if pending_hq[0] is None:
                        return
                    ih0, ci0 = pending_hq[0]
                    pending_hq[0] = None
                    step("tensor", [(vp, vp_tgt)], [], None)
                    xpose_evac(
                        [qq[:, ib * 128:(ib + 1) * 128] for ib in range(11)],
                        [hqT[:, ((ih0 * 11 + ib) * GRP + ci0) * 128:
                             ((ih0 * 11 + ib) * GRP + ci0) * 128 + 128]
                         for ib in range(11)])

                for ih in range(2):
                    w_kb = [0] * 8
                    u_kb = [0] * 8
                    for kb in range(8):
                        step("sync", [(tp, tp_kb_prev[kb])], [(dmw, 16)],
                             lambda sy, kb=kb, ih=ih: sy.dma_start(
                                 out=wgh[:, kb * IH:(kb + 1) * IH].rearrange(
                                     "p (r t) -> p r t", r=11),
                                 in_=wgq[kb, ih * 11:(ih + 1) * 11].rearrange(
                                     "r p t -> p r t")))
                        w_kb[kb] = dmw.n
                        step("sync", [(tp, tp_kb_prev[kb])], [(dmu, 16)],
                             lambda sy, kb=kb, ih=ih: sy.dma_start(
                                 out=wuh[:, kb * IH:(kb + 1) * IH].rearrange(
                                     "p (r t) -> p r t", r=11),
                                 in_=wuq[kb, ih * 11:(ih + 1) * 11].rearrange(
                                     "r p t -> p r t")))
                        u_kb[kb] = dmu.n

                    for ci in range(GRP):
                        c = g * GRP + ci
                        tp_kb_cur = [0] * 8
                        for kb in range(8):
                            def gup_kb(t, ci=ci, kb=kb, g=g):
                                lhs = xqT[(g * GRP + ci) % 8][
                                    :, kb * 128:(kb + 1) * 128]
                                first, last = kb == 0, kb == 7
                                for nw in range(3):
                                    n0, n1 = nw * 512, min((nw + 1) * 512, IH)
                                    t.matmul(ps_a[:, n0:n1], lhs,
                                             wgh[:, kb * IH + n0:kb * IH + n1],
                                             start=first, stop=last)
                                    ins = t.matmul(
                                        ps_b[:, n0:n1], lhs,
                                        wuh[:, kb * IH + n0:kb * IH + n1],
                                        start=first, stop=last)
                                return ins
                            w = [(dmw, w_kb[kb]), (dmu, u_kb[kb])]
                            if kb == 0:
                                w += [(sp, max(sp_free_a[0], sp_xq[0])),
                                      (vp, vp_free_b[0])]
                            step("tensor", w, [(tp, 1)], gup_kb)
                            tp_kb_cur[kb] = tp.n
                        if ci == GRP - 1:
                            tp_kb_prev = tp_kb_cur
                        tp_gup = tp.n

                        step("scalar", [(tp, tp_gup), (vp, vp_free_b[0])], [(sp, 1)],
                             lambda s: s.activation(sgb[:], ps_a[:, 0:IH], AF.Silu))
                        sp_free_a[0] = sp.n

                        hq_vp = vp.n
                        flush_hq(hq_vp)
                        tp_xp = tp.n

                        def hmult(v):
                            return v.tensor_tensor(hb[:], sgb[:], ps_b[:, 0:IH],
                                                   ALU.mult)
                        step("vector", [(sp, sp_free_a[0]), (tp, tp_gup)],
                             [(vp, 1)], hmult)
                        vp_free_b[0] = vp.n

                        sv0 = svv.n
                        svv.n += 2

                        def hquant(v, sv0=sv0):
                            nblk = IH // 16
                            v.reduce_max(maxa[:, 0:nblk], r16(hb[:]), axis=AXX,
                                         apply_absolute_value=True).then_inc(svv.h, 1)
                            v.wait_ge(svv.h, sv0 + 1)
                            v.tensor_scalar(maxa[:, 0:nblk], maxa[:, 0:nblk], 6e-12,
                                            None, ALU.max)
                            v.reciprocal(rm6[:, 0:nblk], maxa[:, 0:nblk]
                                         ).then_inc(svv.h, 1)
                            v.wait_ge(svv.h, sv0 + 2)
                            v.tensor_scalar(rm6[:, 0:nblk], rm6[:, 0:nblk], 6.0,
                                            None, ALU.mult)
                            v.tensor_scalar(sca[:, 0:nblk], maxa[:, 0:nblk],
                                            1.0 / 6.0, None, ALU.mult)
                            v.tensor_tensor(r16(zb[:, 0:IH]), r16(hb[:]),
                                            bc16(rm6[:, 0:nblk], nblk), ALU.mult)
                            return v._custom_dve(
                                FP4_QD, out=r16(qq[:, 0:IH]), in0=r16(zb[:, 0:IH]),
                                in1=bc16(sca[:, 0:nblk], nblk), s0=FP4_C0, s1=FP4_C1)
                        step("vector", [(tp, tp_xp)], [(vp, 1)], hquant)
                        pending_hq[0] = (ih, ci)

                        if ih == 1 and g < NGRP - 1:
                            c_n = (g + 1) * GRP + ci
                            sl = c_n % 8
                            step("sync",
                                 [(vp, resid_done[sl]), (stq, store_done[sl])],
                                 [(dmx, 16)],
                                 lambda sy, c_n=c_n, sl=sl: sy.dma_start(
                                     out=xbr[sl][:],
                                     in_=x_src[c_n * 128:(c_n + 1) * 128, :]))
                            plan_xquant(c_n, xbr[sl][:],
                                        qx[:, ci * D:(ci + 1) * D],
                                        [(tp, qx_free[0])])
                            vp_xq = vp.n

                flush_hq(vp.n)
                for ci in range(GRP):
                    c = g * GRP + ci
                    sl = c % 8
                    ps = ps_a if ci % 2 == 0 else ps_b

                    def down(t, ci=ci, ps=ps):
                        for ib in range(22):
                            lhs = hqT[:, (ib * GRP + ci) * 128:
                                      (ib * GRP + ci) * 128 + 128]
                            for nw in range(2):
                                ins = t.matmul(
                                    ps[:, nw * 512:(nw + 1) * 512], lhs,
                                    wdn[:, ib * D + nw * 512:ib * D + (nw + 1) * 512],
                                    start=(ib == 0), stop=(ib == 21))
                        return ins
                    w = [(sp, sp.n)]
                    if ci >= 2:
                        w.append((vp, resid_done_ps[ci % 2]))
                    elif ci == 1:
                        w.append((vp, vp_free_b[0]))
                    step("tensor", w, [(tp, 1)], down)
                    step("vector", [(tp, tp.n)], [(vp, 1)],
                         lambda v, sl=sl, ps=ps: v.tensor_tensor(
                             xbr[sl][:], xbr[sl][:], ps[:, 0:D], ALU.add))
                    resid_done_ps[ci % 2] = vp.n
                    if fold_out:
                        def fstat(v, c=c, sl=sl):
                            v.reduce_sum(msum[:, c:c + 1], xbr[sl][:], axis=AXX)
                            return v.scalar_tensor_tensor(
                                ab[:], xbr[sl][:], 1.0, xbr[sl][:],
                                ALU.mult, ALU.mult, accum_out=sqs[:, c:c + 1])
                        step("vector", [], [(vp, 1)], fstat)
                    resid_done[sl] = vp.n
                    step("gpsimd", [(vp, vp.n)], [(stq, 16)],
                         lambda g2, c=c, sl=sl: g2.dma_start(
                             out=x_dst[c * 128:(c + 1) * 128, :], in_=xbr[sl][:]))
                    store_done[sl] = stq.n
                vp_free_b[0] = vp.n
                sp_free_a[0] = max(sp_free_a[0], sp.n)

                if g < NGRP - 1:
                    step("tensor", [(vp, vp_xq)], [], None)
                    for ci in range(GRP):
                        sl = ((g + 1) * GRP + ci) % 8
                        xpose_evac(
                            [qx[:, ci * D + kb * 128:ci * D + (kb + 1) * 128]
                             for kb in range(8)],
                            [xqT[sl][:, kb * 128:(kb + 1) * 128]
                             for kb in range(8)])
                    qx_free[0] = tp.n
                    sp_xq[0] = sp.n

            step("gpsimd", [(stq, stq.n)], [], None)
        barrier()

    rec_st = {"sV": 0, "v10": 0}

    def rec_split(k2ap, dst_fn):
        s_scr = sp.n
        sV = {}
        vV = {}
        for m in range(2, M):
            cur, p1, p2 = (m % 3) * D, ((m - 1) % 3) * D, ((m - 2) % 3) * D

            def vbody(v, m=m, cur=cur, p1=p1, p2=p2, k2ap=k2ap):
                if m == 2:
                    v.memset(zb[:, 0:D], 1.0)
                    v.tensor_scalar(zb[:, D:2 * D], k2ap, 0.5, None, ALU.mult)
                v.tensor_tensor(zb[:, cur:cur + D], k2ap,
                                zb[:, p1:p1 + D], ALU.mult)
                return v.tensor_tensor(zb[:, cur:cur + D], zb[:, cur:cur + D],
                                       zb[:, p2:p2 + D], ALU.subtract)
            if m == 2:
                w = [(sp, max(s_scr, rec_st["sV"]))]
            elif m >= 5:
                w = [(sp, sV[m - 3])]
            else:
                w = []
            step("vector", w, [(vp, 1)], vbody)
            vV[m] = vp.n

            step("scalar", [(vp, vV[m])], [(sp, 1)],
                 lambda s, m=m, cur=cur: s.copy(dst_fn(m, 0, D),
                                                zb[:, cur:cur + D]))
            sV[m] = sp.n
        rec_st["sV"] = sV[M - 1]
        rec_st["v10"] = vV[M - 1]

    def attn_phase(x_src, x_dst, stats_ready=False, fold_out=True):
        ln_stats(x_src, stats_ready)
        with ExitStack() as kst:
            kvacc = kst.enter_context(
                nc.sbuf_tensor("kvacc", [128, NPAIR * M * 128], F32))
            kvar = kst.enter_context(
                nc.sbuf_tensor("kvar", [128, NPAIR * M * 128], BF16, side="right"))
            with ExitStack() as ast:
                asb = lambda name, shape, dt=F32: ast.enter_context(
                    nc.sbuf_tensor(name, shape, dt))
                wkv = asb("wkv", [128, 8 * 2048], BF16)
                tkall = asb("tkall", [128, SUB * M * D], BF16)
                vgrp = asb("vgrp", [128, SUB * D], BF16)
                scr2b = asb("scr2b", [128, D])
                xqTa = [asb(f"xqTa{i}", [128, D], BF16) for i in range(2)]

                def load_wkv(sy):
                    ins = None
                    for kb in range(8):
                        if ins is not None:
                            ins.then_inc(dmw.h, 16)
                        ins = sy.dma_start(
                            out=wkv[:, kb * 2048:(kb + 1) * 2048].rearrange(
                                "p (r t) -> p r t", r=16),
                            in_=winT[kb, 8:24].rearrange("r p t -> p r t"))
                    return ins
                dmw.n += 7 * 16
                step("sync", [], [(dmw, 16)], load_wkv)

                def tk0(g2):
                    g2.memset(tkall[:, 0:D], 1.0)
                    return g2.memset(tkall[:, M * D:M * D + D], 1.0)
                step("gpsimd", [], [(gp, 1)], tk0)

                ab_free = [0]
                for sg in range(NCHUNK // SUB):
                    kevac_done = [0, 0]
                    vevac_done = [0, 0]
                    s_scr_c = [0, 0]
                    for ci in range(SUB):
                        c = sg * SUB + ci
                        step("sync", [(vp, vp.n)], [(dmx, 16)],
                             lambda sy, c=c, ci=ci: sy.dma_start(
                                 out=xb[ci][:], in_=x_src[c * 128:(c + 1) * 128, :]))
                        step("vector", [(dmx, dmx.n), (tp, tp.n)], [(vp, 1)],
                             lambda v, c=c, ci=ci: v.tensor_scalar(
                                 qq[:, 0:D], xb[ci][:], mcol[:, c:c + 1],
                                 rstd[:, c:c + 1], ALU.subtract, ALU.mult))
                        step("tensor", [(vp, vp.n)], [], None)
                        xpose_evac(
                            [qq[:, kb * 128:(kb + 1) * 128] for kb in range(8)],
                            [xqTa[ci][:, kb * 128:(kb + 1) * 128] for kb in range(8)],
                            evac_waits=[(stq, stq.n)])
                        step("gpsimd", [(sp, sp.n)], [(stq, 16)],
                             lambda g2, c=c, ci=ci: g2.dma_start(
                                 out=xnT_h[c].rearrange("(kb kp) t -> kp kb t",
                                                        kp=128),
                                 in_=xqTa[ci][:].rearrange("p (kb t) -> p kb t",
                                                           kb=8)))

                        def kvproj(t, ci=ci):
                            for kb in range(8):
                                lhs = xqTa[ci][:, kb * 128:(kb + 1) * 128]
                                for nw in range(4):
                                    ps = ps_a if nw < 2 else ps_b
                                    ins = t.matmul(
                                        ps[:, (nw % 2) * 512:(nw % 2 + 1) * 512], lhs,
                                        wkv[:, kb * 2048 + nw * 512:
                                            kb * 2048 + (nw + 1) * 512],
                                        start=(kb == 0), stop=(kb == 7))
                            return ins
                        step("tensor",
                             [(dmw, dmw.n), (vp, kevac_done[0]),
                              (sp, vevac_done[0])],
                             [(tp, 1)], kvproj)
                        tp_proj = tp.n

                        def kevac(v):
                            v.tensor_scalar(ab[:], ps_a[:, 0:D], 0.125, 1.0,
                                            ALU.mult, ALU.min)
                            return v.tensor_scalar(ab[:], ab[:], -1.0, None, ALU.max)
                        step("vector", [(tp, tp_proj), (sp, ab_free[0])],
                             [(vp, 1)], kevac)
                        kevac_done[0] = vp.n
                        step("scalar", [(tp, tp_proj)], [(sp, 1)],
                             lambda s, ci=ci: s.copy(vgrp[:, ci * D:(ci + 1) * D],
                                                     ps_b[:, 0:D]))
                        vevac_done[0] = sp.n

                        k2dst = scr[:] if ci == 0 else scr2b[:]

                        def sinit(s, ci=ci, k2dst=k2dst):
                            s.copy(tkall[:, ci * M * D + D:ci * M * D + 2 * D],
                                   ab[:])
                            return s.activation(k2dst, ab[:], AF.Copy, scale=2.0)
                        step("scalar",
                             [(vp, max(kevac_done[0], rec_st["v10"]))],
                             [(sp, 1)], sinit)
                        ab_free[0] = sp.n
                        s_scr_c[ci] = sp.n

                    for ci in range(SUB):
                        rec_split(scr[:] if ci == 0 else scr2b[:],
                                  lambda m, lo, hi, ci=ci:
                                  tkall[:, ci * M * D + m * D + lo:
                                        ci * M * D + m * D + hi])

                    for pp in range(NPAIR // 2):
                        def kvmm(t, sg=sg, pp=pp):
                            for half in range(2):
                                pair = pp * 2 + half
                                ps = ps_a if half == 0 else ps_b
                                for ci in range(SUB):
                                    lhs = vgrp[:, ci * D + pair * 128:
                                               ci * D + (pair + 1) * 128]
                                    for mg in range(3):
                                        m0 = mg * 4
                                        mN = min(4, M - m0)
                                        rhs = tkall[:].rearrange(
                                            "p (c m t) -> p c m t", c=SUB, m=M)[
                                            :, ci, m0:m0 + mN,
                                            pair * 128:(pair + 1) * 128]
                                        ins = t.matmul(
                                            ps[:, m0 * 128:(m0 + mN) * 128], lhs, rhs,
                                            start=(ci == 0), stop=(ci == SUB - 1))
                            return ins
                        step("tensor", [(sp, sp.n), (vp, kevac_done[0])],
                             [(tp, 1)], kvmm)

                        def kvadd(v, sg=sg, pp=pp):
                            for half in range(2):
                                pair = pp * 2 + half
                                ps = ps_a if half == 0 else ps_b
                                dst = kvacc[:, pair * M * 128:(pair + 1) * M * 128]
                                if sg == 0:
                                    ins = v.tensor_copy(dst, ps[:, 0:M * 128])
                                else:
                                    ins = v.tensor_tensor(dst, dst, ps[:, 0:M * 128],
                                                          ALU.add)
                            return ins
                        step("vector", [(tp, tp.n)], [(vp, 1)], kvadd)
                        step("tensor", [(vp, vp.n)], [], None)

                step("gpsimd", [(stq, stq.n)], [], None)
                barrier()

                step("gpsimd", [], [(gp, 1)], lambda g2: g2.memset(kvar[:], 0.0))
                step("tensor", [(gp, gp.n)], [], None)
                arr_base = sp.n
                for pair in range(NPAIR):
                    for m in range(M):
                        j = pair * M + m
                        slot = ps_a[:, 0:128] if j % 2 == 0 else ps_b[:, 0:128]
                        step("tensor",
                             [(sp, arr_base + 2 * (j - 1))] if j >= 2 else [],
                             [(tp, 1)],
                             lambda t, pair=pair, m=m, slot=slot: t.transpose(
                                 slot,
                                 kvacc[:, (pair * M + m) * 128:
                                       (pair * M + m + 1) * 128],
                                 identf[:]))

                        def kvevac(s, pair=pair, m=m, slot=slot):
                            base = (pair * M + m) * 128
                            s.activation(kvar[0:64, base:base + 64], slot[0:64, 0:64],
                                         AF.Copy,
                                         scale=betac[0:64, pair * M + m:
                                                     pair * M + m + 1])
                            return s.activation(
                                kvar[64:128, base + 64:base + 128],
                                slot[64:128, 64:128], AF.Copy,
                                scale=betac[64:128, pair * M + m:pair * M + m + 1])
                        step("scalar", [(tp, tp.n)], [(sp, 2)], kvevac)
                step("tensor", [(sp, sp.n)], [], None)
                barrier()

            with ExitStack() as bst:
                bsb2 = lambda name, shape, dt=F32: bst.enter_context(
                    nc.sbuf_tensor(name, shape, dt))
                wq8 = bsb2("wq8", [128, 8 * D], BF16)
                woT = bsb2("woT", [128, 8 * D], BF16)
                xbT = bsb2("xbT", [128, 8 * D], BF16)
                tqa = bsb2("tqa", [128, M * D], BF16)
                aoT = bsb2("aoT", [128, 8 * D], BF16)
                qf = bsb2("qf", [128, D])

                def load_wb(sy):
                    ins = None
                    for kb in range(8):
                        if ins is not None:
                            ins.then_inc(dmw.h, 16)
                        ins = sy.dma_start(
                            out=wq8[:, kb * D:(kb + 1) * D].rearrange(
                                "p (r t) -> p r t", r=8),
                            in_=winT[kb, 0:8].rearrange("r p t -> p r t"))
                        ins.then_inc(dmw.h, 16)
                        ins = sy.dma_start(
                            out=woT[:, kb * D:(kb + 1) * D].rearrange(
                                "p (r t) -> p r t", r=8),
                            in_=woutT[kb, :].rearrange("r p t -> p r t"))
                    return ins
                dmw.n += 15 * 16
                step("sync", [], [(dmw, 16)], load_wb)

                step("gpsimd", [], [(gp, 1)],
                     lambda g2: g2.memset(tqa[:, 0:D], 1.0))

                bstore = [0] * 4
                for stile in range(4):
                    c0 = stile * 8

                    def load_x(sy, c0=c0):
                        ins = None
                        for kb in range(8):
                            if ins is not None:
                                ins.then_inc(dmx.h, 16)
                            ins = sy.dma_start(
                                out=xbT[:, kb * D:(kb + 1) * D].rearrange(
                                    "p (c t) -> p c t", c=8),
                                in_=xnT_h[c0:c0 + 8, kb * 128:(kb + 1) * 128]
                                .rearrange("c kp t -> kp c t"))
                        return ins
                    dmx.n += 7 * 16
                    step("sync", [(vp, vp.n)], [(dmx, 16)], load_x)

                    qf_done = [0]
                    qf_free = [0]
                    ao_done = [0]
                    tq_done = {}
                    pending_out = [None]

                    def plan_outmm(pair):
                        def outmm(t, pair=pair):
                            for m in range(M):
                                lhs = kvar[:, (pair * M + m) * 128:
                                           (pair * M + m + 1) * 128]
                                for nw in range(2):
                                    ins = t.matmul(
                                        ps_b[:, nw * 512:(nw + 1) * 512], lhs,
                                        tqa[:, m * D + nw * 512:
                                            m * D + (nw + 1) * 512],
                                        start=(m == 0), stop=(m == M - 1))
                            return ins
                        step("tensor",
                             [(sp, max(tq_done[pair], ao_done[0]))],
                             [(tp, 1)], outmm)
                        step("scalar", [(tp, tp.n)], [(sp, 1)],
                             lambda s, pair=pair: s.copy(
                                 aoT[:, pair * D:(pair + 1) * D], ps_b[:, 0:D]))
                        ao_done[0] = sp.n

                    for pair in range(NPAIR):
                        def qproj(t, pair=pair):
                            for kb in range(8):
                                lhs = wq8[:, kb * D + pair * 128:
                                          kb * D + (pair + 1) * 128]
                                for nw in range(2):
                                    ins = t.matmul(
                                        ps_a[:, nw * 512:(nw + 1) * 512], lhs,
                                        xbT[:, kb * D + nw * 512:
                                            kb * D + (nw + 1) * 512],
                                        start=(kb == 0), stop=(kb == 7))
                            return ins
                        step("tensor",
                             [(dmx, dmx.n), (dmw, dmw.n), (vp, qf_done[0])],
                             [(tp, 1)], qproj)
                        tp_q = tp.n

                        def qfevac(v):
                            v.tensor_scalar(qf[:], ps_a[:, 0:1024], 0.125, 1.0,
                                            ALU.mult, ALU.min)
                            return v.tensor_scalar(qf[:], qf[:], -1.0, None, ALU.max)
                        step("vector", [(tp, tp_q), (sp, qf_free[0])],
                             [(vp, 1)], qfevac)
                        qf_done[0] = vp.n

                        if pending_out[0] is not None:
                            plan_outmm(pending_out[0])
                        tp_out = tp.n

                        def sinit(s):
                            s.copy(tqa[:, D:2 * D], qf[:])
                            return s.activation(scr[:], qf[:], AF.Copy, scale=2.0)
                        step("scalar",
                             [(vp, max(qf_done[0], rec_st["v10"])),
                              (tp, tp_out)], [(sp, 1)], sinit)
                        qf_free[0] = sp.n

                        rec_split(scr[:],
                                  lambda m, lo, hi: tqa[:, m * D + lo:m * D + hi])
                        tq_done[pair] = sp.n
                        pending_out[0] = pair
                    plan_outmm(pending_out[0])

                    wout_resid = [0, 0]
                    for ci in range(8):
                        c = c0 + ci
                        ps = ps_a if ci % 2 == 0 else ps_b
                        step("sync", [(vp, vp.n), (stq, bstore[ci % 4])],
                             [(dmx, 16)],
                             lambda sy, c=c, ci=ci: sy.dma_start(
                                 out=xb[ci % 4][:],
                                 in_=x_src[c * 128:(c + 1) * 128, :]))

                        def woutmm(t, ci=ci, ps=ps):
                            for kb in range(8):
                                lhs = aoT[:, kb * D + ci * 128:kb * D + ci * 128 + 128]
                                for nw in range(2):
                                    ins = t.matmul(
                                        ps[:, nw * 512:(nw + 1) * 512], lhs,
                                        woT[:, kb * D + nw * 512:
                                            kb * D + (nw + 1) * 512],
                                        start=(kb == 0), stop=(kb == 7))
                            return ins
                        w = [(sp, sp.n)]
                        if ci >= 2:
                            w.append((vp, wout_resid[ci % 2]))
                        step("tensor", w, [(tp, 1)], woutmm)
                        step("vector", [(tp, tp.n), (dmx, dmx.n)], [(vp, 1)],
                             lambda v, ci=ci, ps=ps: v.tensor_tensor(
                                 xb[ci % 4][:], xb[ci % 4][:], ps[:, 0:D], ALU.add))
                        wout_resid[ci % 2] = vp.n
                        if fold_out:
                            def fstat(v, c=c, ci=ci):
                                v.reduce_sum(msum[:, c:c + 1], xb[ci % 4][:],
                                             axis=AXX)
                                return v.scalar_tensor_tensor(
                                    ab[:], xb[ci % 4][:], 1.0, xb[ci % 4][:],
                                    ALU.mult, ALU.mult,
                                    accum_out=sqs[:, c:c + 1])
                            step("vector", [], [(vp, 1)], fstat)
                        step("gpsimd", [(vp, vp.n)], [(stq, 16)],
                             lambda g2, c=c, ci=ci: g2.dma_start(
                                 out=x_dst[c * 128:(c + 1) * 128, :],
                                 in_=xb[ci % 4][:]))
                        bstore[ci % 4] = stq.n

                step("gpsimd", [(stq, stq.n)], [], None)
        barrier()

    nph = [p for p in ("ffn1", "attn", "ffn2") if p in phases]
    cur = x_in
    for idx, p in enumerate(nph):
        dst = y_out if idx == len(nph) - 1 else xs[idx % 2]
        last = idx == len(nph) - 1
        if p in ("ffn1", "ffn2"):
            ffn_phase(cur, dst, stats_ready=idx > 0, fold_out=not last)
        else:
            attn_phase(cur, dst, stats_ready=idx > 0, fold_out=not last)
        cur = dst
    if cur is not y_out:
        for c in range(NCHUNK):
            dmx_tgt = dmx.n + 16

            def cl(sy, c=c, cur=cur, dmx_tgt=dmx_tgt):
                ins = sy.dma_start(out=xb[0][:], in_=cur[c * 128:(c + 1) * 128, :])
                ins.then_inc(dmx.h, 16)
                sy.wait_ge(dmx.h, dmx_tgt)
                return sy.dma_start(out=y_out[c * 128:(c + 1) * 128, :], in_=xb[0][:])
            step("sync", [(stq, stq.n)], [(stq, 16)], cl)
            dmx.n += 16

    step("sync", [(stq, stq.n)], [], None)

    with nc.Block() as block:
        @block.sync
        def _(e):
            for f in prog["sync"]:
                f(e)

        @block.vector
        def _(e):
            for f in prog["vector"]:
                f(e)

        @block.scalar
        def _(e):
            for f in prog["scalar"]:
                f(e)

        @block.tensor
        def _(e):
            for f in prog["tensor"]:
                f(e)

        @block.gpsimd
        def _(e):
            for f in prog["gpsimd"]:
                f(e)

    st.close()
    mybir.codegen_inst_isa_subclasses(nc)
    return nc


_NC_CACHE = {}


def kernel(L0_attn_x, L0_attn_w_in, L0_attn_w_out, L0_attn_beta,
           L0_ffn_w_gate, L0_ffn_w_up, L0_ffn_w_down):
    x = np.ascontiguousarray(np.asarray(L0_attn_x, dtype=np.float32))
    if "nc" not in _NC_CACHE:
        _NC_CACHE["nc"] = build_program()
    nc = _NC_CACHE["nc"]
    w = {
        "w_in": np.ascontiguousarray(np.asarray(L0_attn_w_in, np.float32)),
        "w_out": np.ascontiguousarray(np.asarray(L0_attn_w_out, np.float32)),
        "beta": np.ascontiguousarray(np.asarray(L0_attn_beta, np.float32)),
        "w_gate": np.ascontiguousarray(np.asarray(L0_ffn_w_gate, np.float32)),
        "w_up": np.ascontiguousarray(np.asarray(L0_ffn_w_up, np.float32)),
        "w_down": np.ascontiguousarray(np.asarray(L0_ffn_w_down, np.float32)),
    }
    in_maps = [dict(w, x=np.ascontiguousarray(x[b])) for b in range(B)]
    res = run_bass_kernel_spmd(nc, in_maps, core_ids=list(range(B)))
    out = np.stack([np.asarray(res.results[b]["y"]) for b in range(B)], axis=0)
    return out.astype(np.float32)
